# version 13
# baseline (speedup 1.0000x reference)
import contextlib
import math
import os
import numpy as np
import concourse.bass as bass
import concourse.mybir as mybir
from concourse.bass_utils import run_bass_kernel_spmd

F32 = mybir.dt.float32
BF16 = mybir.dt.bfloat16
I32 = mybir.dt.int32
AF = mybir.ActivationFunctionType
ALU = mybir.AluOpType
AX = mybir.AxisListType

NCORE = 8
D = 1024
DI = 2048
MF = 2816
DS = 16
DR = 64
NE = 8
ME = 2048
NH = 8
TWO_PI = float(2.0 * np.pi)
MAGIC = 12582912.0
TC = 256

ENG = ("pe", "act", "dve", "pool", "sync")


class Buf:
    __slots__ = ("name", "writer", "readers")

    def __init__(self, name):
        self.name = name
        self.writer = None
        self.readers = []


NDMASEM = 8


class Ctx:
    def __init__(self, nc):
        self.nc = nc
        self.prog = {e: [] for e in ENG}
        self.seq = {"pe": 0, "act": 0, "dve": 0, "pool": 0, "sync": 0}
        for q in ("sync", "pool", "act"):
            for i in range(NDMASEM):
                self.seq["dq_%s_%d" % (q, i)] = 0
        self.dmacnt = {"sync": 0, "pool": 0, "act": 0}
        self.hw = {e: {} for e in ENG}
        self.bufs = {}

    def buf(self, name):
        if name not in self.bufs:
            self.bufs[name] = Buf(name)
        return self.bufs[name]

    PARANOID = False

    def op(self, eng, fn, reads=(), writes=(), dma=False):
        reads = [self.buf(r) if isinstance(r, str) else r for r in reads]
        writes = [self.buf(w) if isinstance(w, str) else w for w in writes]
        need = {}
        if Ctx.PARANOID:
            for k in self.seq:
                if Ctx.PARANOID is not True and k not in Ctx.PARANOID:
                    continue
                if self.seq[k] > 0 and self.hw[eng].get(k, 0) < self.seq[k] and not (k == eng == "pe"):
                    need[k] = self.seq[k]

        def add(src):
            if src is None:
                return
            key, s = src
            if key == eng and eng == "pe":
                return
            if self.hw[eng].get(key, 0) >= s:
                return
            if need.get(key, 0) < s:
                need[key] = s
        for b in reads:
            add(b.writer)
        for b in writes:
            add(b.writer)
            for r in b.readers:
                add(r)
        if dma:
            idx = self.dmacnt[eng] % NDMASEM
            self.dmacnt[eng] += 1
            key = "dq_%s_%d" % (eng, idx)
            if self.seq[key] > 0 and self.hw[eng].get(key, 0) < self.seq[key]:
                if need.get(key, 0) < self.seq[key]:
                    need[key] = self.seq[key]
        else:
            key = eng
        for k, val in need.items():
            self.hw[eng][k] = val
        inc = 16 if dma else 1
        self.seq[key] += inc
        myseq = self.seq[key]
        self.prog[eng].append((list(need.items()), fn, key, inc))
        for b in writes:
            b.writer = (key, myseq)
            b.readers = []
        for b in reads:
            b.readers.append((key, myseq))
        return myseq

    def dma(self, q, out_ap, in_ap, reads=(), writes=(), **kw):
        def fn(e, out_ap=out_ap, in_ap=in_ap, kw=kw):
            return e.dma_start(out=out_ap, in_=in_ap, **kw)
        return self.op(q, fn, reads=reads, writes=writes, dma=True)

    def dmat(self, q, out_ap, in_ap, reads=(), writes=()):
        def fn(e, out_ap=out_ap, in_ap=in_ap):
            return e.dma_start_transpose(out=out_ap, in_=in_ap)
        return self.op(q, fn, reads=reads, writes=writes, dma=True)

    def barrier(self):
        snap = dict(self.seq)
        for e in ENG:
            waits = []
            for k, val in snap.items():
                if val > 0 and self.hw[e].get(k, 0) < val:
                    waits.append((k, val))
                    self.hw[e][k] = val
            if waits:
                self.prog[e].append((waits, None, None, 0))

    def finalize(self, sems):
        self.barrier()
        final = dict(self.seq)
        nc = self.nc
        with nc.Block() as block:
            def runner(eng):
                def run(obj):
                    for waits, fn, key, inc in self.prog[eng]:
                        for k, val in waits:
                            obj.wait_ge(sems[k], val)
                        if fn is not None:
                            ins = fn(obj)
                            ins.then_inc(sems[key], inc)
                return run
            block.tensor(runner("pe"))
            block.scalar(runner("act"))
            block.vector(runner("dve"))
            block.gpsimd(runner("pool"))
            block.sync(runner("sync"))


def prep_weights(inp):
    w = {}
    f32 = lambda a: np.ascontiguousarray(np.asarray(a), dtype=np.float32)
    qk_w = f32(inp["L0_attn_qk_w"])
    qk_b = f32(inp["L0_attn_qk_b"])
    perm = np.r_[np.arange(0, D, 2), np.arange(1, D, 2)]
    w["qk_w"] = np.ascontiguousarray(qk_w[:, np.r_[perm, D + perm]])
    w["qk_b"] = np.ascontiguousarray(qk_b[np.r_[perm, D + perm]][None, :])
    w["lepe_w"] = f32(inp["L0_attn_lepe_w"])
    w["lepe_b"] = np.ascontiguousarray(f32(inp["L0_attn_lepe_b"]).reshape(8, 128).T)
    w["wg"] = f32(inp["L0_ffn_w_gate"])
    w["wu"] = f32(inp["L0_ffn_w_up"])
    w["wd"] = f32(inp["L0_ffn_w_down"])
    w["ln_g"] = np.ascontiguousarray(f32(inp["L1_attn_ln_g"]).reshape(8, 128).T)
    w["ln_b"] = np.ascontiguousarray(f32(inp["L1_attn_ln_b"]).reshape(8, 128).T)
    w["in_w"] = np.ascontiguousarray(f32(inp["L1_attn_in_proj_w"]).T)
    w["conv_w"] = np.ascontiguousarray(
        f32(inp["L1_attn_conv_w"])[:, 0, :].reshape(16, 128, 4).transpose(1, 0, 2).reshape(128, 64))
    w["conv_b"] = np.ascontiguousarray(f32(inp["L1_attn_conv_b"]).reshape(16, 128).T)
    xw = f32(inp["L1_attn_x_proj_w"])
    w["xw_dt"] = np.ascontiguousarray(xw[0:DR].T)
    w["xw_B"] = np.ascontiguousarray(xw[DR:DR + DS].T)
    w["xw_C"] = np.ascontiguousarray(xw[DR + DS:].T)
    w["dt_w"] = np.ascontiguousarray(f32(inp["L1_attn_dt_proj_w"]).T)
    w["dt_b"] = np.ascontiguousarray(f32(inp["L1_attn_dt_proj_b"]).reshape(16, 128).T)
    w["A_log"] = f32(inp["L1_attn_A_log"])
    w["D_"] = np.ascontiguousarray(f32(inp["L1_attn_D"]).reshape(16, 128).T)
    w["out_w"] = np.ascontiguousarray(f32(inp["L1_attn_out_proj_w"]).T)
    w["glu_w"] = np.ascontiguousarray(f32(inp["L1_attn_glu_w"]).T)
    w["glu_b"] = np.ascontiguousarray(f32(inp["L1_attn_glu_b"]).reshape(16, 128).T)
    w["Wr"] = f32(inp["L1_ffn_W_router"])
    w["Wg_e"] = f32(inp["L1_ffn_W_gate"])
    w["Wu_e"] = f32(inp["L1_ffn_W_up"])
    w["Wo_e"] = f32(inp["L1_ffn_W_out"])
    w["bg_e"] = np.ascontiguousarray(f32(inp["L1_ffn_b_gate"]).reshape(8, 16, 128).transpose(0, 2, 1))
    w["bu_e"] = np.ascontiguousarray(f32(inp["L1_ffn_b_up"]).reshape(8, 16, 128).transpose(0, 2, 1))
    w["bo_e"] = f32(inp["L1_ffn_b_out"])
    w["ident"] = np.eye(128, dtype=np.float32)
    return w


WEIGHT_SHAPES = {
    "qk_w": [D, 2 * D], "qk_b": [1, 2 * D], "lepe_w": [3, D, D], "lepe_b": [128, 8],
    "wg": [MF, D], "wu": [MF, D], "wd": [D, MF],
    "ln_g": [128, 8], "ln_b": [128, 8], "in_w": [D, 2 * DI],
    "conv_w": [128, 64], "conv_b": [128, 16],
    "xw_dt": [DI, DR], "xw_B": [DI, DS], "xw_C": [DI, DS],
    "dt_w": [DR, DI], "dt_b": [128, 16], "A_log": [DI, DS], "D_": [128, 16],
    "out_w": [DI, D], "glu_w": [D, DI], "glu_b": [128, 16],
    "Wr": [D, NE], "Wg_e": [NE, D, ME], "Wu_e": [NE, D, ME], "Wo_e": [NE, ME, D],
    "bg_e": [NE, 128, 16], "bu_e": [NE, 128, 16], "bo_e": [NE, D],
    "ident": [128, 128],
}

SCRATCH = {
    "xn_tm": (lambda N: [N, D], BF16),
    "xq_tm": (lambda N: [N, D], BF16),
    "cos_sp": (lambda N: [N, 512], F32),
    "sin_sp": (lambda N: [N, 512], F32),
    "theta_row": (lambda N: [1, 512], F32),
    "q_tm": (lambda N: [N, D], BF16),
    "qr_tm": (lambda N: [N, D], BF16),
    "kv_d": (lambda N: [128, 1024], BF16),
    "ks_d": (lambda N: [128, 8], F32),
    "kmean_row": (lambda N: [1, D], F32),
    "z_rows": (lambda N: [NH, N], F32),
    "s0_f": (lambda N: [D, N], F32),
    "wgqT_d": (lambda N: [D, MF], BF16),
    "wuqT_d": (lambda N: [D, MF], BF16),
    "wdqT_d": (lambda N: [8, MF, 128], BF16),
    "hq_st": (lambda N: [512, MF], BF16),
    "x1_f": (lambda N: [D, N], F32),
    "xn2_bf": (lambda N: [D, N], BF16),
    "xn2_f": (lambda N: [D, N], F32),
    "xni_bf": (lambda N: [D, N], BF16),
    "mu_row": (lambda N: [2, N], F32),
    "mu2_row": (lambda N: [2, N], F32),
    "siluz": (lambda N: [DI, N], BF16),
    "xc_f": (lambda N: [DI, N], F32),
    "dt_f": (lambda N: [DI, N], F32),
    "dtu_f": (lambda N: [DI, N], BF16),
    "B_d": (lambda N: [DS, N], F32),
    "C_d": (lambda N: [DS, N], F32),
    "ew_rows": (lambda N: [NE, N], F32),
    "y3": (lambda N: [DI, N], BF16),
    "s1_f": (lambda N: [D, N], F32),
    "moe_f": (lambda N: [D, N], F32),
    "wbf": (lambda N: [3 * NE * D * ME], BF16),
    "dbg1": (lambda N: [D, N], F32),
    "dbg2": (lambda N: [D, N], F32),
    "dbgb": (lambda N: [128, 8192], BF16),
    "dbgf": (lambda N: [128, 8192], F32),
}

PHASE_IO = {
    "P1": (["x_tm"], ["xn_tm", "xq_tm", "cos_sp", "sin_sp"]),
    "PW": (["Wg_e", "Wu_e", "Wo_e"], ["wbf"]),
    "P2": (["xn_tm", "cos_sp", "sin_sp", "qk_w", "qk_b"], ["q_tm", "qr_tm", "kv_d", "ks_d"]),
    "P3a": (["q_tm", "qr_tm", "xn_tm", "kv_d", "ks_d", "lepe_w", "lepe_b"], ["s0_f"]),
    "P3b": (["xq_tm", "wg", "wu", "wd", "s0_f", "x_fm", "ln_g", "ln_b"],
            ["x1_f", "xn2_bf", "xn2_f", "xni_bf"]),
    "P4b": (["xni_bf", "in_w", "conv_w", "conv_b", "xw_dt", "xw_B", "xw_C", "dt_w", "dt_b"],
            ["siluz", "xc_f", "dt_f", "dtu_f", "B_d", "C_d"]),
    "P4c": (["xn2_bf", "Wr"], ["ew_rows"]),
    "P6": (["dt_f", "dtu_f", "xc_f", "siluz", "B_d", "C_d", "A_log", "D_"], ["y3"]),
    "P7": (["wbf", "xn2_bf", "ew_rows", "bg_e", "bu_e", "bo_e"], ["moe_f"]),
    "P65": (["y3", "out_w", "glu_w", "glu_b", "xn2_f", "moe_f", "x1_f"], ["out"]),
}
ALL_PHASES = ["P1", "PW", "P2", "P3a", "P3b", "P4b", "P4c", "P6", "P7", "P65"]


def build_program(n_tok, phases, taps=()):
    nc = bass.Bass("TRN2", target_bir_lowering=False)
    N = n_tok
    NT1 = N // 128
    NT5 = N // 512
    NTC = N // TC
    phases = list(phases)
    need_in = set()
    need_out = set()
    produced = set()
    for ph in phases:
        ins, outs = PHASE_IO[ph]
        for x in ins:
            if x not in produced:
                need_in.add(x)
        for x in outs:
            produced.add(x)
    for ph in phases:
        for x in PHASE_IO[ph][1]:
            need_out.add(x)
    aps = {}
    always_const = {"ident"}
    for name in sorted(need_in | always_const):
        if name in WEIGHT_SHAPES:
            aps[name] = nc.dram_tensor(name, WEIGHT_SHAPES[name], F32, kind="ExternalInput").ap()
        elif name in ("x_tm",):
            aps[name] = nc.dram_tensor(name, [N, D], F32, kind="ExternalInput").ap()
        elif name == "x_fm":
            aps[name] = nc.dram_tensor(name, [D, N], F32, kind="ExternalInput").ap()
        else:
            s, dt = SCRATCH[name][0](N), SCRATCH[name][1]
            aps[name] = nc.dram_tensor(name, s, dt, kind="ExternalInput").ap()
    for name in sorted(need_out):
        if name == "out":
            aps[name] = nc.dram_tensor(name, [D, N], F32, kind="ExternalOutput").ap()
        else:
            s, dt = SCRATCH[name][0](N), SCRATCH[name][1]
            aps[name] = nc.dram_tensor(name, s, dt, kind="ExternalOutput").ap()
    for name, (sfn, dt) in SCRATCH.items():
        if name not in aps:
            kind = "ExternalOutput" if name in taps else "Internal"
            aps[name] = nc.dram_tensor(name, sfn(N), dt, kind=kind).ap()
    if "out" not in aps:
        aps["out"] = nc.dram_tensor("out", [D, N], F32,
                                    kind="ExternalOutput" if "P8" in phases else "Internal").ap()

    ctx = Ctx(nc)
    v = ctx.buf
    _uc = [0]
    def _u(name):
        _uc[0] += 1
        return "%s_%d" % (name, _uc[0])
    stack = contextlib.ExitStack()
    sems = {}
    semnames = ["pe", "act", "dve", "pool", "sync"]
    for q in ("sync", "pool", "act"):
        for i in range(NDMASEM):
            semnames.append("dq_%s_%d" % (q, i))
    for k in semnames:
        sems[k] = stack.enter_context(nc.semaphore(k))

    CONST = stack.enter_context(nc.sbuf_tensor(_u("CONST"), [128, 512], F32))
    SMALL = stack.enter_context(nc.sbuf_tensor(_u("SMALL"), [128, 512], F32))
    PP = stack.enter_context(nc.sbuf_tensor(_u("PP"), [128, 2048], F32))
    BC1 = stack.enter_context(nc.sbuf_tensor(_u("BC1"), [128, 2048], F32))
    IDT = stack.enter_context(nc.sbuf_tensor(_u("IDT"), [128, 128], F32))
    IDB = stack.enter_context(nc.sbuf_tensor(_u("IDB"), [128, 128], BF16))
    ONE = stack.enter_context(nc.sbuf_tensor(_u("ONE"), [128, 8], F32))
    ONEB = stack.enter_context(nc.sbuf_tensor(_u("ONEB"), [128, 8], BF16))

    ctx.dma("sync", IDT[:], aps["ident"][:], writes=["IDT"])
    ctx.op("dve", lambda e: e.tensor_copy(out=IDB[:], in_=IDT[:]), reads=["IDT"], writes=["IDB"])
    ctx.op("dve", lambda e: e.memset(ONE[:], 1.0), writes=["ONE"])
    ctx.op("dve", lambda e: e.memset(ONEB[:], 1.0), writes=["ONEB"])
    cmap = {"conv_w": (0, 64), "conv_b": (64, 16), "dt_b": (80, 16), "D_": (96, 16),
            "glu_b": (112, 16), "ln_g": (128, 8), "ln_b": (136, 8), "lepe_b": (144, 8)}
    for name, (off, w_) in cmap.items():
        if name in aps:
            ctx.dma("sync", CONST[:, off:off + w_], aps[name][:], writes=["CONST"])

    def rsqrt(out_ap, in_ap, t1, t2, rd, wr, nrows=128):
        ctx.op("dve", lambda e: e.tensor_scalar(out=t1.bitcast(I32), in0=in_ap.bitcast(I32),
                                                scalar1=1, scalar2=None,
                                                op0=ALU.logical_shift_right),
               reads=rd, writes=["rsq1"])
        ctx.op("dve", lambda e: e.tensor_scalar(out=t1.bitcast(I32), in0=t1.bitcast(I32),
                                                scalar1=-1, scalar2=0x5f3759df,
                                                op0=ALU.mult, op1=ALU.add),
               reads=["rsq1"], writes=["rsq1"])
        for _ in range(3):
            ctx.op("dve", lambda e: e.tensor_tensor(out=t2, in0=t1, in1=t1, op=ALU.mult),
                   reads=["rsq1"], writes=["rsq2"])
            ctx.op("dve", lambda e: e.tensor_tensor(out=t2, in0=t2, in1=in_ap, op=ALU.mult),
                   reads=["rsq2"] + rd, writes=["rsq2"])
            ctx.op("dve", lambda e: e.tensor_scalar(out=t2, in0=t2, scalar1=-0.5, scalar2=1.5,
                                                    op0=ALU.mult, op1=ALU.add),
                   reads=["rsq2"], writes=["rsq2"])
            ctx.op("dve", lambda e: e.tensor_tensor(out=t1, in0=t1, in1=t2, op=ALU.mult),
                   reads=["rsq1", "rsq2"], writes=["rsq1"])
        ctx.op("dve", lambda e: e.tensor_copy(out=out_ap, in_=t1),
               reads=["rsq1"], writes=wr)

    def quant(eng, x_ap, out_ap, F, bm, sg, vv, acc, tag, rd, wr):
        nb = F // 16
        x3 = x_ap.rearrange("p (a b) -> p a b", b=16)
        ctx.op(eng, lambda e: e.tensor_reduce(out=bm[:, 0:nb], in_=x3, axis=AX.X, op=ALU.max,
                                              apply_absolute_value=True),
               reads=rd, writes=[tag + "bm"])
        ctx.op(eng, lambda e: e.tensor_scalar(out=bm[:, 0:nb], in0=bm[:, 0:nb],
                                              scalar1=1.0 / 6.0, scalar2=1e-12,
                                              op0=ALU.mult, op1=ALU.max),
               reads=[tag + "bm"], writes=[tag + "bm"])
        ctx.op("dve", lambda e: e.reciprocal(out=bm[:, nb:2 * nb], in_=bm[:, 0:nb]),
               reads=[tag + "bm"], writes=[tag + "bm"])
        ctx.op("act", lambda e: e.sign(out=sg[:, 0:F], in_=x_ap), reads=rd, writes=[tag + "sg"])
        ctx.op("act", lambda e: e.activation(out=vv[:, 0:F], in_=x_ap, func=AF.Abs),
               reads=rd, writes=[tag + "v"])
        rscb = bm[:, nb:2 * nb].unsqueeze(2).broadcast_to((128, nb, 16))
        v3 = vv[:, 0:F].rearrange("p (a b) -> p a b", b=16)
        ctx.op(eng, lambda e: e.tensor_tensor(out=v3, in0=v3, in1=rscb, op=ALU.mult),
               reads=[tag + "v", tag + "bm"], writes=[tag + "v"])
        ctx.op(eng, lambda e: e.tensor_scalar(out=bm[:, 0:nb], in0=bm[:, 0:nb], scalar1=0.5,
                                              scalar2=None, op0=ALU.mult),
               reads=[tag + "bm"], writes=[tag + "bm"])
        schb = bm[:, 0:nb].unsqueeze(2).broadcast_to((128, nb, 16))
        s3 = sg[:, 0:F].rearrange("p (a b) -> p a b", b=16)
        ctx.op(eng, lambda e: e.tensor_tensor(out=s3, in0=s3, in1=schb, op=ALU.mult),
               reads=[tag + "sg", tag + "bm"], writes=[tag + "sg"])
        ctx.op(eng, lambda e: e.tensor_scalar(out=acc[:, 0:F], in0=vv[:, 0:F], scalar1=5.0,
                                              scalar2=2.0, op0=ALU.is_gt, op1=ALU.mult),
               reads=[tag + "v"], writes=[tag + "acc"])
        for m in (3.5, 2.5):
            ctx.op(eng, lambda e, m=m: e.scalar_tensor_tensor(out=acc[:, 0:F], in0=vv[:, 0:F],
                                                              scalar=m, in1=acc[:, 0:F],
                                                              op0=ALU.is_gt, op1=ALU.add),
                   reads=[tag + "v", tag + "acc"], writes=[tag + "acc"])
        ctx.op(eng, lambda e: e.tensor_scalar(out=acc[:, 0:F], in0=acc[:, 0:F], scalar1=2.0,
                                              scalar2=None, op0=ALU.mult),
               reads=[tag + "acc"], writes=[tag + "acc"])
        for m in (1.75, 1.25, 0.75, 0.25):
            ctx.op(eng, lambda e, m=m: e.scalar_tensor_tensor(out=acc[:, 0:F], in0=vv[:, 0:F],
                                                              scalar=m, in1=acc[:, 0:F],
                                                              op0=ALU.is_gt, op1=ALU.add),
                   reads=[tag + "v", tag + "acc"], writes=[tag + "acc"])
        ctx.op(eng, lambda e: e.tensor_tensor(out=out_ap, in0=acc[:, 0:F], in1=sg[:, 0:F],
                                              op=ALU.mult),
               reads=[tag + "acc", tag + "sg"], writes=wr)

    def rrange(ap_, tmp_, rdwr):
        ctx.op("dve", lambda e: e.tensor_scalar(out=tmp_, in0=ap_, scalar1=1.0 / TWO_PI,
                                                scalar2=MAGIC, op0=ALU.mult, op1=ALU.add),
               reads=[rdwr], writes=["rrtmp"])
        ctx.op("dve", lambda e: e.tensor_scalar(out=tmp_, in0=tmp_, scalar1=MAGIC, scalar2=None,
                                                op0=ALU.subtract),
               reads=["rrtmp"], writes=["rrtmp"])
        ctx.op("dve", lambda e: e.scalar_tensor_tensor(out=ap_, in0=tmp_, scalar=-TWO_PI,
                                                       in1=ap_, op0=ALU.mult, op1=ALU.add),
               reads=["rrtmp", rdwr], writes=[rdwr])

    def cast_load(dst_ap, src_ap, stage_ap, tag, cols):
        ctx.dma("pool", dst_ap, src_ap, reads=[tag], writes=[tag])

    def _ph_prologue():
        with (nc.sbuf_tensor(_u("XA"), [128, 6656], F32) as XA,
              nc.sbuf_tensor(_u("XB"), [128, 4096], F32) as XB,
              nc.sbuf_tensor(_u("B16"), [128, 4096], BF16) as B16,
              nc.sbuf_tensor(_u("QXA"), [128, 7680], F32) as QXA,
              nc.sbuf_tensor(_u("QXB"), [128, 8448], F32) as QXB,
              nc.sbuf_tensor(_u("QB16A"), [128, 11264], BF16) as QB16A,
              nc.sbuf_tensor(_u("QB16B"), [128, 512], BF16) as QB16B,
              nc.sbuf_tensor(_u("SM2"), [128, 512], F32) as SM2,
              nc.psum_tensor(_u("PST0"), [128, 128], BF16) as PST0,
              nc.psum_tensor(_u("PST1"), [128, 128], BF16) as PST1):
            ctx.op("pool", lambda e: e.iota(PP[0:1, 0:512], [[1, 512]], base=0,
                                            channel_multiplier=0,
                                            allow_small_or_imprecise_dtypes=True),
                   writes=["PP"])
            ctx.op("act", lambda e: e.activation(out=PP[0:1, 512:1024], in_=PP[0:1, 0:512],
                                                 func=AF.Exp, scale=-math.log(10000.0) / 512.0),
                   reads=["PP"], writes=["PP"])
            ctx.dma("pool", aps["theta_row"][:], PP[0:1, 512:1024], reads=["PP"],
                    writes=["theta_d"])
            th_b = bass.AP(aps["theta_row"].tensor, 0, [[0, 128], [1, 512]])
            ctx.dma("sync", BC1[:, 0:512], th_b, reads=["theta_d"], writes=["BC1"])

            def p1_tile(i):
                par = i % 2
                x_t = XA[:, par * 1024:par * 1024 + 1024]
                xn_t = XA[:, 2048:3072]
                xtag = "p1x%d" % par
                ctx.dma("sync", x_t, aps["x_tm"][i * 128:(i + 1) * 128, :], writes=[xtag])
                ctx.op("dve", lambda e, x_t=x_t: e.bn_stats(out=SMALL[:, 0:6],
                                                            in_=x_t[:, 0:512]),
                       reads=[xtag], writes=["SMALL"])
                ctx.op("dve", lambda e, x_t=x_t: e.bn_stats(out=SMALL[:, 6:12],
                                                            in_=x_t[:, 512:1024]),
                       reads=[xtag], writes=["SMALL"])
                ctx.op("dve", lambda e: e.bn_aggr(out=SMALL[:, 12:14], in_=SMALL[:, 0:12]),
                       reads=["SMALL"], writes=["SMALL"])
                ctx.op("dve", lambda e: e.tensor_scalar(out=SMALL[:, 14:15], in0=SMALL[:, 13:14],
                                                        scalar1=1e-5, scalar2=None, op0=ALU.add),
                       reads=["SMALL"], writes=["SMALL"])
                rsqrt(SMALL[:, 15:16], SMALL[:, 14:15], SMALL[:, 16:17], SMALL[:, 17:18],
                      ["SMALL"], ["SMALL"])
                ctx.op("dve", lambda e, x_t=x_t, xn_t=xn_t: e.tensor_scalar(
                    out=xn_t, in0=x_t, scalar1=SMALL[:, 12:13], scalar2=SMALL[:, 15:16],
                    op0=ALU.subtract, op1=ALU.mult),
                       reads=[xtag, "SMALL"], writes=["p1xn"])
                nbt = "p1xnb%d" % par
                ctx.op("dve", lambda e, xn_t=xn_t, par=par: e.tensor_copy(
                    out=B16[:, par * 1024:par * 1024 + 1024], in_=xn_t),
                       reads=["p1xn"], writes=[nbt])
                ctx.dma("act", aps["xn_tm"][i * 128:(i + 1) * 128, :],
                        B16[:, par * 1024:par * 1024 + 1024],
                        reads=[nbt], writes=["xn_tm_d"])
                qbt = "p1xqb%d" % par
                quant("dve", xn_t, B16[:, 2048 + par * 1024:2048 + par * 1024 + 1024], 1024,
                      SMALL[:, 64:192], XB[:, 0:1024],
                      XB[:, 1024:2048], XB[:, 2048:3072], "p1q", ["p1xn"], [qbt])
                ctx.dma("act", aps["xq_tm"][i * 128:(i + 1) * 128, :],
                        B16[:, 2048 + par * 1024:2048 + par * 1024 + 1024],
                        reads=[qbt], writes=["xq_tm_d"])
                ctx.op("pool", lambda e, i=i: e.iota(SMALL[:, 20:21], [[0, 1]], base=i * 128,
                                                     channel_multiplier=1,
                                                     allow_small_or_imprecise_dtypes=True),
                       writes=["p1tok"])
                a_s = XA[:, 3072:3584]
                a_c = XA[:, 3584:4096]
                ctx.op("dve", lambda e, a_s=a_s: e.tensor_scalar(out=a_s, in0=BC1[:, 0:512],
                                                                 scalar1=SMALL[:, 20:21],
                                                                 scalar2=None, op0=ALU.mult),
                       reads=["BC1", "p1tok"], writes=["p1as"])
                ctx.op("dve", lambda e, a_s=a_s, a_c=a_c: e.tensor_scalar(
                    out=a_c, in0=a_s, scalar1=math.pi / 2, scalar2=None, op0=ALU.add),
                       reads=["p1as"], writes=["p1ac"])
                rrange(a_s, XA[:, 4096:4608], "p1as")
                rrange(a_c, XA[:, 4096:4608], "p1ac")
                ctx.op("act", lambda e, a_s=a_s: e.activation(out=XA[:, 4608:5120], in_=a_s,
                                                              func=AF.Sin),
                       reads=["p1as"], writes=["p1sin"])
                ctx.dma("act", aps["sin_sp"][i * 128:(i + 1) * 128, :], XA[:, 4608:5120],
                        reads=["p1sin"], writes=["sin_d"])
                ctx.op("act", lambda e, a_c=a_c: e.activation(out=XA[:, 5120:5632], in_=a_c,
                                                              func=AF.Sin),
                       reads=["p1ac"], writes=["p1cos"])
                ctx.dma("act", aps["cos_sp"][i * 128:(i + 1) * 128, :], XA[:, 5120:5632],
                        reads=["p1cos"], writes=["cos_d"])

            _punit = [0]

            def p30_gu(wname, dname, mc):
                par = _punit[0] % 2
                _punit[0] += 1
                xtag = "p30x%d" % par
                qtag = "p30qb%d" % par
                qa = QXA[:, par * 1024:par * 1024 + 1024]
                qb = QB16A[:, par * 5632:par * 5632 + 1024]
                ctx.dma("sync", qa, aps[wname][mc * 128:(mc + 1) * 128, :],
                        reads=[xtag], writes=[xtag])
                quant("dve", qa, qb, 1024, SM2[:, 0:256],
                      QXB[:, 0:1024], QXB[:, 1024:2048], QXB[:, 2048:3072], "p30q",
                      [xtag], [qtag])
                for kc in range(8):
                    pst = PST0 if kc % 2 == 0 else PST1
                    ptag = "PST%d" % (kc % 2)
                    btag = "p30tb%d" % (kc % 4)
                    bsl = QB16B[:, (kc % 4) * 128:(kc % 4) * 128 + 128]
                    ctx.op("pe", lambda e, kc=kc, pst=pst, qb=qb: e.transpose(
                        pst[:], qb[:, kc * 128:(kc + 1) * 128], IDB[:]),
                           reads=[qtag, "IDB", btag], writes=[ptag])
                    ctx.op("dve", lambda e, bsl=bsl, pst=pst: e.tensor_copy(out=bsl, in_=pst[:]),
                           reads=[ptag], writes=[btag])
                    ctx.dma("pool", aps[dname][kc * 128:(kc + 1) * 128, mc * 128:(mc + 1) * 128],
                            bsl, reads=[btag], writes=[dname + "_d"])

            def p30_wd(dc):
                par = _punit[0] % 2
                _punit[0] += 1
                xtag = "p30x%d" % par
                qtag = "p30qb%d" % par
                qa = QXA[:, 2048 + par * 2816:2048 + par * 2816 + 2816]
                qb = QB16A[:, par * 5632:par * 5632 + 2816]
                ctx.dma("sync", qa, aps["wd"][dc * 128:(dc + 1) * 128, :],
                        reads=[xtag], writes=[xtag])
                quant("dve", qa, qb, 2816, SM2[:, 0:384],
                      QXB[:, 0:2816], QXB[:, 2816:5632], QXB[:, 5632:8448],
                      "p30q", [xtag], [qtag])
                for mc in range(22):
                    pst = PST0 if mc % 2 == 0 else PST1
                    ptag = "PST%d" % (mc % 2)
                    btag = "p30tb%d" % (mc % 4)
                    bsl = QB16B[:, (mc % 4) * 128:(mc % 4) * 128 + 128]
                    ctx.op("pe", lambda e, mc=mc, pst=pst, qb=qb: e.transpose(
                        pst[:], qb[:, mc * 128:(mc + 1) * 128], IDB[:]),
                           reads=[qtag, "IDB", btag], writes=[ptag])
                    ctx.op("dve", lambda e, bsl=bsl, pst=pst: e.tensor_copy(out=bsl, in_=pst[:]),
                           reads=[ptag], writes=[btag])
                    ctx.dma("pool", aps["wdqT_d"][dc, mc * 128:(mc + 1) * 128, :],
                            bsl, reads=[btag], writes=["wdqT_d_d"])

            lp1 = [lambda i=i: p1_tile(i) for i in range(NT1)]
            l30 = ([lambda w=w, d=d, mc=mc: p30_gu(w, d, mc)
                    for w, d in (("wg", "wgqT_d"), ("wu", "wuqT_d")) for mc in range(22)]
                   + [lambda dc=dc: p30_wd(dc) for dc in range(8)])
            r30 = float(len(l30)) / float(len(lp1))
            credit = 0.0
            j = 0
            for t in lp1:
                t()
                credit += r30
                while credit >= 1.0 and j < len(l30):
                    l30[j]()
                    j += 1
                    credit -= 1.0
            while j < len(l30):
                l30[j]()
                j += 1
        ctx.barrier()

    def _ph_casts():
        GU = NE * D * ME
        for e_ in range(NE):
            for base, srcname, nr, ncol in ((0, "Wg_e", D, ME), (GU, "Wu_e", D, ME),
                                            (2 * GU, "Wo_e", ME, D)):
                dst = bass.AP(aps["wbf"].tensor, base + e_ * D * ME, [[ncol, nr], [1, ncol]])
                ctx.dma("pool", dst, aps[srcname][e_], writes=["wbf_d"])

    def _ph_P2a():
        with (nc.sbuf_tensor(_u("W1"), [128, 16384], BF16) as W1,
              nc.sbuf_tensor(_u("XA"), [128, 6144], F32) as XA,
              nc.sbuf_tensor(_u("XB"), [128, 4096], F32) as XB,
              nc.sbuf_tensor(_u("B16S"), [128, 4096], BF16) as B16S,
              nc.sbuf_tensor(_u("B16T"), [128, 4096], BF16) as B16T,
              nc.psum_tensor(_u("PSQ0"), [128, 512], F32) as PSQ0,
              nc.psum_tensor(_u("PSQ1"), [128, 512], F32) as PSQ1,
              nc.psum_tensor(_u("PSKV0"), [128, 512], F32) as PSKV0,
              nc.psum_tensor(_u("PSKV1"), [128, 512], F32) as PSKV1,
              nc.psum_tensor(_u("PSKS"), [128, 8], F32) as PSKS):
            for c in range(8):
                cast_load(W1[:, c * 2048:(c + 1) * 2048],
                          aps["qk_w"][c * 128:(c + 1) * 128, :], PP, "w1", 2048)
            qkb_b = bass.AP(aps["qk_b"].tensor, 0, [[0, 128], [1, 2048]])
            ctx.dma("sync", BC1[:, 0:2048], qkb_b, reads=["BC1"], writes=["BC1"])
            NT5_ = N // 512
            for st in range(NT5_):
                n0 = st * 512
                for c in range(8):
                    ctx.dmat("sync", B16S[:, c * 512:(c + 1) * 512],
                             aps["xn_tm"][n0:n0 + 512, c * 128:(c + 1) * 128],
                             reads=["xn_tm_d"], writes=["p2xnT"])
                for sub in range(4):
                    i = st * 4 + sub
                    first = (i == 0)
                    last = (i == NT1 - 1)
                    ctx.dma("sync", B16T[:, 0:1024],
                            aps["xn_tm"][i * 128:(i + 1) * 128, :],
                            reads=["xn_tm_d"], writes=["p2vh"])
                    for fg in range(4):
                        ps = PSQ0 if fg % 2 == 0 else PSQ1
                        pbuf = "PSQ%d" % (fg % 2)
                        def mmq2(e, fg=fg, sub=sub, ps=ps):
                            ins = None
                            for k in range(8):
                                ins = e.matmul(ps[:],
                                               B16S[:, k * 512 + sub * 128: k * 512 + sub * 128 + 128],
                                               W1[:, k * 2048 + fg * 512: k * 2048 + fg * 512 + 512],
                                               start=(k == 0), stop=(k == 7))
                            return ins
                        ctx.op("pe", mmq2, reads=["p2xnT", "w1"], writes=[pbuf])
                        ctx.op("dve", lambda e, fg=fg, ps=ps: e.tensor_tensor(
                            out=XA[:, fg * 512:(fg + 1) * 512], in0=ps[:],
                            in1=BC1[:, fg * 512:(fg + 1) * 512], op=ALU.add),
                               reads=[pbuf, "BC1"], writes=["p2qk"])
                    for half, dst0 in ((0, 4096), (1, 5120)):
                        src = XA[:, half * 1024:(half + 1) * 1024]
                        ctx.op("dve", lambda e, src=src: e.tensor_scalar_min(
                            out=XB[:, 0:1024], in0=src, scalar1=0.0),
                               reads=["p2qk"], writes=["p2t"])
                        ctx.op("act", lambda e: e.activation(out=XB[:, 0:1024], in_=XB[:, 0:1024],
                                                             func=AF.Exp),
                               reads=["p2t"], writes=["p2t"])
                        ctx.op("dve", lambda e, src=src, dst0=dst0: e.scalar_tensor_tensor(
                            out=XA[:, dst0:dst0 + 1024], in0=src, scalar=0.0,
                            in1=XB[:, 0:1024], op0=ALU.max, op1=ALU.add),
                               reads=["p2qk", "p2t"], writes=["p2q" if half == 0 else "p2k"])
                    ctx.op("dve", lambda e: e.tensor_copy(out=B16T[:, 1024:2048],
                                                          in_=XA[:, 4096:5120]),
                           reads=["p2q"], writes=["p2qb"])
                    ctx.dma("pool", aps["q_tm"][i * 128:(i + 1) * 128, :], B16T[:, 1024:2048],
                            reads=["p2qb"], writes=["q_tm_d"])
                    ctx.op("dve", lambda e: e.tensor_copy(out=B16T[:, 2048:3072],
                                                          in_=XA[:, 5120:6144]),
                           reads=["p2k"], writes=["p2kb"])
                    def mmks(e, first=first, last=last):
                        ins = None
                        for c in range(8):
                            ins = e.matmul(PSKS[:, c:c + 1], B16T[:, 2048 + c * 128:2048 + (c + 1) * 128],
                                           ONEB[:, 0:1], start=(first and c == 0), stop=last)
                        return ins
                    ctx.op("pe", mmks, reads=["p2kb", "ONEB"], writes=["PSKS"])
                    ctx.dma("sync", XB[:, 1024:1536], aps["cos_sp"][i * 128:(i + 1) * 128, :],
                            reads=["cos_d"], writes=["p2cos"])
                    ctx.dma("sync", XB[:, 1536:2048], aps["sin_sp"][i * 128:(i + 1) * 128, :],
                            reads=["sin_d"], writes=["p2sin"])
                    for half, qoff, rbuf in ((0, 4096, "p2qr"), (1, 5120, "p2kr")):
                        qe = XA[:, qoff:qoff + 512]
                        qo = XA[:, qoff + 512:qoff + 1024]
                        ctx.op("dve", lambda e, qe=qe: e.tensor_tensor(out=XB[:, 2048:2560],
                                                                       in0=XB[:, 1024:1536], in1=qe,
                                                                       op=ALU.mult),
                               reads=["p2cos", "p2q", "p2k"], writes=["p2r1"])
                        ctx.op("dve", lambda e, qo=qo: e.tensor_tensor(out=XB[:, 2560:3072],
                                                                       in0=XB[:, 1536:2048], in1=qo,
                                                                       op=ALU.mult),
                               reads=["p2sin", "p2q", "p2k"], writes=["p2r2"])
                        ctx.op("dve", lambda e: e.tensor_tensor(out=XB[:, 3072:3584],
                                                                in0=XB[:, 2048:2560],
                                                                in1=XB[:, 2560:3072],
                                                                op=ALU.subtract),
                               reads=["p2r1", "p2r2"], writes=["p2pr"])
                        ctx.op("dve", lambda e, qo=qo: e.tensor_tensor(out=XB[:, 2048:2560],
                                                                       in0=XB[:, 1024:1536], in1=qo,
                                                                       op=ALU.mult),
                               reads=["p2cos", "p2q", "p2k", "p2r1"], writes=["p2r1"])
                        ctx.op("dve", lambda e, qe=qe: e.tensor_tensor(out=XB[:, 2560:3072],
                                                                       in0=XB[:, 1536:2048], in1=qe,
                                                                       op=ALU.mult),
                               reads=["p2sin", "p2q", "p2k", "p2r2"], writes=["p2r2"])
                        ctx.op("dve", lambda e: e.tensor_tensor(out=XB[:, 3584:4096],
                                                                in0=XB[:, 2048:2560],
                                                                in1=XB[:, 2560:3072],
                                                                op=ALU.add),
                               reads=["p2r1", "p2r2"], writes=["p2pi"])
                        ctx.op("dve", lambda e: e.tensor_copy(out=B16T[:, 3072:3584],
                                                              in_=XB[:, 3072:3584]),
                               reads=["p2pr"], writes=[rbuf])
                        ctx.op("dve", lambda e: e.tensor_copy(out=B16T[:, 3584:4096],
                                                              in_=XB[:, 3584:4096]),
                               reads=["p2pi"], writes=[rbuf])
                        if half == 0:
                            ctx.dma("pool", aps["qr_tm"][i * 128:(i + 1) * 128, :],
                                    B16T[:, 3072:4096], reads=[rbuf], writes=["qr_tm_d"])
                        else:
                            ctx.op("dve", lambda e: e.tensor_copy(out=B16T[:, 2048:3072],
                                                                  in_=B16T[:, 3072:4096]),
                                   reads=[rbuf], writes=["p2kb"])
                    def mmkv(e, first=first, last=last):
                        ins = None
                        for h in range(8):
                            ps = PSKV0 if h < 4 else PSKV1
                            ins = e.matmul(ps[:, (h % 4) * 128:(h % 4 + 1) * 128],
                                           B16T[:, 2048 + h * 128:2048 + (h + 1) * 128],
                                           B16T[:, h * 128:(h + 1) * 128],
                                           start=(first and h % 4 == 0), stop=last)
                        return ins
                    ctx.op("pe", mmkv, reads=["p2kb", "p2vh"], writes=["PSKV0", "PSKV1"])
            sc = 1.0 / math.sqrt(float(N))
            ctx.op("act", lambda e: e.activation(out=B16T[:, 0:512], in_=PSKV0[:],
                                                 func=AF.Identity, scale=sc),
                   reads=["PSKV0"], writes=["p2kvb"])
            ctx.op("act", lambda e: e.activation(out=B16T[:, 512:1024], in_=PSKV1[:],
                                                 func=AF.Identity, scale=sc),
                   reads=["PSKV1"], writes=["p2kvb"])
            ctx.dma("pool", aps["kv_d"][:], B16T[:, 0:1024], reads=["p2kvb"], writes=["kv_dd"])
            ctx.op("dve", lambda e: e.tensor_copy(out=SMALL[:, 0:8], in_=PSKS[:]),
                   reads=["PSKS"], writes=["p2ks"])
            ctx.dma("pool", aps["ks_d"][:], SMALL[:, 0:8], reads=["p2ks"], writes=["ks_dd"])
        ctx.barrier()

    def _ph_P3aa():
        with (nc.sbuf_tensor(_u("W2"), [128, 24576], BF16) as W2,
              nc.sbuf_tensor(_u("XA"), [128, 4096], F32) as XA,
              nc.sbuf_tensor(_u("XD"), [128, 2048], F32) as XD,
              nc.sbuf_tensor(_u("B16S"), [128, 8192], BF16) as B16S,
              nc.sbuf_tensor(_u("B16T"), [128, 2048], BF16) as B16T,
              nc.sbuf_tensor(_u("KVZ"), [128, 1024], BF16) as KVZ,
              nc.psum_tensor(_u("PSL0"), [128, 512], F32) as PSL0,
              nc.psum_tensor(_u("PSL1"), [128, 512], F32) as PSL1,
              nc.psum_tensor(_u("PSA"), [128, 512], F32) as PSA,
              nc.psum_tensor(_u("PSZ"), [128, 128], F32) as PSZ):
            for w_ in range(3):
                for kc in range(8):
                    cast_load(W2[:, (w_ * 8 + kc) * 1024:(w_ * 8 + kc + 1) * 1024],
                              aps["lepe_w"][w_, kc * 128:(kc + 1) * 128, :], PP, "w2", 1024)
            ctx.dma("sync", KVZ[:], aps["kv_d"][:], reads=["kv_dd"], writes=["KVZ"])
            ctx.dma("sync", SMALL[:, 0:8], aps["ks_d"][:], reads=["ks_dd"], writes=["p3ks"])
            ctx.op("dve", lambda e: e.tensor_scalar_mul(out=SMALL[:, 8:16], in0=SMALL[:, 0:8],
                                                        scalar1=1.0 / float(N)),
                   reads=["p3ks"], writes=["p3km"])
            ctx.op("pe", lambda e: e.transpose(PSZ[0:8, :], SMALL[:, 8:16], IDT[:]),
                   reads=["p3km", "IDT"], writes=["PSZ"])
            ctx.op("dve", lambda e: e.tensor_copy(out=PP[0:8, 0:128], in_=PSZ[0:8, :]),
                   reads=["PSZ"], writes=["p3kmr"])
            km_dst = bass.AP(aps["kmean_row"].tensor, 0, [[128, 8], [1, 128]])
            ctx.dma("pool", km_dst, PP[0:8, 0:128], reads=["p3kmr"], writes=["kmean_d"])
            km_b = bass.AP(aps["kmean_row"].tensor, 0, [[0, 128], [1, 1024]])
            ctx.dma("sync", BC1[:, 0:1024], km_b, reads=["kmean_d", "BC1"], writes=["BC1"])
            for i in range(NT1):
                ctx.dma("sync", B16T[:, 0:1024], aps["q_tm"][i * 128:(i + 1) * 128, :],
                        reads=["q_tm_d"], writes=["p3q"])
                ctx.op("dve", lambda e: e.tensor_tensor(out=XA[:, 0:1024], in0=B16T[:, 0:1024],
                                                        in1=BC1[:, 0:1024], op=ALU.mult),
                       reads=["p3q", "BC1"], writes=["p3t"])
                den3 = XA[:, 0:1024].rearrange("p (g d) -> p g d", d=64)
                ctx.op("dve", lambda e, den3=den3: e.tensor_reduce(out=SMALL[:, 24:40], in_=den3,
                                                                   axis=AX.X, op=ALU.add),
                       reads=["p3t"], writes=["p3den"])
                ctx.op("dve", lambda e: e.tensor_tensor(out=SMALL[:, 40:48], in0=SMALL[:, 24:32],
                                                        in1=SMALL[:, 32:40], op=ALU.add),
                       reads=["p3den"], writes=["p3den"])
                ctx.op("dve", lambda e: e.tensor_scalar(out=SMALL[:, 40:48], in0=SMALL[:, 40:48],
                                                        scalar1=1e-6, scalar2=None, op0=ALU.add),
                       reads=["p3den"], writes=["p3den"])
                ctx.op("dve", lambda e: e.reciprocal(out=SMALL[:, 48:56], in_=SMALL[:, 40:48]),
                       reads=["p3den"], writes=["p3z"])
                ctx.op("pe", lambda e: e.transpose(PSZ[0:8, :], SMALL[:, 48:56], IDT[:]),
                       reads=["p3z", "IDT", "p3zr"], writes=["PSZ"])
                ctx.op("dve", lambda e, i=i: e.tensor_copy(out=PP[64:72, (i % 4) * 128:(i % 4 + 1) * 128],
                                                           in_=PSZ[0:8, :]),
                       reads=["PSZ"], writes=["p3zr"])
                if i % 4 == 3:
                    st = i // 4
                    ctx.dma("pool", aps["z_rows"][:, st * 512:(st + 1) * 512], PP[64:72, 0:512],
                            reads=["p3zr"], writes=["z_rows_d"])
            for st in range(N // 512):
                n0 = st * 512
                for c in range(8):
                    dst = B16S[:, c * 576:(c + 1) * 576]
                    ctx.dmat("sync", dst[:, 32:544],
                             aps["xn_tm"][n0:n0 + 512, c * 128:(c + 1) * 128],
                             reads=["xn_tm_d"], writes=["p3xh"])
                    if st > 0:
                        ctx.dmat("sync", B16T[:, 1536:1664],
                                 aps["xn_tm"][n0 - 128:n0, c * 128:(c + 1) * 128],
                                 reads=["xn_tm_d"], writes=["p3edge"])
                        ctx.op("dve", lambda e, dst=dst: e.tensor_copy(out=dst[:, 31:32],
                                                                      in_=B16T[:, 1663:1664]),
                               reads=["p3edge"], writes=["p3xh"])
                    else:
                        ctx.op("dve", lambda e, dst=dst: e.memset(dst[:, 31:32], 0.0),
                               writes=["p3xh"])
                    if st < N // 512 - 1:
                        ctx.dmat("sync", B16T[:, 1536:1664],
                                 aps["xn_tm"][n0 + 512:n0 + 640, c * 128:(c + 1) * 128],
                                 reads=["xn_tm_d"], writes=["p3edge"])
                        ctx.op("dve", lambda e, dst=dst: e.tensor_copy(out=dst[:, 544:545],
                                                                      in_=B16T[:, 1536:1537]),
                               reads=["p3edge"], writes=["p3xh"])
                    else:
                        ctx.op("dve", lambda e, dst=dst: e.memset(dst[:, 544:545], 0.0),
                               writes=["p3xh"])
                for h in range(8):
                    psl = PSL0 if h % 2 == 0 else PSL1
                    plbuf = "PSL%d" % (h % 2)
                    def mml(e, h=h, psl=psl):
                        ins = None
                        for w_ in range(3):
                            for kc in range(8):
                                ins = e.matmul(psl[:],
                                               W2[:, (w_ * 8 + kc) * 1024 + h * 128:(w_ * 8 + kc) * 1024 + (h + 1) * 128],
                                               B16S[:, kc * 576 + 31 + w_: kc * 576 + 31 + w_ + 512],
                                               start=(w_ == 0 and kc == 0), stop=(w_ == 2 and kc == 7))
                        return ins
                    ctx.op("pe", mml, reads=["w2", "p3xh"], writes=[plbuf])
                    ctx.dmat("sync", B16T[:, 1024:1536],
                             aps["qr_tm"][n0:n0 + 512, h * 128:(h + 1) * 128],
                             reads=["qr_tm_d"], writes=["p3qrT"])
                    def mma(e, h=h):
                        return e.matmul(PSA[:], KVZ[:, h * 128:(h + 1) * 128], B16T[:, 1024:1536],
                                        start=True, stop=True)
                    ctx.op("pe", mma, reads=["KVZ", "p3qrT"], writes=["PSA"])
                    zb = bass.AP(aps["z_rows"].tensor, h * N + n0, [[0, 128], [1, 512]])
                    ctx.dma("sync", XD[:, 0:512], zb, reads=["z_rows_d"], writes=["p3zb"])
                    ctx.op("dve", lambda e: e.tensor_tensor(out=XD[:, 512:1024], in0=PSA[:],
                                                            in1=XD[:, 0:512], op=ALU.mult),
                           reads=["PSA", "p3zb"], writes=["p3oz"])
                    ctx.op("dve", lambda e, h=h, psl=psl: e.scalar_tensor_tensor(
                        out=XD[:, 1024:1536], in0=psl[:], scalar=CONST[:, 144 + h:145 + h],
                        in1=XD[:, 512:1024], op0=ALU.add, op1=ALU.add),
                           reads=[plbuf, "p3oz", "CONST"], writes=["p3s0"])
                    ctx.dma("pool", aps["s0_f"][h * 128:(h + 1) * 128, n0:n0 + 512],
                            XD[:, 1024:1536], reads=["p3s0"], writes=["s0_d"])
        ctx.barrier()

    def _ph_P3bma():
        with (nc.sbuf_tensor(_u("W1"), [128, 22528], BF16) as W1,
              nc.sbuf_tensor(_u("W2"), [128, 22528], BF16) as W2,
              nc.sbuf_tensor(_u("W3"), [128, 2816], BF16) as W3,
              nc.sbuf_tensor(_u("XA"), [128, 2816], F32) as XA,
              nc.sbuf_tensor(_u("XC"), [128, 4096], F32) as XC,
              nc.sbuf_tensor(_u("XD"), [128, 1536], F32) as XD,
              nc.sbuf_tensor(_u("XB"), [128, 8448], F32) as XB,
              nc.sbuf_tensor(_u("B16A"), [128, 2816], BF16) as B16A,
              nc.sbuf_tensor(_u("B16B"), [128, 11264], BF16) as B16B,
              nc.psum_tensor(_u("PSG0"), [128, 512], F32) as PSG0,
              nc.psum_tensor(_u("PSG1"), [128, 512], F32) as PSG1,
              nc.psum_tensor(_u("PSG2"), [128, 512], F32) as PSG2,
              nc.psum_tensor(_u("PSG3"), [128, 512], F32) as PSG3,
              nc.psum_tensor(_u("PSG4"), [128, 512], F32) as PSG4,
              nc.psum_tensor(_u("PSG5"), [128, 512], F32) as PSG5,
              nc.psum_tensor(_u("PSD0"), [128, 512], F32) as PSD0,
              nc.psum_tensor(_u("PSD1"), [128, 512], F32) as PSD1):
            PSG = [PSG0, PSG1, PSG2, PSG3, PSG4, PSG5]
            for kc in (range(8) if int(os.environ.get("P3B_LVL", "9")) >= 1 else []):
                ctx.dma("sync", W1[:, kc * 2816:(kc + 1) * 2816],
                        aps["wgqT_d"][kc * 128:(kc + 1) * 128, :],
                        reads=["wgqT_d_d"], writes=["p3w1"])
                ctx.dma("sync", W2[:, kc * 2816:(kc + 1) * 2816],
                        aps["wuqT_d"][kc * 128:(kc + 1) * 128, :],
                        reads=["wuqT_d_d"], writes=["p3w2"])
            for st in range(N // 512):
                n0 = st * 512
                for c in range(8):
                    ctx.dmat("sync", B16B[:, c * 512:(c + 1) * 512],
                             aps["xq_tm"][n0:n0 + 512, c * 128:(c + 1) * 128],
                             reads=["xq_tm_d"], writes=["p3xqT"])
                for ns in range(4):
                    mgs = [(0, 512), (512, 1024), (1024, 1536), (1536, 2048), (2048, 2560), (2560, 2816)]
                    for W_, tag in ((W1, "p3w1"), (W2, "p3w2")):
                        def mmg(e, W_=W_, ns=ns):
                            ins = None
                            for gi, (a, b) in enumerate(mgs):
                                for k in range(8):
                                    ins = e.matmul(PSG[gi][:, 0:b - a],
                                                   B16B[:, k * 512 + ns * 128: k * 512 + ns * 128 + 128],
                                                   W_[:, k * 2816 + a: k * 2816 + b],
                                                   start=(k == 0), stop=(k == 7))
                            return ins
                        if W_ is W1:
                            ctx.op("pe", mmg, reads=["p3xqT", tag], writes=["PSGx"])
                            for gi, (a, b) in enumerate(mgs):
                                ctx.op("act", lambda e, gi=gi, a=a, b=b: e.activation(
                                    out=XA[:, a:b], in_=PSG[gi][:, 0:b - a], func=AF.Silu),
                                       reads=["PSGx"], writes=["p3sg"])
                        else:
                            ctx.op("pe", mmg, reads=["p3xqT", tag, "p3sg"], writes=["PSGx"])
                            for gi, (a, b) in enumerate(mgs):
                                ctx.op("dve", lambda e, gi=gi, a=a, b=b: e.tensor_tensor(
                                    out=B16A[:, a:b], in0=XA[:, a:b], in1=PSG[gi][:, 0:b - a],
                                    op=ALU.mult),
                                       reads=["PSGx", "p3sg"], writes=["p3h"])
                    quant("dve", B16A[:, 0:2816], B16A[:, 0:2816], 2816, SMALL[:, 64:416],
                          XB[:, 0:2816], XB[:, 2816:5632], XB[:, 5632:8448], "p3hq",
                          ["p3h"], ["p3hqb", "p3h"])
                    ctx.dma("act", aps["hq_st"][ns * 128:(ns + 1) * 128, :], B16A[:, 0:2816],
                            reads=["p3hqb", "p3h"], writes=["hq_st_d"])
                for mc in range(22):
                    ctx.dmat("sync", B16B[:, mc * 512:(mc + 1) * 512],
                             aps["hq_st"][0:512, mc * 128:(mc + 1) * 128],
                             reads=["hq_st_d"], writes=["p3hqT", "p3xqT"])
                for dc in range(8):
                    ctx.dma("sync", W3[:, 0:2816],
                            aps["wdqT_d"][dc].rearrange("(mc p) d -> p mc d", p=128),
                            reads=["wdqT_d_d", "p3dw"], writes=["p3w3"])
                    psd = PSD0 if dc % 2 == 0 else PSD1
                    pdb = "PSD%d" % (dc % 2)
                    def mmd(e, psd=psd):
                        ins = None
                        for mc in range(22):
                            ins = e.matmul(psd[:], W3[:, mc * 128:(mc + 1) * 128],
                                           B16B[:, mc * 512:(mc + 1) * 512],
                                           start=(mc == 0), stop=(mc == 21))
                        return ins
                    ctx.op("pe", mmd, reads=["p3w3", "p3hqT"], writes=[pdb])
                    ctx.op("act", lambda e, psd=psd: e.activation(out=XD[:, 0:512], in_=psd[:],
                                                                  func=AF.Silu),
                           reads=[pdb], writes=["p3g0"])
                    ctx.dma("sync", XD[:, 512:1024], aps["s0_f"][dc * 128:(dc + 1) * 128, n0:n0 + 512],
                            reads=["s0_d"], writes=["p3s0l"])
                    ctx.op("dve", lambda e: e.tensor_tensor(out=XD[:, 1024:1536], in0=XD[:, 0:512],
                                                            in1=XD[:, 512:1024], op=ALU.mult),
                           reads=["p3g0", "p3s0l"], writes=["p3sg0"])
                    ctx.dma("sync", XD[:, 0:512], aps["x_fm"][dc * 128:(dc + 1) * 128, n0:n0 + 512],
                            writes=["p3xl", "p3g0"])
                    ctx.op("dve", lambda e, dc=dc: e.tensor_tensor(
                        out=XC[:, dc * 512:(dc + 1) * 512], in0=XD[:, 1024:1536],
                        in1=XD[:, 0:512], op=ALU.add),
                           reads=["p3sg0", "p3xl"], writes=["p3x1"])
                    ctx.dma("act", aps["x1_f"][dc * 128:(dc + 1) * 128, n0:n0 + 512],
                            XC[:, dc * 512:(dc + 1) * 512], reads=["p3x1"], writes=["x1_d"])
                def stats(src_bufname, srcap_fn, mu_off, sq_tag, sq_fn=None):
                    if sq_fn is None:
                        sq_fn = lambda dc: XB[:, dc * 512:(dc + 1) * 512]
                    def mmu(e):
                        ins = None
                        for dc in range(8):
                            ins = e.matmul(PSD0[0:1, :], ONE[:, 0:1], srcap_fn(dc),
                                           start=(dc == 0), stop=(dc == 7))
                        return ins
                    ctx.op("pe", mmu, reads=[src_bufname, "ONE", "p4mu"], writes=["PSD0"])
                    for dc in range(8):
                        ctx.op("act", lambda e, dc=dc: e.activation(
                            out=sq_fn(dc), in_=srcap_fn(dc), func=AF.Square),
                               reads=[src_bufname], writes=[sq_tag])
                    def mmsq(e):
                        ins = None
                        for dc in range(8):
                            ins = e.matmul(PSD1[0:1, :], ONE[:, 0:1],
                                           sq_fn(dc),
                                           start=(dc == 0), stop=(dc == 7))
                        return ins
                    ctx.op("pe", mmsq, reads=[sq_tag, "ONE", "p4mu"], writes=["PSD1"])
                    ctx.op("dve", lambda e: e.tensor_scalar_mul(out=XA[0:1, 0:512],
                                                                in0=PSD0[0:1, :],
                                                                scalar1=1.0 / 1024.0),
                           reads=["PSD0"], writes=["p4mu"])
                    ctx.op("dve", lambda e: e.tensor_scalar_mul(out=XA[0:1, 512:1024],
                                                                in0=PSD1[0:1, :],
                                                                scalar1=1.0 / 1024.0),
                           reads=["PSD1"], writes=["p4mu"])
                    ctx.op("dve", lambda e: e.tensor_tensor(out=XA[0:1, 1024:1536],
                                                            in0=XA[0:1, 0:512],
                                                            in1=XA[0:1, 0:512], op=ALU.mult),
                           reads=["p4mu"], writes=["p4mu"])
                    ctx.op("dve", lambda e: e.tensor_tensor(out=XA[0:1, 512:1024],
                                                            in0=XA[0:1, 512:1024],
                                                            in1=XA[0:1, 1024:1536],
                                                            op=ALU.subtract),
                           reads=["p4mu"], writes=["p4mu"])
                    ctx.op("dve", lambda e: e.tensor_scalar(out=XA[0:1, 512:1024],
                                                            in0=XA[0:1, 512:1024],
                                                            scalar1=1e-5, scalar2=None,
                                                            op0=ALU.add),
                           reads=["p4mu"], writes=["p4mu"])
                    rsqrt(XA[0:1, 1024:1536], XA[0:1, 512:1024], XA[0:1, 1536:2048],
                          XA[0:1, 2048:2560], ["p4mu"], ["p4mu"], nrows=1)
                    ctx.dma("act", aps[mu_off][0:1, n0:n0 + 512], XA[0:1, 0:512],
                            reads=["p4mu"], writes=[mu_off + "_d"])
                    ctx.dma("act", aps[mu_off][1:2, n0:n0 + 512], XA[0:1, 1024:1536],
                            reads=["p4mu"], writes=[mu_off + "_d"])
                    mb = bass.AP(aps[mu_off].tensor, n0, [[0, 128], [1, 512]])
                    rb = bass.AP(aps[mu_off].tensor, N + n0, [[0, 128], [1, 512]])
                    ctx.dma("sync", XD[:, 0:512], mb, reads=[mu_off + "_d"], writes=["p4mb"])
                    ctx.dma("sync", XD[:, 512:1024], rb, reads=[mu_off + "_d"], writes=["p4rb"])
                stats("p3x1", lambda dc: XC[:, dc * 512:(dc + 1) * 512], "mu_row", "p4sq")
                for dc in range(8):
                    ctx.op("dve", lambda e, dc=dc: e.tensor_tensor(
                        out=XB[:, dc * 512:(dc + 1) * 512], in0=XC[:, dc * 512:(dc + 1) * 512],
                        in1=XD[:, 0:512], op=ALU.subtract),
                           reads=["p3x1", "p4mb", "p4sq"], writes=["p4xn2"])
                    ctx.op("dve", lambda e, dc=dc: e.tensor_tensor(
                        out=XB[:, dc * 512:(dc + 1) * 512], in0=XB[:, dc * 512:(dc + 1) * 512],
                        in1=XD[:, 512:1024], op=ALU.mult),
                           reads=["p4xn2", "p4rb"], writes=["p4xn2"])
                    ctx.dma("act", aps["xn2_f"][dc * 128:(dc + 1) * 128, n0:n0 + 512],
                            XB[:, dc * 512:(dc + 1) * 512], reads=["p4xn2"], writes=["xn2f_d"])
                    ctx.op("dve", lambda e, dc=dc: e.tensor_copy(
                        out=B16B[:, dc * 512:(dc + 1) * 512], in_=XB[:, dc * 512:(dc + 1) * 512]),
                           reads=["p4xn2"], writes=["p4xn2b", "p3hqT", "p3xqT"])
                    ctx.dma("act", aps["xn2_bf"][dc * 128:(dc + 1) * 128, n0:n0 + 512],
                            B16B[:, dc * 512:(dc + 1) * 512], reads=["p4xn2b"], writes=["xn2b_d"])
                stats("p4xn2", lambda dc: XB[:, dc * 512:(dc + 1) * 512], "mu2_row", "p4sq2",
                      sq_fn=lambda dc: XB[:, 4096 + dc * 512:4096 + (dc + 1) * 512])
                for dc in range(8):
                    ctx.op("dve", lambda e, dc=dc: e.tensor_tensor(
                        out=XC[:, dc * 512:(dc + 1) * 512], in0=XB[:, dc * 512:(dc + 1) * 512],
                        in1=XD[:, 0:512], op=ALU.subtract),
                           reads=["p4xn2", "p4mb", "p3x1"], writes=["p4xni"])
                    ctx.op("dve", lambda e, dc=dc: e.tensor_tensor(
                        out=XC[:, dc * 512:(dc + 1) * 512], in0=XC[:, dc * 512:(dc + 1) * 512],
                        in1=XD[:, 512:1024], op=ALU.mult),
                           reads=["p4xni", "p4rb"], writes=["p4xni"])
                    ctx.op("dve", lambda e, dc=dc: e.tensor_scalar(
                        out=B16B[:, 4096 + dc * 512:4096 + (dc + 1) * 512],
                        in0=XC[:, dc * 512:(dc + 1) * 512],
                        scalar1=CONST[:, 128 + dc:129 + dc], scalar2=CONST[:, 136 + dc:137 + dc],
                        op0=ALU.mult, op1=ALU.add),
                           reads=["p4xni", "CONST", "p4xn2b"], writes=["p4xnib", "p3hqT"])
                    ctx.dma("act", aps["xni_bf"][dc * 128:(dc + 1) * 128, n0:n0 + 512],
                            B16B[:, 4096 + dc * 512:4096 + (dc + 1) * 512],
                            reads=["p4xnib"], writes=["xni_d"])
        ctx.barrier()

    def _ph_P4ba():
        with (nc.sbuf_tensor(_u("W1"), [128, 32768], BF16) as W1,
              nc.sbuf_tensor(_u("WS"), [128, 2048], BF16) as WS,
              nc.sbuf_tensor(_u("WS2"), [64, 2048], BF16) as WS2,
              nc.sbuf_tensor(_u("XA"), [128, 8192], F32) as XA,
              nc.sbuf_tensor(_u("XE"), [128, 3584], F32) as XE,
              nc.sbuf_tensor(_u("HALO"), [128, 48], F32) as HALO,
              nc.sbuf_tensor(_u("B16A"), [128, 8192], BF16) as B16A,
              nc.sbuf_tensor(_u("B16B"), [128, 8192], BF16) as B16B,
              nc.sbuf_tensor(_u("DTB"), [64, 512], BF16) as DTB,
              nc.sbuf_tensor(_u("DT16"), [128, 512], BF16) as DT16,
              nc.psum_tensor(_u("PSI0"), [128, 512], F32) as PSI0,
              nc.psum_tensor(_u("PSI1"), [128, 512], F32) as PSI1,
              nc.psum_tensor(_u("PSDT"), [64, 512], F32) as PSDT,
              nc.psum_tensor(_u("PSB"), [16, 512], F32) as PSB,
              nc.psum_tensor(_u("PSC"), [16, 512], F32) as PSC,
              nc.psum_tensor(_u("PSD2"), [128, 512], F32) as PSD2):
            for kc in range(8):
                ctx.dma("pool", W1[:, kc * 4096:(kc + 1) * 4096],
                        aps["in_w"][kc * 128:(kc + 1) * 128, :],
                        reads=["p4w1"], writes=["p4w1"])
            for kc in range(16):
                ctx.dma("pool", WS[:, kc * 96:kc * 96 + 64],
                        aps["xw_dt"][kc * 128:(kc + 1) * 128, :],
                        reads=["p4ws"], writes=["p4ws"])
                ctx.dma("pool", WS[:, kc * 96 + 64:kc * 96 + 80],
                        aps["xw_B"][kc * 128:(kc + 1) * 128, :],
                        reads=["p4ws"], writes=["p4ws"])
                ctx.dma("pool", WS[:, kc * 96 + 80:kc * 96 + 96],
                        aps["xw_C"][kc * 128:(kc + 1) * 128, :],
                        reads=["p4ws"], writes=["p4ws"])
            ctx.dma("pool", WS2[:], aps["dt_w"][:], reads=["p4ws2"], writes=["p4ws2"])
            ctx.op("dve", lambda e: e.memset(HALO[:], 0.0), writes=["p4halo"])
            for st in range(N // 512):
                n0 = st * 512
                for c in range(8):
                    ctx.dma("sync", B16A[:, c * 512:(c + 1) * 512],
                            aps["xni_bf"][c * 128:(c + 1) * 128, n0:n0 + 512],
                            reads=["xni_d"], writes=["p4xni"])
                for mc in range(32):
                    ps = PSI0 if mc % 2 == 0 else PSI1
                    pb = "PSI%d" % (mc % 2)
                    def mmi(e, mc=mc, ps=ps):
                        ins = None
                        for k in range(8):
                            ins = e.matmul(ps[:], W1[:, k * 4096 + mc * 128: k * 4096 + (mc + 1) * 128],
                                           B16A[:, k * 512:(k + 1) * 512],
                                           start=(k == 0), stop=(k == 7))
                        return ins
                    ctx.op("pe", mmi, reads=["p4w1", "p4xni"], writes=[pb])
                    if mc < 16:
                        ctx.op("dve", lambda e, mc=mc, ps=ps: e.tensor_copy(
                            out=XA[:, mc * 512:(mc + 1) * 512], in_=ps[:]),
                               reads=[pb], writes=["p4xp", "p4st"])
                    else:
                        ctx.op("act", lambda e, mc=mc, ps=ps: e.activation(
                            out=B16B[:, (mc - 16) * 512:(mc - 15) * 512], in_=ps[:], func=AF.Silu),
                               reads=[pb], writes=["p4sz"])
                for c in range(16):
                    ctx.dma("act", aps["siluz"][c * 128:(c + 1) * 128, n0:n0 + 512],
                            B16B[:, c * 512:(c + 1) * 512], reads=["p4sz"], writes=["siluz_d"])
                for c in range(16):
                    ext = XE[:, 0:515]
                    ctx.op("dve", lambda e, c=c: e.tensor_copy(out=XE[:, 0:3],
                                                               in_=HALO[:, c * 3:(c + 1) * 3]),
                           reads=["p4halo"], writes=["p4ext", "p4B", "p4C"])
                    ctx.op("dve", lambda e, c=c: e.tensor_copy(out=XE[:, 3:515],
                                                               in_=XA[:, c * 512:(c + 1) * 512]),
                           reads=["p4xp"], writes=["p4ext"])
                    ctx.op("dve", lambda e, c=c: e.tensor_copy(out=HALO[:, c * 3:(c + 1) * 3],
                                                               in_=XA[:, c * 512 + 509:c * 512 + 512]),
                           reads=["p4xp", "p4ext"], writes=["p4halo"])
                    ctx.op("dve", lambda e, c=c: e.tensor_scalar(out=XE[:, 515:1027],
                                                                 in0=XE[:, 0:512],
                                                                 scalar1=CONST[:, c * 4:c * 4 + 1],
                                                                 scalar2=None, op0=ALU.mult),
                           reads=["p4ext", "CONST"], writes=["p4cv"])
                    for kk in range(1, 4):
                        ctx.op("dve", lambda e, c=c, kk=kk: e.scalar_tensor_tensor(
                            out=XE[:, 515:1027], in0=XE[:, kk:kk + 512],
                            scalar=CONST[:, c * 4 + kk:c * 4 + kk + 1],
                            in1=XE[:, 515:1027], op0=ALU.mult, op1=ALU.add),
                               reads=["p4ext", "CONST", "p4cv"], writes=["p4cv"])
                    ctx.op("act", lambda e, c=c: e.activation(out=XE[:, 1027:1539],
                                                              in_=XE[:, 515:1027], func=AF.Silu,
                                                              bias=CONST[:, 64 + c:65 + c]),
                           reads=["p4cv", "CONST"], writes=["p4xc"])
                    ctx.dma("act", aps["xc_f"][c * 128:(c + 1) * 128, n0:n0 + 512],
                            XE[:, 1027:1539], reads=["p4xc"], writes=["xc_d"])
                    ctx.op("dve", lambda e, c=c: e.tensor_copy(
                        out=B16A[:, c * 512:(c + 1) * 512], in_=XE[:, 1027:1539]),
                           reads=["p4xc"], writes=["p4xcb", "p4xni"])
                def mmx(e):
                    ins = None
                    for k in range(16):
                        ins = e.matmul(PSDT[:], WS[:, k * 96:k * 96 + 64],
                                       B16A[:, k * 512:(k + 1) * 512],
                                       start=(k == 0), stop=(k == 15))
                    for k in range(16):
                        ins = e.matmul(PSB[:], WS[:, k * 96 + 64:k * 96 + 80],
                                       B16A[:, k * 512:(k + 1) * 512],
                                       start=(k == 0), stop=(k == 15))
                    for k in range(16):
                        ins = e.matmul(PSC[:], WS[:, k * 96 + 80:k * 96 + 96],
                                       B16A[:, k * 512:(k + 1) * 512],
                                       start=(k == 0), stop=(k == 15))
                    return ins
                ctx.op("pe", mmx, reads=["p4ws", "p4xcb", "p4xni"], writes=["PSX"])
                ctx.op("dve", lambda e: e.tensor_copy(out=XE[0:16, 0:512], in_=PSB[:]),
                       reads=["PSX"], writes=["p4B"])
                ctx.dma("act", aps["B_d"][:, n0:n0 + 512], XE[0:16, 0:512],
                        reads=["p4B"], writes=["B_dd"])
                ctx.op("dve", lambda e: e.tensor_copy(out=XE[0:16, 512:1024], in_=PSC[:]),
                       reads=["PSX", "p4B"], writes=["p4C"])
                ctx.dma("act", aps["C_d"][:, n0:n0 + 512], XE[0:16, 512:1024],
                        reads=["p4C"], writes=["C_dd"])
                ctx.op("dve", lambda e: e.tensor_copy(out=DTB[:], in_=PSDT[:]),
                       reads=["PSX"], writes=["p4dtb"])
                for c in range(16):
                    def mmdt(e, c=c):
                        return e.matmul(PSD2[:], WS2[:, c * 128:(c + 1) * 128], DTB[:],
                                        start=True, stop=True)
                    ctx.op("pe", mmdt, reads=["p4ws2", "p4dtb"], writes=["PSDc"])
                    ctx.op("dve", lambda e, c=c: e.tensor_scalar(
                        out=XE[:, 1664:2176], in0=PSD2[:], scalar1=CONST[:, 80 + c:81 + c],
                        scalar2=85.0, op0=ALU.add, op1=ALU.min),
                           reads=["PSDc", "CONST"], writes=["p4dta", "p4dt"])
                    ctx.op("act", lambda e: e.activation(out=XE[:, 2176:2688], in_=XE[:, 1664:2176],
                                                         func=AF.Exp),
                           reads=["p4dta"], writes=["p4dte"])
                    ctx.op("act", lambda e: e.activation(out=XE[:, 1664:2176], in_=XE[:, 2176:2688],
                                                         func=AF.Ln, bias=1.0),
                           reads=["p4dte", "p4dta"], writes=["p4dt"])
                    ctx.dma("act", aps["dt_f"][c * 128:(c + 1) * 128, n0:n0 + 512],
                            XE[:, 1664:2176], reads=["p4dt"], writes=["dt_dd"])
                    ctx.op("dve", lambda e, c=c: e.tensor_tensor(
                        out=DT16[:, 0:512], in0=XE[:, 1664:2176],
                        in1=B16A[:, c * 512:(c + 1) * 512], op=ALU.mult),
                           reads=["p4dt", "p4xcb", "p4dtu", "p4xni"], writes=["p4dtu"])
                    ctx.dma("act", aps["dtu_f"][c * 128:(c + 1) * 128, n0:n0 + 512],
                            DT16[:, 0:512], reads=["p4dtu"], writes=["dtu_dd"])
        ctx.barrier()

    def _ph_P4ca():
        with (nc.sbuf_tensor(_u("WSR"), [128, 64], BF16) as WSR,
              nc.sbuf_tensor(_u("XA"), [128, 1024], F32) as XA,
              nc.sbuf_tensor(_u("B16A"), [128, 4096], BF16) as B16A,
              nc.psum_tensor(_u("PSR"), [8, 512], F32) as PSR,
              nc.psum_tensor(_u("PSZ2"), [128, 8], F32) as PSZ2,
              nc.psum_tensor(_u("PSZ3"), [8, 128], F32) as PSZ3):
            for kc in range(8):
                ctx.dma("pool", WSR[:, kc * 8:(kc + 1) * 8],
                        aps["Wr"][kc * 128:(kc + 1) * 128, :],
                        reads=["p4wr"], writes=["p4wr"])
            for st in range(N // 512):
                n0 = st * 512
                for c in range(8):
                    ctx.dma("sync", B16A[:, c * 512:(c + 1) * 512],
                            aps["xn2_bf"][c * 128:(c + 1) * 128, n0:n0 + 512],
                            reads=["xn2b_d"], writes=["p4cx"])
                def mmr(e):
                    ins = None
                    for k in range(8):
                        ins = e.matmul(PSR[:], WSR[:, k * 8:(k + 1) * 8],
                                       B16A[:, k * 512:(k + 1) * 512],
                                       start=(k == 0), stop=(k == 7))
                    return ins
                ctx.op("pe", mmr, reads=["p4wr", "p4cx"], writes=["PSR"])
                ctx.op("dve", lambda e: e.tensor_copy(out=PP[0:8, 0:512], in_=PSR[:]),
                       reads=["PSR"], writes=["p4lg"])
                for s in range(4):
                    ctx.op("pe", lambda e, s=s: e.transpose(PSZ2[:], PP[0:8, s * 128:(s + 1) * 128],
                                                            IDT[0:8, 0:8]),
                           reads=["p4lg", "IDT", "p4sm"], writes=["PSZ2b"])
                    p = XA[:, 0:8]
                    ctx.op("dve", lambda e: e.tensor_copy(out=p, in_=PSZ2[:]),
                           reads=["PSZ2b"], writes=["p4sm"])
                    ctx.op("dve", lambda e: e.reduce_max(out=SMALL[:, 0:1], in_=p, axis=AX.X),
                           reads=["p4sm"], writes=["SMALL"])
                    ctx.op("dve", lambda e: e.tensor_scalar(out=p, in0=p, scalar1=SMALL[:, 0:1],
                                                            scalar2=None, op0=ALU.subtract),
                           reads=["p4sm", "SMALL"], writes=["p4sm"])
                    ctx.op("act", lambda e: e.activation(out=p, in_=p, func=AF.Exp),
                           reads=["p4sm"], writes=["p4sm"])
                    ctx.op("dve", lambda e: e.reduce_sum(out=SMALL[:, 0:1], in_=p, axis=AX.X),
                           reads=["p4sm"], writes=["SMALL"])
                    ctx.op("dve", lambda e: e.reciprocal(out=SMALL[:, 1:2], in_=SMALL[:, 0:1]),
                           reads=["SMALL"], writes=["SMALL"])
                    ctx.op("dve", lambda e: e.tensor_scalar(out=p, in0=p, scalar1=SMALL[:, 1:2],
                                                            scalar2=None, op0=ALU.mult),
                           reads=["p4sm", "SMALL"], writes=["p4sm"])
                    ctx.op("dve", lambda e: e.reduce_max(out=SMALL[:, 2:3], in_=p, axis=AX.X),
                           reads=["p4sm"], writes=["SMALL"])
                    eq1 = XA[:, 8:16]
                    ctx.op("dve", lambda e, eq1=eq1: e.tensor_scalar(out=eq1, in0=p,
                                                                     scalar1=SMALL[:, 2:3],
                                                                     scalar2=None, op0=ALU.is_ge),
                           reads=["p4sm", "SMALL"], writes=["p4eq1"])
                    pm = XA[:, 16:24]
                    ctx.op("dve", lambda e, pm=pm, eq1=eq1: e.tensor_tensor(out=pm, in0=p, in1=eq1,
                                                                            op=ALU.mult),
                           reads=["p4sm", "p4eq1"], writes=["p4pm"])
                    ctx.op("dve", lambda e, pm=pm: e.tensor_tensor(out=pm, in0=p, in1=pm,
                                                                   op=ALU.subtract),
                           reads=["p4sm", "p4pm"], writes=["p4pm"])
                    ctx.op("dve", lambda e, pm=pm: e.reduce_max(out=SMALL[:, 3:4], in_=pm,
                                                                axis=AX.X),
                           reads=["p4pm"], writes=["SMALL"])
                    ctx.op("dve", lambda e: e.tensor_tensor(out=SMALL[:, 4:5], in0=SMALL[:, 2:3],
                                                            in1=SMALL[:, 3:4], op=ALU.add),
                           reads=["SMALL"], writes=["SMALL"])
                    ctx.op("dve", lambda e: e.tensor_scalar(out=SMALL[:, 4:5], in0=SMALL[:, 4:5],
                                                            scalar1=1e-8, scalar2=None,
                                                            op0=ALU.add),
                           reads=["SMALL"], writes=["SMALL"])
                    ctx.op("dve", lambda e: e.reciprocal(out=SMALL[:, 5:6], in_=SMALL[:, 4:5]),
                           reads=["SMALL"], writes=["SMALL"])
                    ctx.op("dve", lambda e: e.tensor_tensor(out=SMALL[:, 6:7], in0=SMALL[:, 2:3],
                                                            in1=SMALL[:, 5:6], op=ALU.mult),
                           reads=["SMALL"], writes=["SMALL"])
                    ctx.op("dve", lambda e: e.tensor_tensor(out=SMALL[:, 7:8], in0=SMALL[:, 3:4],
                                                            in1=SMALL[:, 5:6], op=ALU.mult),
                           reads=["SMALL"], writes=["SMALL"])
                    eq2 = XA[:, 24:32]
                    ctx.op("dve", lambda e, eq2=eq2, pm=pm: e.tensor_scalar(
                        out=eq2, in0=pm, scalar1=SMALL[:, 3:4], scalar2=None, op0=ALU.is_ge),
                           reads=["p4pm", "SMALL"], writes=["p4eq2"])
                    ew = XA[:, 32:40]
                    ctx.op("dve", lambda e, ew=ew, eq1=eq1: e.tensor_scalar(
                        out=ew, in0=eq1, scalar1=SMALL[:, 6:7], scalar2=None, op0=ALU.mult),
                           reads=["p4eq1", "SMALL"], writes=["p4ew"])
                    ctx.op("dve", lambda e, ew=ew, eq2=eq2: e.scalar_tensor_tensor(
                        out=ew, in0=eq2, scalar=SMALL[:, 7:8], in1=ew, op0=ALU.mult, op1=ALU.add),
                           reads=["p4eq2", "SMALL", "p4ew"], writes=["p4ew"])
                    ctx.op("pe", lambda e, ew=ew: e.transpose(PSZ3[:], ew, IDT[:]),
                           reads=["p4ew", "IDT", "p4ewr"], writes=["PSZ3b"])
                    ctx.op("dve", lambda e, s=s: e.tensor_copy(out=PP[64:72, s * 128:(s + 1) * 128],
                                                               in_=PSZ3[:]),
                           reads=["PSZ3b"], writes=["p4ewr"])
                ctx.dma("act", aps["ew_rows"][:, n0:n0 + 512], PP[64:72, 0:512],
                        reads=["p4ewr"], writes=["ew_d"])
        ctx.barrier()

    def _ph_P67():
        GU = NE * D * ME
        NTC = N // TC
        with contextlib.ExitStack() as _st:
            _a = lambda nm, shape, dt: _st.enter_context(nc.sbuf_tensor(_u(nm), shape, dt))
            _p = lambda nm, shape: _st.enter_context(nc.psum_tensor(_u(nm), shape, F32))
            XA = _a("XA", [128, 4096], F32)
            XB = _a("XB", [128, 4096], BF16)
            XCB = _a("XCB", [128, 12288], BF16)
            XD = _a("XD", [128, 1536], F32)
            DTB = _a("DTB", [128, 512], BF16)
            SCB = _a("SCB", [128, 1024], BF16)
            ASB = _a("ASB", [128, 256], F32)
            HC = _a("HC", [128, 256], F32)
            W1 = _a("W1", [128, 16384], BF16)
            W2 = _a("W2", [128, 16384], BF16)
            H16 = _a("H16", [128, 8192], BF16)
            X16 = _a("X16", [128, 8192], BF16)
            XF = _a("XF", [128, 4608], F32)
            EWBP = [_a("EWB%d" % i, [128, 512], F32) for i in range(2)]
            EWB16 = _a("EWB16", [8, 512], BF16)
            BOE = _a("BOE", [8, 1024], BF16)
            PSGL = [_p("PSG%d" % i, [128, 512]) for i in range(3)]
            PSUL = [_p("PSU%d" % i, [128, 512]) for i in range(3)]
            PSOL = [_p("PSO%d" % i, [128, 512]) for i in range(2)]
            WSLOT = [(W1, 0), (W1, 8192), (W2, 0), (W2, 8192)]
            for dc in range(16):
                ctx.dma("sync", SMALL[:, 0:16], aps["A_log"][dc * 128:(dc + 1) * 128, :],
                        reads=["p6al"], writes=["p6al"])
                ctx.op("act", lambda e, dc=dc: e.activation(out=ASB[:, dc * 16:(dc + 1) * 16],
                                                            in_=SMALL[:, 0:16], func=AF.Exp),
                       reads=["p6al"], writes=["ASB"])
            ctx.op("dve", lambda e: e.tensor_scalar_mul(out=ASB[:], in0=ASB[:], scalar1=-1.0),
                   reads=["ASB"], writes=["ASB"])
            ctx.op("dve", lambda e: e.memset(HC[:], 0.0), writes=["HC"])
            ctx.dma("pool", BOE[:], aps["bo_e"][:], writes=["p7boe"])

            def su_prolog(tc):
                t0 = tc * TC
                bb = bass.AP(aps["B_d"].tensor, t0, [[0, 128], [N, DS], [1, TC]])
                cb = bass.AP(aps["C_d"].tensor, t0, [[0, 128], [N, DS], [1, TC]])
                ctx.dma("pool", XCB[:, 0:DS * TC].rearrange("p (n t) -> p n t", t=TC), bb,
                        reads=["B_dd"], writes=["p6Bb"])
                ctx.dma("pool", XCB[:, DS * TC:2 * DS * TC].rearrange("p (n t) -> p n t", t=TC),
                        cb, reads=["C_dd"], writes=["p6Cb"])

            def su_dc(tc, dc):
                t0 = tc * TC
                r0 = dc * 128
                par = dc % 2
                xo = par * 3 * TC
                so = par * 2 * TC
                dtt = "p6dt%d" % par
                dut = "p6dtu%d" % par
                ut = "p6u%d" % par
                szt = "p6sz%d" % par
                units = []

                def u_load():
                    ctx.dma("sync", XD[:, xo:xo + TC], aps["dt_f"][r0:r0 + 128, t0:t0 + TC],
                            reads=["dt_dd"], writes=[dtt])
                    ctx.dma("sync", DTB[:, par * TC:(par + 1) * TC],
                            aps["dtu_f"][r0:r0 + 128, t0:t0 + TC],
                            reads=["dtu_dd"], writes=[dut])
                    ctx.dma("sync", XD[:, xo + TC:xo + 2 * TC],
                            aps["xc_f"][r0:r0 + 128, t0:t0 + TC],
                            reads=["xc_d"], writes=[ut])
                    ctx.dma("sync", SCB[:, so:so + TC], aps["siluz"][r0:r0 + 128, t0:t0 + TC],
                            reads=["siluz_d"], writes=[szt])
                units.append(u_load)

                def mk_exp(g):
                    extra = {0: ["p6t0"], 1: ["p6t0"], 2: ["p6t1"], 3: ["p6t2", "p6t3"]}[g]
                    def u_exp():
                        def fn(e, g=g, dc=dc):
                            ins = None
                            for n_ in range(4 * g, 4 * g + 4):
                                ins = e.activation(
                                    out=XA[:, n_ * TC:(n_ + 1) * TC], in_=XD[:, xo:xo + TC],
                                    func=AF.Exp,
                                    scale=ASB[:, dc * 16 + n_:dc * 16 + n_ + 1])
                            return ins
                        ctx.op("act", fn, reads=[dtt, "ASB"],
                               writes=["p6dA%d" % g] + extra)
                    return u_exp
                units.append(mk_exp(0))
                units.append(mk_exp(1))

                def u_dBu():
                    dtu_b = DTB[:, par * TC:(par + 1) * TC].unsqueeze(1).broadcast_to(
                        (128, DS, TC))
                    ctx.op("pool", lambda e, dtu_b=dtu_b: e.tensor_tensor(
                        out=XCB[:, 2 * DS * TC:3 * DS * TC].rearrange("p (n t) -> p n t", t=TC),
                        in0=dtu_b, in1=XCB[:, 0:DS * TC].rearrange("p (n t) -> p n t", t=TC),
                        op=ALU.mult),
                           reads=[dut, "p6Bb"], writes=["p6dBu"])
                units.append(u_dBu)
                units.append(mk_exp(2))
                units.append(mk_exp(3))

                def mk_scan(g):
                    def u_scan():
                        def fn(e, g=g, dc=dc):
                            ins = None
                            for n_ in range(4 * g, 4 * g + 4):
                                ins = e.tensor_tensor_scan(
                                    out=XB[:, n_ * TC:(n_ + 1) * TC],
                                    data0=XA[:, n_ * TC:(n_ + 1) * TC],
                                    data1=XCB[:, 2 * DS * TC + n_ * TC:
                                              2 * DS * TC + (n_ + 1) * TC],
                                    initial=HC[:, dc * 16 + n_:dc * 16 + n_ + 1],
                                    op0=ALU.mult, op1=ALU.add)
                            return ins
                        ctx.op("dve", fn, reads=["p6dA%d" % g, "p6dBu", "HC"],
                               writes=["p6hs%d" % g])
                    return u_scan
                for g in range(4):
                    units.append(mk_scan(g))

                def u_hc():
                    hs3 = XB[:, 0:DS * TC].rearrange("p (n t) -> p n t", t=TC)
                    ctx.op("act", lambda e, hs3=hs3, dc=dc: e.copy(
                        out=HC[:, dc * 16:(dc + 1) * 16], in_=hs3[:, :, TC - 1]),
                           reads=["p6hs0", "p6hs1", "p6hs2", "p6hs3"], writes=["HC"])
                    ctx.op("pool", lambda e: e.tensor_tensor(
                        out=XCB[:, 2 * DS * TC:3 * DS * TC],
                        in0=XB[:, 0:DS * TC],
                        in1=XCB[:, DS * TC:2 * DS * TC], op=ALU.mult),
                           reads=["p6hs0", "p6hs1", "p6hs2", "p6hs3", "p6Cb"],
                           writes=["p6hc", "p6dBu"])
                units.append(u_hc)

                def u_l01():
                    ctx.op("pool", lambda e: e.tensor_tensor(
                        out=XA[:, 0:8 * TC],
                        in0=XCB[:, 2 * DS * TC:2 * DS * TC + 8 * TC],
                        in1=XCB[:, 2 * DS * TC + 8 * TC:2 * DS * TC + 16 * TC],
                        op=ALU.add),
                           reads=["p6hc"], writes=["p6t0", "p6dA0", "p6dA1"])
                    ctx.op("pool", lambda e: e.tensor_tensor(
                        out=XA[:, 8 * TC:12 * TC],
                        in0=XA[:, 0:4 * TC], in1=XA[:, 4 * TC:8 * TC], op=ALU.add),
                           reads=["p6t0"], writes=["p6t1", "p6dA2"])
                units.append(u_l01)

                def u_l23y():
                    ctx.op("dve", lambda e: e.tensor_tensor(
                        out=XA[:, 12 * TC:14 * TC],
                        in0=XA[:, 8 * TC:10 * TC], in1=XA[:, 10 * TC:12 * TC], op=ALU.add),
                           reads=["p6t1"], writes=["p6t2", "p6dA3"])
                    ctx.op("dve", lambda e: e.tensor_tensor(
                        out=XA[:, 14 * TC:15 * TC],
                        in0=XA[:, 12 * TC:13 * TC], in1=XA[:, 13 * TC:14 * TC], op=ALU.add),
                           reads=["p6t2"], writes=["p6t3"])
                    ctx.op("dve", lambda e, dc=dc: e.scalar_tensor_tensor(
                        out=XD[:, xo + 2 * TC:xo + 3 * TC], in0=XD[:, xo + TC:xo + 2 * TC],
                        scalar=CONST[:, 96 + dc:97 + dc], in1=XA[:, 14 * TC:15 * TC],
                        op0=ALU.mult, op1=ALU.add),
                           reads=[ut, "CONST", "p6t3"], writes=["p6y%d" % par])
                units.append(u_l23y)

                def u_y3():
                    ctx.op("pool", lambda e: e.tensor_tensor(
                        out=SCB[:, so + TC:so + 2 * TC],
                        in0=XD[:, xo + 2 * TC:xo + 3 * TC],
                        in1=SCB[:, so:so + TC], op=ALU.mult),
                           reads=["p6y%d" % par, szt], writes=["p6y3%d" % par])
                    ctx.dma("pool", aps["y3"][r0:r0 + 128, t0:t0 + TC],
                            SCB[:, so + TC:so + 2 * TC],
                            reads=["p6y3%d" % par], writes=["y3_d"])
                units.append(u_y3)
                return units

            def mu_quantum(nt, e_):
                n0 = nt * 512
                qp = (nt * NE + e_) % 2
                xp = nt % 2
                xoff = xp * 4096
                xt = "p7x%d" % xp
                par = e_ % 2
                ewo = par * 512
                boff = 32 + par * 32
                ewt = "p7ewb%d" % par
                bgt = "p7bg%d" % par
                s_a = 2 * qp
                s_b = 2 * qp + 1
                s_c = 2 - 2 * qp
                s_d = 3 - 2 * qp
                stag = lambda s: "WS%d" % s

                def wview(s):
                    t, off = WSLOT[s]
                    return t[:, off:off + 8192].rearrange("p (k m) -> p k m", m=1024)

                def wsl(s, k, m0, m1):
                    t, off = WSLOT[s]
                    return t[:, off + k * 1024 + m0:off + k * 1024 + m1]
                units = []

                def u_a():
                    if e_ == 0:
                        for c in range(8):
                            ctx.dma("sync", X16[:, xoff + c * 512:xoff + (c + 1) * 512],
                                    aps["xn2_bf"][c * 128:(c + 1) * 128, n0:n0 + 512],
                                    reads=["xn2b_d"], writes=[xt])
                        ctx.dma("pool", EWB16[:], aps["ew_rows"][:, n0:n0 + 512],
                                reads=["p7ew16"], writes=["p7ew16"])
                    ewb_src = bass.AP(aps["ew_rows"].tensor, e_ * N + n0, [[0, 128], [1, 512]])
                    ctx.dma("sync", EWBP[par][:], ewb_src,
                            reads=["ew_d"], writes=[ewt])
                    ctx.dma("sync", SMALL[:, boff:boff + 16], aps["bg_e"][e_],
                            reads=[bgt], writes=[bgt])
                    ctx.dma("sync", SMALL[:, boff + 16:boff + 32], aps["bu_e"][e_],
                            reads=[bgt], writes=[bgt])
                units.append(u_a)

                def mk_w(half):
                    sg = s_a if half == 0 else s_c
                    su_ = s_b if half == 0 else s_d

                    def u_w():
                        gsrc = bass.AP(aps["wbf"].tensor, e_ * D * ME + half * 1024,
                                       [[ME, 128], [128 * ME, 8], [1, 1024]])
                        ctx.dma("sync", wview(sg), gsrc, reads=["wbf_d"], writes=[stag(sg)])
                        usrc = bass.AP(aps["wbf"].tensor, GU + e_ * D * ME + half * 1024,
                                       [[ME, 128], [128 * ME, 8], [1, 1024]])
                        ctx.dma("act", wview(su_), usrc, reads=["wbf_d"], writes=[stag(su_)])
                    return u_w
                units.append(mk_w(0))
                units.append(mk_w(1))

                def mk_gu(half, mc8):
                    sg = s_a if half == 0 else s_c
                    su_ = s_b if half == 0 else s_d
                    mcg = half * 8 + mc8
                    psg = PSGL[mcg % 3]
                    psu = PSUL[mcg % 3]
                    gb = "PSG%d" % (mcg % 3)
                    ub = "PSU%d" % (mcg % 3)
                    ht = "p7h%d" % half

                    def u_gu():
                        def mmg(e, mc8=mc8, psg=psg, sg=sg):
                            ins = None
                            for k in range(8):
                                ins = e.matmul(psg[:], wsl(sg, k, mc8 * 128, (mc8 + 1) * 128),
                                               X16[:, xoff + k * 512:xoff + (k + 1) * 512],
                                               start=(k == 0), stop=(k == 7))
                            return ins
                        ctx.op("pe", mmg, reads=[stag(sg), xt], writes=[gb])
                        ctx.op("act", lambda e, mcg=mcg, psg=psg: e.activation(
                            out=H16[:, mcg * 512:mcg * 512 + 512], in_=psg[:], func=AF.Silu,
                            bias=SMALL[:, boff + mcg:boff + mcg + 1]),
                               reads=[gb, bgt], writes=[ht])
                        def mmu(e, mc8=mc8, psu=psu, su_=su_):
                            ins = None
                            for k in range(8):
                                ins = e.matmul(psu[:], wsl(su_, k, mc8 * 128, (mc8 + 1) * 128),
                                               X16[:, xoff + k * 512:xoff + (k + 1) * 512],
                                               start=(k == 0), stop=(k == 7))
                            return ins
                        ctx.op("pe", mmu, reads=[stag(su_), xt], writes=[ub])
                        ctx.op("dve", lambda e, mcg=mcg, psu=psu: e.scalar_tensor_tensor(
                            out=XF[:, 4096:4608], in0=psu[:],
                            scalar=SMALL[:, boff + 16 + mcg:boff + 17 + mcg],
                            in1=H16[:, mcg * 512:mcg * 512 + 512],
                            op0=ALU.add, op1=ALU.mult),
                               reads=[ub, bgt, ht], writes=["p7tmp"])
                        ctx.op("dve", lambda e, mcg=mcg: e.tensor_tensor(
                            out=H16[:, mcg * 512:mcg * 512 + 512], in0=XF[:, 4096:4608],
                            in1=EWBP[par][:], op=ALU.mult),
                               reads=["p7tmp", ewt], writes=[ht])
                    return u_gu
                for mc8 in range(8):
                    units.append(mk_gu(0, mc8))

                def u_wo():
                    osrc0 = bass.AP(aps["wbf"].tensor, 2 * GU + e_ * ME * D,
                                    [[D, 128], [128 * D, 8], [1, 1024]])
                    ctx.dma("sync", wview(s_a), osrc0, reads=["wbf_d"], writes=[stag(s_a)])
                    osrc1 = bass.AP(aps["wbf"].tensor, 2 * GU + e_ * ME * D + 8 * 128 * D,
                                    [[D, 128], [128 * D, 8], [1, 1024]])
                    ctx.dma("act", wview(s_b), osrc1, reads=["wbf_d"], writes=[stag(s_b)])
                units.append(u_wo)
                for mc8 in range(8):
                    units.append(mk_gu(1, mc8))

                def mk_out(pp, pas):
                    dcs = [2 * pp, 2 * pp + 1]
                    so_ = s_a if pas == 0 else s_b

                    def u_out():
                        def mmo(e, pas=pas, dcs=dcs, so_=so_, e_=e_):
                            ins = None
                            for dc in dcs:
                                pso = PSOL[dc % 2]
                                for mc8 in range(8):
                                    mc = pas * 8 + mc8
                                    ins = e.matmul(
                                        pso[:], wsl(so_, mc8, dc * 128, (dc + 1) * 128),
                                        H16[:, mc * 512:(mc + 1) * 512],
                                        start=(pas == 0 and mc8 == 0),
                                        stop=(pas == 1 and mc8 == 7 and e_ != 0))
                                if pas == 1 and e_ == 0:
                                    ins = e.matmul(pso[:], BOE[0:8, dc * 128:(dc + 1) * 128],
                                                   EWB16[0:8, :], start=False, stop=True)
                            return ins
                        ctx.op("pe", mmo, reads=[stag(so_), "p7h%d" % pas, "p7boe", "p7ew16"],
                               writes=["PSO%d" % (d % 2) for d in dcs])
                        if pas == 1:
                            for dc in dcs:
                                pso = PSOL[dc % 2]
                                pob = "PSO%d" % (dc % 2)
                                if e_ == 0:
                                    ctx.op("dve", lambda e, dc=dc, pso=pso: e.tensor_copy(
                                        out=XF[:, dc * 512:(dc + 1) * 512], in_=pso[:]),
                                           reads=[pob], writes=["p7acc"])
                                else:
                                    ctx.op("dve", lambda e, dc=dc, pso=pso: e.tensor_tensor(
                                        out=XF[:, dc * 512:(dc + 1) * 512],
                                        in0=XF[:, dc * 512:(dc + 1) * 512],
                                        in1=pso[:], op=ALU.add),
                                           reads=[pob, "p7acc"], writes=["p7acc"])
                    return u_out
                for pp in range(4):
                    units.append(mk_out(pp, 0))
                    units.append(mk_out(pp, 1))

                if e_ == NE - 1:
                    def u_store():
                        for dc in range(8):
                            ctx.dma("pool", aps["moe_f"][dc * 128:(dc + 1) * 128, n0:n0 + 512],
                                    XF[:, dc * 512:(dc + 1) * 512],
                                    reads=["p7acc"], writes=["moe_d"])
                    units.append(u_store)
                return units
            su = []
            for tc in range(NTC):
                su.append(lambda tc=tc: su_prolog(tc))
                for dc in range(DI // 128):
                    su.extend(su_dc(tc, dc))
            mu = []
            for nt in range(N // 512):
                for e_ in range(NE):
                    mu.extend(mu_quantum(nt, e_))
            ratio = float(len(mu)) / float(len(su))
            credit = 0.0
            mi = 0
            for s in su:
                s()
                credit += ratio
                while credit >= 1.0 and mi < len(mu):
                    mu[mi]()
                    mi += 1
                    credit -= 1.0
            while mi < len(mu):
                mu[mi]()
                mi += 1
        ctx.barrier()
    def _ph_P65a():
        with (nc.sbuf_tensor(_u("W1"), [128, 16384], BF16) as W1,
              nc.sbuf_tensor(_u("W2"), [128, 16384], BF16) as W2,
              nc.sbuf_tensor(_u("B16A"), [128, 8192], BF16) as B16A,
              nc.sbuf_tensor(_u("B16M"), [128, 4096], BF16) as B16M,
              nc.sbuf_tensor(_u("B16S"), [128, 4096], BF16) as B16S,
              nc.sbuf_tensor(_u("XA"), [128, 3072], F32) as XA,
              nc.psum_tensor(_u("PSA0"), [128, 512], F32) as PSA0,
              nc.psum_tensor(_u("PSA1"), [128, 512], F32) as PSA1):
            for kc in range(16):
                cast_load(W1[:, kc * 1024:(kc + 1) * 1024],
                          aps["out_w"][kc * 128:(kc + 1) * 128, :], PP, "p65w1", 1024)
            for kc in range(8):
                cast_load(W2[:, kc * 2048:(kc + 1) * 2048],
                          aps["glu_w"][kc * 128:(kc + 1) * 128, :], PP, "p65w2", 2048)
            for nt in range(N // 512):
                n0 = nt * 512
                for c in range(16):
                    ctx.dma("sync", B16A[:, c * 512:(c + 1) * 512],
                            aps["y3"][c * 128:(c + 1) * 128, n0:n0 + 512],
                            reads=["y3_d"], writes=["p65y"])
                for mc in range(8):
                    ps = PSA0 if mc % 2 == 0 else PSA1
                    pb = "PSA%d" % (mc % 2)
                    def mmop(e, mc=mc, ps=ps):
                        ins = None
                        for k in range(16):
                            ins = e.matmul(ps[:], W1[:, k * 1024 + mc * 128: k * 1024 + (mc + 1) * 128],
                                           B16A[:, k * 512:(k + 1) * 512],
                                           start=(k == 0), stop=(k == 15))
                        return ins
                    ctx.op("pe", mmop, reads=["p65w1", "p65y"], writes=[pb])
                    ctx.op("act", lambda e, mc=mc, ps=ps: e.activation(
                        out=B16M[:, mc * 512:(mc + 1) * 512], in_=ps[:], func=AF.Gelu),
                           reads=[pb], writes=["p65m"])
                for gc in range(8, 16):
                    ps = PSA0 if gc % 2 == 0 else PSA1
                    pb = "PSA%d" % (gc % 2)
                    def mmgl(e, gc=gc, ps=ps):
                        ins = None
                        for k in range(8):
                            ins = e.matmul(ps[:], W2[:, k * 2048 + gc * 128: k * 2048 + (gc + 1) * 128],
                                           B16M[:, k * 512:(k + 1) * 512],
                                           start=(k == 0), stop=(k == 7))
                        return ins
                    ctx.op("pe", mmgl, reads=["p65w2", "p65m"], writes=[pb])
                    ctx.op("act", lambda e, gc=gc, ps=ps: e.activation(
                        out=B16S[:, (gc - 8) * 512:(gc - 7) * 512], in_=ps[:], func=AF.Sigmoid,
                        bias=CONST[:, 112 + gc:113 + gc]),
                           reads=[pb, "CONST"], writes=["p65sig"])
                for gc in range(8):
                    ps = PSA0 if gc % 2 == 0 else PSA1
                    pb = "PSA%d" % (gc % 2)
                    def mmgl2(e, gc=gc, ps=ps):
                        ins = None
                        for k in range(8):
                            ins = e.matmul(ps[:], W2[:, k * 2048 + gc * 128: k * 2048 + (gc + 1) * 128],
                                           B16M[:, k * 512:(k + 1) * 512],
                                           start=(k == 0), stop=(k == 7))
                        return ins
                    ctx.op("pe", mmgl2, reads=["p65w2", "p65m"], writes=[pb])
                    ctx.op("dve", lambda e, gc=gc, ps=ps: e.scalar_tensor_tensor(
                        out=XA[:, 0:512], in0=ps[:], scalar=CONST[:, 112 + gc:113 + gc],
                        in1=B16S[:, gc * 512:(gc + 1) * 512], op0=ALU.add, op1=ALU.mult),
                           reads=[pb, "CONST", "p65sig"], writes=["p65m2"])
                    ctx.dma("sync", XA[:, 512:1024],
                            aps["xn2_f"][gc * 128:(gc + 1) * 128, n0:n0 + 512],
                            reads=["xn2f_d", "p65sk"], writes=["p65sk"])
                    ctx.op("dve", lambda e: e.tensor_tensor(out=XA[:, 1024:1536],
                                                            in0=XA[:, 0:512], in1=XA[:, 512:1024],
                                                            op=ALU.add),
                           reads=["p65m2", "p65sk"], writes=["p65s1"])
                    ctx.dma("sync", XA[:, 1536:2048],
                            aps["moe_f"][gc * 128:(gc + 1) * 128, n0:n0 + 512],
                            reads=["moe_d", "p65mo"], writes=["p65mo", "p65o"])
                    ctx.op("act", lambda e: e.activation(out=XA[:, 2048:2560],
                                                         in_=XA[:, 1536:2048], func=AF.Silu),
                           reads=["p65mo"], writes=["p65sg"])
                    ctx.op("dve", lambda e: e.tensor_tensor(out=XA[:, 0:512],
                                                            in0=XA[:, 2048:2560],
                                                            in1=XA[:, 1024:1536], op=ALU.mult),
                           reads=["p65sg", "p65s1", "p65m2"], writes=["p65sg", "p65m2"])
                    ctx.dma("sync", XA[:, 2560:3072],
                            aps["x1_f"][gc * 128:(gc + 1) * 128, n0:n0 + 512],
                            reads=["x1_d", "p65x1"], writes=["p65x1"])
                    ctx.op("dve", lambda e: e.tensor_tensor(out=XA[:, 1536:2048],
                                                            in0=XA[:, 0:512],
                                                            in1=XA[:, 2560:3072], op=ALU.add),
                           reads=["p65sg", "p65x1", "p65m2"], writes=["p65o", "p65mo"])
                    ctx.dma("pool", aps["out"][gc * 128:(gc + 1) * 128, n0:n0 + 512],
                            XA[:, 1536:2048], reads=["p65o"], writes=["out_d"])
        ctx.barrier()


    if "P1" in phases:
        _ph_prologue()
    if "P2" in phases:
        _ph_P2a()
    if "P3a" in phases:
        _ph_P3aa()
    if "PW" in phases:
        _ph_casts()
    if "P3b" in phases:
        _ph_P3bma()
    if "P4b" in phases:
        _ph_P4ba()
    if "P4c" in phases:
        _ph_P4ca()
    if "P6" in phases and "P7" in phases:
        _ph_P67()
    if "P65" in phases:
        _ph_P65a()

    ctx.finalize(sems)
    stack.close()
    return nc, sorted(need_in | always_const)


GROUPS = [ALL_PHASES]


def kernel(**inputs):
    n_tok = inputs["L0_attn_x"].shape[1]
    w = prep_weights(inputs)
    x = np.ascontiguousarray(np.asarray(inputs["L0_attn_x"], dtype=np.float32))
    ncore = x.shape[0]
    host = {}
    for c in range(ncore):
        host[("x_tm", c)] = np.ascontiguousarray(x[c])
        host[("x_fm", c)] = np.ascontiguousarray(x[c].T)
    state = [dict() for _ in range(ncore)]
    out = None
    for gi, group in enumerate(GROUPS):
        nc, in_names = build_program(n_tok, group)
        in_maps = []
        for c in range(ncore):
            m = {}
            for name in in_names:
                if name in w:
                    m[name] = w[name]
                elif (name, c) in host:
                    m[name] = host[(name, c)]
                else:
                    m[name] = state[c][name]
            in_maps.append(m)
        res = run_bass_kernel_spmd(nc, in_maps, core_ids=list(range(ncore)))
        for c in range(ncore):
            for k, arr in res.results[c].items():
                state[c][k] = arr
    outs = [np.ascontiguousarray(state[c]["out"].T) for c in range(ncore)]
    return np.stack(outs, axis=0).astype(np.float32)


# version 14
# speedup vs baseline: 1.0112x; 1.0112x over previous
import contextlib
import math
import os
import numpy as np
import concourse.bass as bass
import concourse.mybir as mybir
from concourse.bass_utils import run_bass_kernel_spmd

F32 = mybir.dt.float32
BF16 = mybir.dt.bfloat16
I32 = mybir.dt.int32
AF = mybir.ActivationFunctionType
ALU = mybir.AluOpType
AX = mybir.AxisListType

NCORE = 8
D = 1024
DI = 2048
MF = 2816
DS = 16
DR = 64
NE = 8
ME = 2048
NH = 8
TWO_PI = float(2.0 * np.pi)
MAGIC = 12582912.0
TC = 256

ENG = ("pe", "act", "dve", "pool", "sync")


class Buf:
    __slots__ = ("name", "writer", "readers")

    def __init__(self, name):
        self.name = name
        self.writer = None
        self.readers = []


NDMASEM = 8


class Ctx:
    def __init__(self, nc):
        self.nc = nc
        self.prog = {e: [] for e in ENG}
        self.seq = {"pe": 0, "act": 0, "dve": 0, "pool": 0, "sync": 0}
        for q in ("sync", "pool", "act"):
            for i in range(NDMASEM):
                self.seq["dq_%s_%d" % (q, i)] = 0
        self.dmacnt = {"sync": 0, "pool": 0, "act": 0}
        self.hw = {e: {} for e in ENG}
        self.bufs = {}

    def buf(self, name):
        if name not in self.bufs:
            self.bufs[name] = Buf(name)
        return self.bufs[name]

    PARANOID = False

    def op(self, eng, fn, reads=(), writes=(), dma=False):
        reads = [self.buf(r) if isinstance(r, str) else r for r in reads]
        writes = [self.buf(w) if isinstance(w, str) else w for w in writes]
        need = {}
        if Ctx.PARANOID:
            for k in self.seq:
                if Ctx.PARANOID is not True and k not in Ctx.PARANOID:
                    continue
                if self.seq[k] > 0 and self.hw[eng].get(k, 0) < self.seq[k] and not (k == eng == "pe"):
                    need[k] = self.seq[k]

        def add(src):
            if src is None:
                return
            key, s = src
            if key == eng and eng == "pe":
                return
            if self.hw[eng].get(key, 0) >= s:
                return
            if need.get(key, 0) < s:
                need[key] = s
        for b in reads:
            add(b.writer)
        for b in writes:
            add(b.writer)
            for r in b.readers:
                add(r)
        if dma:
            idx = self.dmacnt[eng] % NDMASEM
            self.dmacnt[eng] += 1
            key = "dq_%s_%d" % (eng, idx)
            if self.seq[key] > 0 and self.hw[eng].get(key, 0) < self.seq[key]:
                if need.get(key, 0) < self.seq[key]:
                    need[key] = self.seq[key]
        else:
            key = eng
        for k, val in need.items():
            self.hw[eng][k] = val
        inc = 16 if dma else 1
        self.seq[key] += inc
        myseq = self.seq[key]
        self.prog[eng].append((list(need.items()), fn, key, inc))
        for b in writes:
            b.writer = (key, myseq)
            b.readers = []
        for b in reads:
            b.readers.append((key, myseq))
        return myseq

    def dma(self, q, out_ap, in_ap, reads=(), writes=(), **kw):
        def fn(e, out_ap=out_ap, in_ap=in_ap, kw=kw):
            return e.dma_start(out=out_ap, in_=in_ap, **kw)
        return self.op(q, fn, reads=reads, writes=writes, dma=True)

    def dmat(self, q, out_ap, in_ap, reads=(), writes=()):
        def fn(e, out_ap=out_ap, in_ap=in_ap):
            return e.dma_start_transpose(out=out_ap, in_=in_ap)
        return self.op(q, fn, reads=reads, writes=writes, dma=True)

    def barrier(self):
        snap = dict(self.seq)
        for e in ENG:
            waits = []
            for k, val in snap.items():
                if val > 0 and self.hw[e].get(k, 0) < val:
                    waits.append((k, val))
                    self.hw[e][k] = val
            if waits:
                self.prog[e].append((waits, None, None, 0))

    def finalize(self, sems):
        self.barrier()
        final = dict(self.seq)
        nc = self.nc
        with nc.Block() as block:
            def runner(eng):
                def run(obj):
                    for waits, fn, key, inc in self.prog[eng]:
                        for k, val in waits:
                            obj.wait_ge(sems[k], val)
                        if fn is not None:
                            ins = fn(obj)
                            ins.then_inc(sems[key], inc)
                return run
            block.tensor(runner("pe"))
            block.scalar(runner("act"))
            block.vector(runner("dve"))
            block.gpsimd(runner("pool"))
            block.sync(runner("sync"))


def prep_weights(inp):
    w = {}
    f32 = lambda a: np.ascontiguousarray(np.asarray(a), dtype=np.float32)
    qk_w = f32(inp["L0_attn_qk_w"])
    qk_b = f32(inp["L0_attn_qk_b"])
    perm = np.r_[np.arange(0, D, 2), np.arange(1, D, 2)]
    w["qk_w"] = np.ascontiguousarray(qk_w[:, np.r_[perm, D + perm]])
    w["qk_b"] = np.ascontiguousarray(qk_b[np.r_[perm, D + perm]][None, :])
    w["lepe_w"] = f32(inp["L0_attn_lepe_w"])
    w["lepe_b"] = np.ascontiguousarray(f32(inp["L0_attn_lepe_b"]).reshape(8, 128).T)
    w["wg"] = f32(inp["L0_ffn_w_gate"])
    w["wu"] = f32(inp["L0_ffn_w_up"])
    w["wd"] = f32(inp["L0_ffn_w_down"])
    w["ln_g"] = np.ascontiguousarray(f32(inp["L1_attn_ln_g"]).reshape(8, 128).T)
    w["ln_b"] = np.ascontiguousarray(f32(inp["L1_attn_ln_b"]).reshape(8, 128).T)
    w["in_w"] = np.ascontiguousarray(f32(inp["L1_attn_in_proj_w"]).T)
    w["conv_w"] = np.ascontiguousarray(
        f32(inp["L1_attn_conv_w"])[:, 0, :].reshape(16, 128, 4).transpose(1, 0, 2).reshape(128, 64))
    w["conv_b"] = np.ascontiguousarray(f32(inp["L1_attn_conv_b"]).reshape(16, 128).T)
    xw = f32(inp["L1_attn_x_proj_w"])
    w["xw_dt"] = np.ascontiguousarray(xw[0:DR].T)
    w["xw_B"] = np.ascontiguousarray(xw[DR:DR + DS].T)
    w["xw_C"] = np.ascontiguousarray(xw[DR + DS:].T)
    w["dt_w"] = np.ascontiguousarray(f32(inp["L1_attn_dt_proj_w"]).T)
    w["dt_b"] = np.ascontiguousarray(f32(inp["L1_attn_dt_proj_b"]).reshape(16, 128).T)
    w["A_log"] = f32(inp["L1_attn_A_log"])
    w["D_"] = np.ascontiguousarray(f32(inp["L1_attn_D"]).reshape(16, 128).T)
    w["out_w"] = np.ascontiguousarray(f32(inp["L1_attn_out_proj_w"]).T)
    w["glu_w"] = np.ascontiguousarray(f32(inp["L1_attn_glu_w"]).T)
    w["glu_b"] = np.ascontiguousarray(f32(inp["L1_attn_glu_b"]).reshape(16, 128).T)
    w["Wr"] = f32(inp["L1_ffn_W_router"])
    w["Wg_e"] = f32(inp["L1_ffn_W_gate"])
    w["Wu_e"] = f32(inp["L1_ffn_W_up"])
    w["Wo_e"] = f32(inp["L1_ffn_W_out"])
    w["bg_e"] = np.ascontiguousarray(f32(inp["L1_ffn_b_gate"]).reshape(8, 16, 128).transpose(0, 2, 1))
    w["bu_e"] = np.ascontiguousarray(f32(inp["L1_ffn_b_up"]).reshape(8, 16, 128).transpose(0, 2, 1))
    w["bo_e"] = f32(inp["L1_ffn_b_out"])
    w["ident"] = np.eye(128, dtype=np.float32)
    return w


WEIGHT_SHAPES = {
    "qk_w": [D, 2 * D], "qk_b": [1, 2 * D], "lepe_w": [3, D, D], "lepe_b": [128, 8],
    "wg": [MF, D], "wu": [MF, D], "wd": [D, MF],
    "ln_g": [128, 8], "ln_b": [128, 8], "in_w": [D, 2 * DI],
    "conv_w": [128, 64], "conv_b": [128, 16],
    "xw_dt": [DI, DR], "xw_B": [DI, DS], "xw_C": [DI, DS],
    "dt_w": [DR, DI], "dt_b": [128, 16], "A_log": [DI, DS], "D_": [128, 16],
    "out_w": [DI, D], "glu_w": [D, DI], "glu_b": [128, 16],
    "Wr": [D, NE], "Wg_e": [NE, D, ME], "Wu_e": [NE, D, ME], "Wo_e": [NE, ME, D],
    "bg_e": [NE, 128, 16], "bu_e": [NE, 128, 16], "bo_e": [NE, D],
    "ident": [128, 128],
}

SCRATCH = {
    "xn_tm": (lambda N: [N, D], BF16),
    "xq_tm": (lambda N: [N, D], BF16),
    "cos_sp": (lambda N: [N, 512], F32),
    "sin_sp": (lambda N: [N, 512], F32),
    "theta_row": (lambda N: [1, 512], F32),
    "q_tm": (lambda N: [N, D], BF16),
    "qr_tm": (lambda N: [N, D], BF16),
    "kv_d": (lambda N: [128, 1024], BF16),
    "ks_d": (lambda N: [128, 8], F32),
    "kmean_row": (lambda N: [1, D], F32),
    "z_rows": (lambda N: [NH, N], F32),
    "s0_f": (lambda N: [D, N], F32),
    "wgqT_d": (lambda N: [D, MF], BF16),
    "wuqT_d": (lambda N: [D, MF], BF16),
    "wdqT_d": (lambda N: [8, MF, 128], BF16),
    "hq_st": (lambda N: [512, MF], BF16),
    "x1_f": (lambda N: [D, N], F32),
    "xn2_bf": (lambda N: [D, N], BF16),
    "xn2_f": (lambda N: [D, N], F32),
    "xni_bf": (lambda N: [D, N], BF16),
    "mu_row": (lambda N: [2, N], F32),
    "mu2_row": (lambda N: [2, N], F32),
    "siluz": (lambda N: [DI, N], BF16),
    "xc_f": (lambda N: [DI, N], F32),
    "dt_f": (lambda N: [DI, N], F32),
    "dtu_f": (lambda N: [DI, N], BF16),
    "B_d": (lambda N: [DS, N], F32),
    "C_d": (lambda N: [DS, N], F32),
    "ew_rows": (lambda N: [NE, N], F32),
    "y3": (lambda N: [DI, N], BF16),
    "s1_f": (lambda N: [D, N], F32),
    "moe_f": (lambda N: [D, N], F32),
    "wbf": (lambda N: [3 * NE * D * ME], BF16),
    "dbg1": (lambda N: [D, N], F32),
    "dbg2": (lambda N: [D, N], F32),
    "dbgb": (lambda N: [128, 8192], BF16),
    "dbgf": (lambda N: [128, 8192], F32),
}

PHASE_IO = {
    "P1": (["x_tm"], ["xn_tm", "xq_tm", "cos_sp", "sin_sp"]),
    "PW": (["Wg_e", "Wu_e", "Wo_e"], ["wbf"]),
    "P2": (["xn_tm", "cos_sp", "sin_sp", "qk_w", "qk_b"], ["q_tm", "qr_tm", "kv_d", "ks_d"]),
    "P3a": (["q_tm", "qr_tm", "xn_tm", "kv_d", "ks_d", "lepe_w", "lepe_b"], ["s0_f"]),
    "P3b": (["xq_tm", "wg", "wu", "wd", "s0_f", "x_fm", "ln_g", "ln_b"],
            ["x1_f", "xn2_bf", "xn2_f", "xni_bf"]),
    "P4b": (["xni_bf", "in_w", "conv_w", "conv_b", "xw_dt", "xw_B", "xw_C", "dt_w", "dt_b"],
            ["siluz", "xc_f", "dt_f", "dtu_f", "B_d", "C_d"]),
    "P4c": (["xn2_bf", "Wr"], ["ew_rows"]),
    "P6": (["dt_f", "dtu_f", "xc_f", "siluz", "B_d", "C_d", "A_log", "D_"], ["y3"]),
    "P7": (["wbf", "xn2_bf", "ew_rows", "bg_e", "bu_e", "bo_e"], ["moe_f"]),
    "P65": (["y3", "out_w", "glu_w", "glu_b", "xn2_f", "moe_f", "x1_f"], ["out"]),
}
ALL_PHASES = ["P1", "PW", "P2", "P3a", "P3b", "P4b", "P4c", "P6", "P7", "P65"]


def build_program(n_tok, phases, taps=()):
    nc = bass.Bass("TRN2", target_bir_lowering=False)
    N = n_tok
    NT1 = N // 128
    NT5 = N // 512
    NTC = N // TC
    phases = list(phases)
    need_in = set()
    need_out = set()
    produced = set()
    for ph in phases:
        ins, outs = PHASE_IO[ph]
        for x in ins:
            if x not in produced:
                need_in.add(x)
        for x in outs:
            produced.add(x)
    for ph in phases:
        for x in PHASE_IO[ph][1]:
            need_out.add(x)
    aps = {}
    always_const = {"ident"}
    for name in sorted(need_in | always_const):
        if name in WEIGHT_SHAPES:
            aps[name] = nc.dram_tensor(name, WEIGHT_SHAPES[name], F32, kind="ExternalInput").ap()
        elif name in ("x_tm",):
            aps[name] = nc.dram_tensor(name, [N, D], F32, kind="ExternalInput").ap()
        elif name == "x_fm":
            aps[name] = nc.dram_tensor(name, [D, N], F32, kind="ExternalInput").ap()
        else:
            s, dt = SCRATCH[name][0](N), SCRATCH[name][1]
            aps[name] = nc.dram_tensor(name, s, dt, kind="ExternalInput").ap()
    for name in sorted(need_out):
        if name == "out":
            aps[name] = nc.dram_tensor(name, [D, N], F32, kind="ExternalOutput").ap()
        else:
            s, dt = SCRATCH[name][0](N), SCRATCH[name][1]
            aps[name] = nc.dram_tensor(name, s, dt, kind="ExternalOutput").ap()
    for name, (sfn, dt) in SCRATCH.items():
        if name not in aps:
            kind = "ExternalOutput" if name in taps else "Internal"
            aps[name] = nc.dram_tensor(name, sfn(N), dt, kind=kind).ap()
    if "out" not in aps:
        aps["out"] = nc.dram_tensor("out", [D, N], F32,
                                    kind="ExternalOutput" if "P8" in phases else "Internal").ap()

    ctx = Ctx(nc)
    v = ctx.buf
    _uc = [0]
    def _u(name):
        _uc[0] += 1
        return "%s_%d" % (name, _uc[0])
    stack = contextlib.ExitStack()
    sems = {}
    semnames = ["pe", "act", "dve", "pool", "sync"]
    for q in ("sync", "pool", "act"):
        for i in range(NDMASEM):
            semnames.append("dq_%s_%d" % (q, i))
    for k in semnames:
        sems[k] = stack.enter_context(nc.semaphore(k))

    CONST = stack.enter_context(nc.sbuf_tensor(_u("CONST"), [128, 512], F32))
    SMALL = stack.enter_context(nc.sbuf_tensor(_u("SMALL"), [128, 512], F32))
    PP = stack.enter_context(nc.sbuf_tensor(_u("PP"), [128, 2048], F32))
    BC1 = stack.enter_context(nc.sbuf_tensor(_u("BC1"), [128, 2048], F32))
    IDT = stack.enter_context(nc.sbuf_tensor(_u("IDT"), [128, 128], F32))
    IDB = stack.enter_context(nc.sbuf_tensor(_u("IDB"), [128, 128], BF16))
    ONE = stack.enter_context(nc.sbuf_tensor(_u("ONE"), [128, 8], F32))
    ONEB = stack.enter_context(nc.sbuf_tensor(_u("ONEB"), [128, 8], BF16))

    ctx.dma("sync", IDT[:], aps["ident"][:], writes=["IDT"])
    ctx.op("dve", lambda e: e.tensor_copy(out=IDB[:], in_=IDT[:]), reads=["IDT"], writes=["IDB"])
    ctx.op("dve", lambda e: e.memset(ONE[:], 1.0), writes=["ONE"])
    ctx.op("dve", lambda e: e.memset(ONEB[:], 1.0), writes=["ONEB"])
    cmap = {"conv_w": (0, 64), "conv_b": (64, 16), "dt_b": (80, 16), "D_": (96, 16),
            "glu_b": (112, 16), "ln_g": (128, 8), "ln_b": (136, 8), "lepe_b": (144, 8)}
    for name, (off, w_) in cmap.items():
        if name in aps:
            ctx.dma("sync", CONST[:, off:off + w_], aps[name][:], writes=["CONST"])

    def rsqrt(out_ap, in_ap, t1, t2, rd, wr, nrows=128):
        ctx.op("dve", lambda e: e.tensor_scalar(out=t1.bitcast(I32), in0=in_ap.bitcast(I32),
                                                scalar1=1, scalar2=None,
                                                op0=ALU.logical_shift_right),
               reads=rd, writes=["rsq1"])
        ctx.op("dve", lambda e: e.tensor_scalar(out=t1.bitcast(I32), in0=t1.bitcast(I32),
                                                scalar1=-1, scalar2=0x5f3759df,
                                                op0=ALU.mult, op1=ALU.add),
               reads=["rsq1"], writes=["rsq1"])
        for _ in range(3):
            ctx.op("dve", lambda e: e.tensor_tensor(out=t2, in0=t1, in1=t1, op=ALU.mult),
                   reads=["rsq1"], writes=["rsq2"])
            ctx.op("dve", lambda e: e.tensor_tensor(out=t2, in0=t2, in1=in_ap, op=ALU.mult),
                   reads=["rsq2"] + rd, writes=["rsq2"])
            ctx.op("dve", lambda e: e.tensor_scalar(out=t2, in0=t2, scalar1=-0.5, scalar2=1.5,
                                                    op0=ALU.mult, op1=ALU.add),
                   reads=["rsq2"], writes=["rsq2"])
            ctx.op("dve", lambda e: e.tensor_tensor(out=t1, in0=t1, in1=t2, op=ALU.mult),
                   reads=["rsq1", "rsq2"], writes=["rsq1"])
        ctx.op("dve", lambda e: e.tensor_copy(out=out_ap, in_=t1),
               reads=["rsq1"], writes=wr)

    def quant(eng, x_ap, out_ap, F, bm, sg, vv, acc, tag, rd, wr):
        nb = F // 16
        x3 = x_ap.rearrange("p (a b) -> p a b", b=16)
        ctx.op(eng, lambda e: e.tensor_reduce(out=bm[:, 0:nb], in_=x3, axis=AX.X, op=ALU.max,
                                              apply_absolute_value=True),
               reads=rd, writes=[tag + "bm"])
        ctx.op(eng, lambda e: e.tensor_scalar(out=bm[:, 0:nb], in0=bm[:, 0:nb],
                                              scalar1=1.0 / 6.0, scalar2=1e-12,
                                              op0=ALU.mult, op1=ALU.max),
               reads=[tag + "bm"], writes=[tag + "bm"])
        ctx.op("dve", lambda e: e.reciprocal(out=bm[:, nb:2 * nb], in_=bm[:, 0:nb]),
               reads=[tag + "bm"], writes=[tag + "bm"])
        ctx.op("act", lambda e: e.sign(out=sg[:, 0:F], in_=x_ap), reads=rd, writes=[tag + "sg"])
        ctx.op("act", lambda e: e.activation(out=vv[:, 0:F], in_=x_ap, func=AF.Abs),
               reads=rd, writes=[tag + "v"])
        rscb = bm[:, nb:2 * nb].unsqueeze(2).broadcast_to((128, nb, 16))
        v3 = vv[:, 0:F].rearrange("p (a b) -> p a b", b=16)
        ctx.op(eng, lambda e: e.tensor_tensor(out=v3, in0=v3, in1=rscb, op=ALU.mult),
               reads=[tag + "v", tag + "bm"], writes=[tag + "v"])
        ctx.op(eng, lambda e: e.tensor_scalar(out=bm[:, 0:nb], in0=bm[:, 0:nb], scalar1=0.5,
                                              scalar2=None, op0=ALU.mult),
               reads=[tag + "bm"], writes=[tag + "bm"])
        schb = bm[:, 0:nb].unsqueeze(2).broadcast_to((128, nb, 16))
        s3 = sg[:, 0:F].rearrange("p (a b) -> p a b", b=16)
        ctx.op(eng, lambda e: e.tensor_tensor(out=s3, in0=s3, in1=schb, op=ALU.mult),
               reads=[tag + "sg", tag + "bm"], writes=[tag + "sg"])
        ctx.op(eng, lambda e: e.tensor_scalar(out=acc[:, 0:F], in0=vv[:, 0:F], scalar1=5.0,
                                              scalar2=2.0, op0=ALU.is_gt, op1=ALU.mult),
               reads=[tag + "v"], writes=[tag + "acc"])
        for m in (3.5, 2.5):
            ctx.op(eng, lambda e, m=m: e.scalar_tensor_tensor(out=acc[:, 0:F], in0=vv[:, 0:F],
                                                              scalar=m, in1=acc[:, 0:F],
                                                              op0=ALU.is_gt, op1=ALU.add),
                   reads=[tag + "v", tag + "acc"], writes=[tag + "acc"])
        ctx.op(eng, lambda e: e.tensor_scalar(out=acc[:, 0:F], in0=acc[:, 0:F], scalar1=2.0,
                                              scalar2=None, op0=ALU.mult),
               reads=[tag + "acc"], writes=[tag + "acc"])
        for m in (1.75, 1.25, 0.75, 0.25):
            ctx.op(eng, lambda e, m=m: e.scalar_tensor_tensor(out=acc[:, 0:F], in0=vv[:, 0:F],
                                                              scalar=m, in1=acc[:, 0:F],
                                                              op0=ALU.is_gt, op1=ALU.add),
                   reads=[tag + "v", tag + "acc"], writes=[tag + "acc"])
        ctx.op(eng, lambda e: e.tensor_tensor(out=out_ap, in0=acc[:, 0:F], in1=sg[:, 0:F],
                                              op=ALU.mult),
               reads=[tag + "acc", tag + "sg"], writes=wr)

    def rrange(ap_, tmp_, rdwr):
        ctx.op("dve", lambda e: e.tensor_scalar(out=tmp_, in0=ap_, scalar1=1.0 / TWO_PI,
                                                scalar2=MAGIC, op0=ALU.mult, op1=ALU.add),
               reads=[rdwr], writes=["rrtmp"])
        ctx.op("dve", lambda e: e.tensor_scalar(out=tmp_, in0=tmp_, scalar1=MAGIC, scalar2=None,
                                                op0=ALU.subtract),
               reads=["rrtmp"], writes=["rrtmp"])
        ctx.op("dve", lambda e: e.scalar_tensor_tensor(out=ap_, in0=tmp_, scalar=-TWO_PI,
                                                       in1=ap_, op0=ALU.mult, op1=ALU.add),
               reads=["rrtmp", rdwr], writes=[rdwr])

    def cast_load(dst_ap, src_ap, stage_ap, tag, cols):
        ctx.dma("pool", dst_ap, src_ap, reads=[tag], writes=[tag])

    def _ph_prologue():
        with (nc.sbuf_tensor(_u("XA"), [128, 6656], F32) as XA,
              nc.sbuf_tensor(_u("XB"), [128, 4096], F32) as XB,
              nc.sbuf_tensor(_u("B16"), [128, 4096], BF16) as B16,
              nc.sbuf_tensor(_u("QXA"), [128, 7680], F32) as QXA,
              nc.sbuf_tensor(_u("QXB"), [128, 8448], F32) as QXB,
              nc.sbuf_tensor(_u("QB16A"), [128, 11264], BF16) as QB16A,
              nc.sbuf_tensor(_u("QB16B"), [128, 512], BF16) as QB16B,
              nc.sbuf_tensor(_u("SM2"), [128, 512], F32) as SM2,
              nc.psum_tensor(_u("PST0"), [128, 128], BF16) as PST0,
              nc.psum_tensor(_u("PST1"), [128, 128], BF16) as PST1):
            ctx.op("pool", lambda e: e.iota(PP[0:1, 0:512], [[1, 512]], base=0,
                                            channel_multiplier=0,
                                            allow_small_or_imprecise_dtypes=True),
                   writes=["PP"])
            ctx.op("act", lambda e: e.activation(out=PP[0:1, 512:1024], in_=PP[0:1, 0:512],
                                                 func=AF.Exp, scale=-math.log(10000.0) / 512.0),
                   reads=["PP"], writes=["PP"])
            ctx.dma("pool", aps["theta_row"][:], PP[0:1, 512:1024], reads=["PP"],
                    writes=["theta_d"])
            th_b = bass.AP(aps["theta_row"].tensor, 0, [[0, 128], [1, 512]])
            ctx.dma("sync", BC1[:, 0:512], th_b, reads=["theta_d"], writes=["BC1"])

            def p1_tile(i):
                par = i % 2
                x_t = XA[:, par * 1024:par * 1024 + 1024]
                xn_t = XA[:, 2048:3072]
                xtag = "p1x%d" % par
                ctx.dma("sync", x_t, aps["x_tm"][i * 128:(i + 1) * 128, :], writes=[xtag])
                ctx.op("dve", lambda e, x_t=x_t: e.bn_stats(out=SMALL[:, 0:6],
                                                            in_=x_t[:, 0:512]),
                       reads=[xtag], writes=["SMALL"])
                ctx.op("dve", lambda e, x_t=x_t: e.bn_stats(out=SMALL[:, 6:12],
                                                            in_=x_t[:, 512:1024]),
                       reads=[xtag], writes=["SMALL"])
                ctx.op("dve", lambda e: e.bn_aggr(out=SMALL[:, 12:14], in_=SMALL[:, 0:12]),
                       reads=["SMALL"], writes=["SMALL"])
                ctx.op("dve", lambda e: e.tensor_scalar(out=SMALL[:, 14:15], in0=SMALL[:, 13:14],
                                                        scalar1=1e-5, scalar2=None, op0=ALU.add),
                       reads=["SMALL"], writes=["SMALL"])
                rsqrt(SMALL[:, 15:16], SMALL[:, 14:15], SMALL[:, 16:17], SMALL[:, 17:18],
                      ["SMALL"], ["SMALL"])
                ctx.op("dve", lambda e, x_t=x_t, xn_t=xn_t: e.tensor_scalar(
                    out=xn_t, in0=x_t, scalar1=SMALL[:, 12:13], scalar2=SMALL[:, 15:16],
                    op0=ALU.subtract, op1=ALU.mult),
                       reads=[xtag, "SMALL"], writes=["p1xn"])
                nbt = "p1xnb%d" % par
                ctx.op("dve", lambda e, xn_t=xn_t, par=par: e.tensor_copy(
                    out=B16[:, par * 1024:par * 1024 + 1024], in_=xn_t),
                       reads=["p1xn"], writes=[nbt])
                ctx.dma("act", aps["xn_tm"][i * 128:(i + 1) * 128, :],
                        B16[:, par * 1024:par * 1024 + 1024],
                        reads=[nbt], writes=["xn_tm_d"])
                qbt = "p1xqb%d" % par
                quant("dve", xn_t, B16[:, 2048 + par * 1024:2048 + par * 1024 + 1024], 1024,
                      SMALL[:, 64:192], XB[:, 0:1024],
                      XB[:, 1024:2048], XB[:, 2048:3072], "p1q", ["p1xn"], [qbt])
                ctx.dma("act", aps["xq_tm"][i * 128:(i + 1) * 128, :],
                        B16[:, 2048 + par * 1024:2048 + par * 1024 + 1024],
                        reads=[qbt], writes=["xq_tm_d"])
                ctx.op("pool", lambda e, i=i: e.iota(SMALL[:, 20:21], [[0, 1]], base=i * 128,
                                                     channel_multiplier=1,
                                                     allow_small_or_imprecise_dtypes=True),
                       writes=["p1tok"])
                a_s = XA[:, 3072:3584]
                a_c = XA[:, 3584:4096]
                ctx.op("dve", lambda e, a_s=a_s: e.tensor_scalar(out=a_s, in0=BC1[:, 0:512],
                                                                 scalar1=SMALL[:, 20:21],
                                                                 scalar2=None, op0=ALU.mult),
                       reads=["BC1", "p1tok"], writes=["p1as"])
                ctx.op("dve", lambda e, a_s=a_s, a_c=a_c: e.tensor_scalar(
                    out=a_c, in0=a_s, scalar1=math.pi / 2, scalar2=None, op0=ALU.add),
                       reads=["p1as"], writes=["p1ac"])
                rrange(a_s, XA[:, 4096:4608], "p1as")
                rrange(a_c, XA[:, 4096:4608], "p1ac")
                ctx.op("act", lambda e, a_s=a_s: e.activation(out=XA[:, 4608:5120], in_=a_s,
                                                              func=AF.Sin),
                       reads=["p1as"], writes=["p1sin"])
                ctx.dma("act", aps["sin_sp"][i * 128:(i + 1) * 128, :], XA[:, 4608:5120],
                        reads=["p1sin"], writes=["sin_d"])
                ctx.op("act", lambda e, a_c=a_c: e.activation(out=XA[:, 5120:5632], in_=a_c,
                                                              func=AF.Sin),
                       reads=["p1ac"], writes=["p1cos"])
                ctx.dma("act", aps["cos_sp"][i * 128:(i + 1) * 128, :], XA[:, 5120:5632],
                        reads=["p1cos"], writes=["cos_d"])

            _punit = [0]

            def p30_gu(wname, dname, mc):
                par = _punit[0] % 2
                _punit[0] += 1
                xtag = "p30x%d" % par
                qtag = "p30qb%d" % par
                qa = QXA[:, par * 1024:par * 1024 + 1024]
                qb = QB16A[:, par * 5632:par * 5632 + 1024]
                ctx.dma("sync", qa, aps[wname][mc * 128:(mc + 1) * 128, :],
                        reads=[xtag], writes=[xtag])
                quant("dve", qa, qb, 1024, SM2[:, 0:256],
                      QXB[:, 0:1024], QXB[:, 1024:2048], QXB[:, 2048:3072], "p30q",
                      [xtag], [qtag])
                for kc in range(8):
                    pst = PST0 if kc % 2 == 0 else PST1
                    ptag = "PST%d" % (kc % 2)
                    btag = "p30tb%d" % (kc % 4)
                    bsl = QB16B[:, (kc % 4) * 128:(kc % 4) * 128 + 128]
                    ctx.op("pe", lambda e, kc=kc, pst=pst, qb=qb: e.transpose(
                        pst[:], qb[:, kc * 128:(kc + 1) * 128], IDB[:]),
                           reads=[qtag, "IDB", btag], writes=[ptag])
                    ctx.op("dve", lambda e, bsl=bsl, pst=pst: e.tensor_copy(out=bsl, in_=pst[:]),
                           reads=[ptag], writes=[btag])
                    ctx.dma("pool", aps[dname][kc * 128:(kc + 1) * 128, mc * 128:(mc + 1) * 128],
                            bsl, reads=[btag], writes=[dname + "_d"])

            def p30_wd(dc):
                par = _punit[0] % 2
                _punit[0] += 1
                xtag = "p30x%d" % par
                qtag = "p30qb%d" % par
                qa = QXA[:, 2048 + par * 2816:2048 + par * 2816 + 2816]
                qb = QB16A[:, par * 5632:par * 5632 + 2816]
                ctx.dma("sync", qa, aps["wd"][dc * 128:(dc + 1) * 128, :],
                        reads=[xtag], writes=[xtag])
                quant("dve", qa, qb, 2816, SM2[:, 0:384],
                      QXB[:, 0:2816], QXB[:, 2816:5632], QXB[:, 5632:8448],
                      "p30q", [xtag], [qtag])
                for mc in range(22):
                    pst = PST0 if mc % 2 == 0 else PST1
                    ptag = "PST%d" % (mc % 2)
                    btag = "p30tb%d" % (mc % 4)
                    bsl = QB16B[:, (mc % 4) * 128:(mc % 4) * 128 + 128]
                    ctx.op("pe", lambda e, mc=mc, pst=pst, qb=qb: e.transpose(
                        pst[:], qb[:, mc * 128:(mc + 1) * 128], IDB[:]),
                           reads=[qtag, "IDB", btag], writes=[ptag])
                    ctx.op("dve", lambda e, bsl=bsl, pst=pst: e.tensor_copy(out=bsl, in_=pst[:]),
                           reads=[ptag], writes=[btag])
                    ctx.dma("pool", aps["wdqT_d"][dc, mc * 128:(mc + 1) * 128, :],
                            bsl, reads=[btag], writes=["wdqT_d_d"])

            lp1 = [lambda i=i: p1_tile(i) for i in range(NT1)]
            l30 = ([lambda w=w, d=d, mc=mc: p30_gu(w, d, mc)
                    for w, d in (("wg", "wgqT_d"), ("wu", "wuqT_d")) for mc in range(22)]
                   + [lambda dc=dc: p30_wd(dc) for dc in range(8)])
            r30 = float(len(l30)) / float(len(lp1))
            credit = 0.0
            j = 0
            for t in lp1:
                t()
                credit += r30
                while credit >= 1.0 and j < len(l30):
                    l30[j]()
                    j += 1
                    credit -= 1.0
            while j < len(l30):
                l30[j]()
                j += 1
        ctx.barrier()

    def _ph_casts():
        GU = NE * D * ME
        for e_ in range(NE):
            for base, srcname, nr, ncol in ((0, "Wg_e", D, ME), (GU, "Wu_e", D, ME),
                                            (2 * GU, "Wo_e", ME, D)):
                dst = bass.AP(aps["wbf"].tensor, base + e_ * D * ME, [[ncol, nr], [1, ncol]])
                ctx.dma("pool", dst, aps[srcname][e_], writes=["wbf_d"])

    def _ph_P2a():
        with (nc.sbuf_tensor(_u("W1"), [128, 16384], BF16) as W1,
              nc.sbuf_tensor(_u("XA"), [128, 6144], F32) as XA,
              nc.sbuf_tensor(_u("XB"), [128, 4096], F32) as XB,
              nc.sbuf_tensor(_u("B16S"), [128, 4096], BF16) as B16S,
              nc.sbuf_tensor(_u("B16T"), [128, 4096], BF16) as B16T,
              nc.psum_tensor(_u("PSQ0"), [128, 512], F32) as PSQ0,
              nc.psum_tensor(_u("PSQ1"), [128, 512], F32) as PSQ1,
              nc.psum_tensor(_u("PSKV0"), [128, 512], F32) as PSKV0,
              nc.psum_tensor(_u("PSKV1"), [128, 512], F32) as PSKV1,
              nc.psum_tensor(_u("PSKS"), [128, 8], F32) as PSKS):
            for c in range(8):
                cast_load(W1[:, c * 2048:(c + 1) * 2048],
                          aps["qk_w"][c * 128:(c + 1) * 128, :], PP, "w1", 2048)
            qkb_b = bass.AP(aps["qk_b"].tensor, 0, [[0, 128], [1, 2048]])
            ctx.dma("sync", BC1[:, 0:2048], qkb_b, reads=["BC1"], writes=["BC1"])
            NT5_ = N // 512
            for st in range(NT5_):
                n0 = st * 512
                for c in range(8):
                    ctx.dmat("sync", B16S[:, c * 512:(c + 1) * 512],
                             aps["xn_tm"][n0:n0 + 512, c * 128:(c + 1) * 128],
                             reads=["xn_tm_d"], writes=["p2xnT"])
                for sub in range(4):
                    i = st * 4 + sub
                    first = (i == 0)
                    last = (i == NT1 - 1)
                    ctx.dma("sync", B16T[:, 0:1024],
                            aps["xn_tm"][i * 128:(i + 1) * 128, :],
                            reads=["xn_tm_d"], writes=["p2vh"])
                    for fg in range(4):
                        ps = PSQ0 if fg % 2 == 0 else PSQ1
                        pbuf = "PSQ%d" % (fg % 2)
                        def mmq2(e, fg=fg, sub=sub, ps=ps):
                            ins = None
                            for k in range(8):
                                ins = e.matmul(ps[:],
                                               B16S[:, k * 512 + sub * 128: k * 512 + sub * 128 + 128],
                                               W1[:, k * 2048 + fg * 512: k * 2048 + fg * 512 + 512],
                                               start=(k == 0), stop=(k == 7))
                            return ins
                        ctx.op("pe", mmq2, reads=["p2xnT", "w1"], writes=[pbuf])
                        ctx.op("dve", lambda e, fg=fg, ps=ps: e.tensor_tensor(
                            out=XA[:, fg * 512:(fg + 1) * 512], in0=ps[:],
                            in1=BC1[:, fg * 512:(fg + 1) * 512], op=ALU.add),
                               reads=[pbuf, "BC1"], writes=["p2qk"])
                    for half, dst0 in ((0, 4096), (1, 5120)):
                        src = XA[:, half * 1024:(half + 1) * 1024]
                        ctx.op("dve", lambda e, src=src: e.tensor_scalar_min(
                            out=XB[:, 0:1024], in0=src, scalar1=0.0),
                               reads=["p2qk"], writes=["p2t"])
                        ctx.op("act", lambda e: e.activation(out=XB[:, 0:1024], in_=XB[:, 0:1024],
                                                             func=AF.Exp),
                               reads=["p2t"], writes=["p2t"])
                        ctx.op("dve", lambda e, src=src, dst0=dst0: e.scalar_tensor_tensor(
                            out=XA[:, dst0:dst0 + 1024], in0=src, scalar=0.0,
                            in1=XB[:, 0:1024], op0=ALU.max, op1=ALU.add),
                               reads=["p2qk", "p2t"], writes=["p2q" if half == 0 else "p2k"])
                    ctx.op("dve", lambda e: e.tensor_copy(out=B16T[:, 1024:2048],
                                                          in_=XA[:, 4096:5120]),
                           reads=["p2q"], writes=["p2qb"])
                    ctx.dma("pool", aps["q_tm"][i * 128:(i + 1) * 128, :], B16T[:, 1024:2048],
                            reads=["p2qb"], writes=["q_tm_d"])
                    ctx.op("dve", lambda e: e.tensor_copy(out=B16T[:, 2048:3072],
                                                          in_=XA[:, 5120:6144]),
                           reads=["p2k"], writes=["p2kb"])
                    def mmks(e, first=first, last=last):
                        ins = None
                        for c in range(8):
                            ins = e.matmul(PSKS[:, c:c + 1], B16T[:, 2048 + c * 128:2048 + (c + 1) * 128],
                                           ONEB[:, 0:1], start=(first and c == 0), stop=last)
                        return ins
                    ctx.op("pe", mmks, reads=["p2kb", "ONEB"], writes=["PSKS"])
                    ctx.dma("sync", XB[:, 1024:1536], aps["cos_sp"][i * 128:(i + 1) * 128, :],
                            reads=["cos_d"], writes=["p2cos"])
                    ctx.dma("sync", XB[:, 1536:2048], aps["sin_sp"][i * 128:(i + 1) * 128, :],
                            reads=["sin_d"], writes=["p2sin"])
                    for half, qoff, rbuf in ((0, 4096, "p2qr"), (1, 5120, "p2kr")):
                        qe = XA[:, qoff:qoff + 512]
                        qo = XA[:, qoff + 512:qoff + 1024]
                        ctx.op("dve", lambda e, qe=qe: e.tensor_tensor(out=XB[:, 2048:2560],
                                                                       in0=XB[:, 1024:1536], in1=qe,
                                                                       op=ALU.mult),
                               reads=["p2cos", "p2q", "p2k"], writes=["p2r1"])
                        ctx.op("dve", lambda e, qo=qo: e.tensor_tensor(out=XB[:, 2560:3072],
                                                                       in0=XB[:, 1536:2048], in1=qo,
                                                                       op=ALU.mult),
                               reads=["p2sin", "p2q", "p2k"], writes=["p2r2"])
                        ctx.op("dve", lambda e: e.tensor_tensor(out=XB[:, 3072:3584],
                                                                in0=XB[:, 2048:2560],
                                                                in1=XB[:, 2560:3072],
                                                                op=ALU.subtract),
                               reads=["p2r1", "p2r2"], writes=["p2pr"])
                        ctx.op("dve", lambda e, qo=qo: e.tensor_tensor(out=XB[:, 2048:2560],
                                                                       in0=XB[:, 1024:1536], in1=qo,
                                                                       op=ALU.mult),
                               reads=["p2cos", "p2q", "p2k", "p2r1"], writes=["p2r1"])
                        ctx.op("dve", lambda e, qe=qe: e.tensor_tensor(out=XB[:, 2560:3072],
                                                                       in0=XB[:, 1536:2048], in1=qe,
                                                                       op=ALU.mult),
                               reads=["p2sin", "p2q", "p2k", "p2r2"], writes=["p2r2"])
                        ctx.op("dve", lambda e: e.tensor_tensor(out=XB[:, 3584:4096],
                                                                in0=XB[:, 2048:2560],
                                                                in1=XB[:, 2560:3072],
                                                                op=ALU.add),
                               reads=["p2r1", "p2r2"], writes=["p2pi"])
                        ctx.op("dve", lambda e: e.tensor_copy(out=B16T[:, 3072:3584],
                                                              in_=XB[:, 3072:3584]),
                               reads=["p2pr"], writes=[rbuf])
                        ctx.op("dve", lambda e: e.tensor_copy(out=B16T[:, 3584:4096],
                                                              in_=XB[:, 3584:4096]),
                               reads=["p2pi"], writes=[rbuf])
                        if half == 0:
                            ctx.dma("pool", aps["qr_tm"][i * 128:(i + 1) * 128, :],
                                    B16T[:, 3072:4096], reads=[rbuf], writes=["qr_tm_d"])
                        else:
                            ctx.op("dve", lambda e: e.tensor_copy(out=B16T[:, 2048:3072],
                                                                  in_=B16T[:, 3072:4096]),
                                   reads=[rbuf], writes=["p2kb"])
                    def mmkv(e, first=first, last=last):
                        ins = None
                        for h in range(8):
                            ps = PSKV0 if h < 4 else PSKV1
                            ins = e.matmul(ps[:, (h % 4) * 128:(h % 4 + 1) * 128],
                                           B16T[:, 2048 + h * 128:2048 + (h + 1) * 128],
                                           B16T[:, h * 128:(h + 1) * 128],
                                           start=(first and h % 4 == 0), stop=last)
                        return ins
                    ctx.op("pe", mmkv, reads=["p2kb", "p2vh"], writes=["PSKV0", "PSKV1"])
            sc = 1.0 / math.sqrt(float(N))
            ctx.op("act", lambda e: e.activation(out=B16T[:, 0:512], in_=PSKV0[:],
                                                 func=AF.Identity, scale=sc),
                   reads=["PSKV0"], writes=["p2kvb"])
            ctx.op("act", lambda e: e.activation(out=B16T[:, 512:1024], in_=PSKV1[:],
                                                 func=AF.Identity, scale=sc),
                   reads=["PSKV1"], writes=["p2kvb"])
            ctx.dma("pool", aps["kv_d"][:], B16T[:, 0:1024], reads=["p2kvb"], writes=["kv_dd"])
            ctx.op("dve", lambda e: e.tensor_copy(out=SMALL[:, 0:8], in_=PSKS[:]),
                   reads=["PSKS"], writes=["p2ks"])
            ctx.dma("pool", aps["ks_d"][:], SMALL[:, 0:8], reads=["p2ks"], writes=["ks_dd"])
        ctx.barrier()

    def _ph_P3aa():
        with (nc.sbuf_tensor(_u("W2"), [128, 24576], BF16) as W2,
              nc.sbuf_tensor(_u("XA"), [128, 4096], F32) as XA,
              nc.sbuf_tensor(_u("XD"), [128, 2048], F32) as XD,
              nc.sbuf_tensor(_u("B16S"), [128, 8192], BF16) as B16S,
              nc.sbuf_tensor(_u("B16T"), [128, 2048], BF16) as B16T,
              nc.sbuf_tensor(_u("KVZ"), [128, 1024], BF16) as KVZ,
              nc.psum_tensor(_u("PSL0"), [128, 512], F32) as PSL0,
              nc.psum_tensor(_u("PSL1"), [128, 512], F32) as PSL1,
              nc.psum_tensor(_u("PSA"), [128, 512], F32) as PSA,
              nc.psum_tensor(_u("PSZ"), [128, 128], F32) as PSZ):
            for w_ in range(3):
                for kc in range(8):
                    cast_load(W2[:, (w_ * 8 + kc) * 1024:(w_ * 8 + kc + 1) * 1024],
                              aps["lepe_w"][w_, kc * 128:(kc + 1) * 128, :], PP, "w2", 1024)
            ctx.dma("sync", KVZ[:], aps["kv_d"][:], reads=["kv_dd"], writes=["KVZ"])
            ctx.dma("sync", SMALL[:, 0:8], aps["ks_d"][:], reads=["ks_dd"], writes=["p3ks"])
            ctx.op("dve", lambda e: e.tensor_scalar_mul(out=SMALL[:, 8:16], in0=SMALL[:, 0:8],
                                                        scalar1=1.0 / float(N)),
                   reads=["p3ks"], writes=["p3km"])
            ctx.op("pe", lambda e: e.transpose(PSZ[0:8, :], SMALL[:, 8:16], IDT[:]),
                   reads=["p3km", "IDT"], writes=["PSZ"])
            ctx.op("dve", lambda e: e.tensor_copy(out=PP[0:8, 0:128], in_=PSZ[0:8, :]),
                   reads=["PSZ"], writes=["p3kmr"])
            km_dst = bass.AP(aps["kmean_row"].tensor, 0, [[128, 8], [1, 128]])
            ctx.dma("pool", km_dst, PP[0:8, 0:128], reads=["p3kmr"], writes=["kmean_d"])
            km_b = bass.AP(aps["kmean_row"].tensor, 0, [[0, 128], [1, 1024]])
            ctx.dma("sync", BC1[:, 0:1024], km_b, reads=["kmean_d", "BC1"], writes=["BC1"])
            for i in range(NT1):
                ctx.dma("sync", B16T[:, 0:1024], aps["q_tm"][i * 128:(i + 1) * 128, :],
                        reads=["q_tm_d"], writes=["p3q"])
                ctx.op("dve", lambda e: e.tensor_tensor(out=XA[:, 0:1024], in0=B16T[:, 0:1024],
                                                        in1=BC1[:, 0:1024], op=ALU.mult),
                       reads=["p3q", "BC1"], writes=["p3t"])
                den3 = XA[:, 0:1024].rearrange("p (g d) -> p g d", d=64)
                ctx.op("dve", lambda e, den3=den3: e.tensor_reduce(out=SMALL[:, 24:40], in_=den3,
                                                                   axis=AX.X, op=ALU.add),
                       reads=["p3t"], writes=["p3den"])
                ctx.op("dve", lambda e: e.tensor_tensor(out=SMALL[:, 40:48], in0=SMALL[:, 24:32],
                                                        in1=SMALL[:, 32:40], op=ALU.add),
                       reads=["p3den"], writes=["p3den"])
                ctx.op("dve", lambda e: e.tensor_scalar(out=SMALL[:, 40:48], in0=SMALL[:, 40:48],
                                                        scalar1=1e-6, scalar2=None, op0=ALU.add),
                       reads=["p3den"], writes=["p3den"])
                ctx.op("dve", lambda e: e.reciprocal(out=SMALL[:, 48:56], in_=SMALL[:, 40:48]),
                       reads=["p3den"], writes=["p3z"])
                ctx.op("pe", lambda e: e.transpose(PSZ[0:8, :], SMALL[:, 48:56], IDT[:]),
                       reads=["p3z", "IDT", "p3zr"], writes=["PSZ"])
                ctx.op("dve", lambda e, i=i: e.tensor_copy(out=PP[64:72, (i % 4) * 128:(i % 4 + 1) * 128],
                                                           in_=PSZ[0:8, :]),
                       reads=["PSZ"], writes=["p3zr"])
                if i % 4 == 3:
                    st = i // 4
                    ctx.dma("pool", aps["z_rows"][:, st * 512:(st + 1) * 512], PP[64:72, 0:512],
                            reads=["p3zr"], writes=["z_rows_d"])
            for st in range(N // 512):
                n0 = st * 512
                for c in range(8):
                    dst = B16S[:, c * 576:(c + 1) * 576]
                    ctx.dmat("sync", dst[:, 32:544],
                             aps["xn_tm"][n0:n0 + 512, c * 128:(c + 1) * 128],
                             reads=["xn_tm_d"], writes=["p3xh"])
                    if st > 0:
                        ctx.dmat("sync", B16T[:, 1536:1664],
                                 aps["xn_tm"][n0 - 128:n0, c * 128:(c + 1) * 128],
                                 reads=["xn_tm_d"], writes=["p3edge"])
                        ctx.op("dve", lambda e, dst=dst: e.tensor_copy(out=dst[:, 31:32],
                                                                      in_=B16T[:, 1663:1664]),
                               reads=["p3edge"], writes=["p3xh"])
                    else:
                        ctx.op("dve", lambda e, dst=dst: e.memset(dst[:, 31:32], 0.0),
                               writes=["p3xh"])
                    if st < N // 512 - 1:
                        ctx.dmat("sync", B16T[:, 1536:1664],
                                 aps["xn_tm"][n0 + 512:n0 + 640, c * 128:(c + 1) * 128],
                                 reads=["xn_tm_d"], writes=["p3edge"])
                        ctx.op("dve", lambda e, dst=dst: e.tensor_copy(out=dst[:, 544:545],
                                                                      in_=B16T[:, 1536:1537]),
                               reads=["p3edge"], writes=["p3xh"])
                    else:
                        ctx.op("dve", lambda e, dst=dst: e.memset(dst[:, 544:545], 0.0),
                               writes=["p3xh"])
                for h in range(8):
                    psl = PSL0 if h % 2 == 0 else PSL1
                    plbuf = "PSL%d" % (h % 2)
                    def mml(e, h=h, psl=psl):
                        ins = None
                        for w_ in range(3):
                            for kc in range(8):
                                ins = e.matmul(psl[:],
                                               W2[:, (w_ * 8 + kc) * 1024 + h * 128:(w_ * 8 + kc) * 1024 + (h + 1) * 128],
                                               B16S[:, kc * 576 + 31 + w_: kc * 576 + 31 + w_ + 512],
                                               start=(w_ == 0 and kc == 0), stop=(w_ == 2 and kc == 7))
                        return ins
                    ctx.op("pe", mml, reads=["w2", "p3xh"], writes=[plbuf])
                    ctx.dmat("sync", B16T[:, 1024:1536],
                             aps["qr_tm"][n0:n0 + 512, h * 128:(h + 1) * 128],
                             reads=["qr_tm_d"], writes=["p3qrT"])
                    def mma(e, h=h):
                        return e.matmul(PSA[:], KVZ[:, h * 128:(h + 1) * 128], B16T[:, 1024:1536],
                                        start=True, stop=True)
                    ctx.op("pe", mma, reads=["KVZ", "p3qrT"], writes=["PSA"])
                    zb = bass.AP(aps["z_rows"].tensor, h * N + n0, [[0, 128], [1, 512]])
                    ctx.dma("sync", XD[:, 0:512], zb, reads=["z_rows_d"], writes=["p3zb"])
                    ctx.op("dve", lambda e: e.tensor_tensor(out=XD[:, 512:1024], in0=PSA[:],
                                                            in1=XD[:, 0:512], op=ALU.mult),
                           reads=["PSA", "p3zb"], writes=["p3oz"])
                    ctx.op("dve", lambda e, h=h, psl=psl: e.scalar_tensor_tensor(
                        out=XD[:, 1024:1536], in0=psl[:], scalar=CONST[:, 144 + h:145 + h],
                        in1=XD[:, 512:1024], op0=ALU.add, op1=ALU.add),
                           reads=[plbuf, "p3oz", "CONST"], writes=["p3s0"])
                    ctx.dma("pool", aps["s0_f"][h * 128:(h + 1) * 128, n0:n0 + 512],
                            XD[:, 1024:1536], reads=["p3s0"], writes=["s0_d"])
        ctx.barrier()

    def _ph_P3bma():
        with (nc.sbuf_tensor(_u("W1"), [128, 22528], BF16) as W1,
              nc.sbuf_tensor(_u("W2"), [128, 22528], BF16) as W2,
              nc.sbuf_tensor(_u("W3"), [128, 2816], BF16) as W3,
              nc.sbuf_tensor(_u("XA"), [128, 2816], F32) as XA,
              nc.sbuf_tensor(_u("XC"), [128, 4096], F32) as XC,
              nc.sbuf_tensor(_u("XD"), [128, 1536], F32) as XD,
              nc.sbuf_tensor(_u("XB"), [128, 8448], F32) as XB,
              nc.sbuf_tensor(_u("B16A"), [128, 2816], BF16) as B16A,
              nc.sbuf_tensor(_u("B16B"), [128, 11264], BF16) as B16B,
              nc.psum_tensor(_u("PSG0"), [128, 512], F32) as PSG0,
              nc.psum_tensor(_u("PSG1"), [128, 512], F32) as PSG1,
              nc.psum_tensor(_u("PSG2"), [128, 512], F32) as PSG2,
              nc.psum_tensor(_u("PSG3"), [128, 512], F32) as PSG3,
              nc.psum_tensor(_u("PSG4"), [128, 512], F32) as PSG4,
              nc.psum_tensor(_u("PSG5"), [128, 512], F32) as PSG5,
              nc.psum_tensor(_u("PSD0"), [128, 512], F32) as PSD0,
              nc.psum_tensor(_u("PSD1"), [128, 512], F32) as PSD1):
            PSG = [PSG0, PSG1, PSG2, PSG3, PSG4, PSG5]
            for kc in (range(8) if int(os.environ.get("P3B_LVL", "9")) >= 1 else []):
                ctx.dma("sync", W1[:, kc * 2816:(kc + 1) * 2816],
                        aps["wgqT_d"][kc * 128:(kc + 1) * 128, :],
                        reads=["wgqT_d_d"], writes=["p3w1"])
                ctx.dma("sync", W2[:, kc * 2816:(kc + 1) * 2816],
                        aps["wuqT_d"][kc * 128:(kc + 1) * 128, :],
                        reads=["wuqT_d_d"], writes=["p3w2"])
            for st in range(N // 512):
                n0 = st * 512
                for c in range(8):
                    ctx.dmat("sync", B16B[:, c * 512:(c + 1) * 512],
                             aps["xq_tm"][n0:n0 + 512, c * 128:(c + 1) * 128],
                             reads=["xq_tm_d"], writes=["p3xqT"])
                for ns in range(4):
                    mgs = [(0, 512), (512, 1024), (1024, 1536), (1536, 2048), (2048, 2560), (2560, 2816)]
                    for W_, tag in ((W1, "p3w1"), (W2, "p3w2")):
                        def mmg(e, W_=W_, ns=ns):
                            ins = None
                            for gi, (a, b) in enumerate(mgs):
                                for k in range(8):
                                    ins = e.matmul(PSG[gi][:, 0:b - a],
                                                   B16B[:, k * 512 + ns * 128: k * 512 + ns * 128 + 128],
                                                   W_[:, k * 2816 + a: k * 2816 + b],
                                                   start=(k == 0), stop=(k == 7))
                            return ins
                        if W_ is W1:
                            ctx.op("pe", mmg, reads=["p3xqT", tag], writes=["PSGx"])
                            for gi, (a, b) in enumerate(mgs):
                                ctx.op("act", lambda e, gi=gi, a=a, b=b: e.activation(
                                    out=XA[:, a:b], in_=PSG[gi][:, 0:b - a], func=AF.Silu),
                                       reads=["PSGx"], writes=["p3sg"])
                        else:
                            ctx.op("pe", mmg, reads=["p3xqT", tag, "p3sg"], writes=["PSGx"])
                            for gi, (a, b) in enumerate(mgs):
                                ctx.op("dve", lambda e, gi=gi, a=a, b=b: e.tensor_tensor(
                                    out=B16A[:, a:b], in0=XA[:, a:b], in1=PSG[gi][:, 0:b - a],
                                    op=ALU.mult),
                                       reads=["PSGx", "p3sg"], writes=["p3h"])
                    quant("dve", B16A[:, 0:2816], B16A[:, 0:2816], 2816, SMALL[:, 64:416],
                          XB[:, 0:2816], XB[:, 2816:5632], XB[:, 5632:8448], "p3hq",
                          ["p3h"], ["p3hqb", "p3h"])
                    ctx.dma("act", aps["hq_st"][ns * 128:(ns + 1) * 128, :], B16A[:, 0:2816],
                            reads=["p3hqb", "p3h"], writes=["hq_st_d"])
                for mc in range(22):
                    ctx.dmat("sync", B16B[:, mc * 512:(mc + 1) * 512],
                             aps["hq_st"][0:512, mc * 128:(mc + 1) * 128],
                             reads=["hq_st_d"], writes=["p3hqT", "p3xqT"])
                for dc in range(8):
                    ctx.dma("sync", W3[:, 0:2816],
                            aps["wdqT_d"][dc].rearrange("(mc p) d -> p mc d", p=128),
                            reads=["wdqT_d_d", "p3dw"], writes=["p3w3"])
                    psd = PSD0 if dc % 2 == 0 else PSD1
                    pdb = "PSD%d" % (dc % 2)
                    def mmd(e, psd=psd):
                        ins = None
                        for mc in range(22):
                            ins = e.matmul(psd[:], W3[:, mc * 128:(mc + 1) * 128],
                                           B16B[:, mc * 512:(mc + 1) * 512],
                                           start=(mc == 0), stop=(mc == 21))
                        return ins
                    ctx.op("pe", mmd, reads=["p3w3", "p3hqT"], writes=[pdb])
                    ctx.op("act", lambda e, psd=psd: e.activation(out=XD[:, 0:512], in_=psd[:],
                                                                  func=AF.Silu),
                           reads=[pdb], writes=["p3g0"])
                    ctx.dma("sync", XD[:, 512:1024], aps["s0_f"][dc * 128:(dc + 1) * 128, n0:n0 + 512],
                            reads=["s0_d"], writes=["p3s0l"])
                    ctx.op("dve", lambda e: e.tensor_tensor(out=XD[:, 1024:1536], in0=XD[:, 0:512],
                                                            in1=XD[:, 512:1024], op=ALU.mult),
                           reads=["p3g0", "p3s0l"], writes=["p3sg0"])
                    ctx.dma("sync", XD[:, 0:512], aps["x_fm"][dc * 128:(dc + 1) * 128, n0:n0 + 512],
                            writes=["p3xl", "p3g0"])
                    ctx.op("dve", lambda e, dc=dc: e.tensor_tensor(
                        out=XC[:, dc * 512:(dc + 1) * 512], in0=XD[:, 1024:1536],
                        in1=XD[:, 0:512], op=ALU.add),
                           reads=["p3sg0", "p3xl"], writes=["p3x1"])
                    ctx.dma("act", aps["x1_f"][dc * 128:(dc + 1) * 128, n0:n0 + 512],
                            XC[:, dc * 512:(dc + 1) * 512], reads=["p3x1"], writes=["x1_d"])
                def stats(src_bufname, srcap_fn, mu_off, sq_tag, sq_fn=None):
                    if sq_fn is None:
                        sq_fn = lambda dc: XB[:, dc * 512:(dc + 1) * 512]
                    def mmu(e):
                        ins = None
                        for dc in range(8):
                            ins = e.matmul(PSD0[0:1, :], ONE[:, 0:1], srcap_fn(dc),
                                           start=(dc == 0), stop=(dc == 7))
                        return ins
                    ctx.op("pe", mmu, reads=[src_bufname, "ONE", "p4mu"], writes=["PSD0"])
                    for dc in range(8):
                        ctx.op("act", lambda e, dc=dc: e.activation(
                            out=sq_fn(dc), in_=srcap_fn(dc), func=AF.Square),
                               reads=[src_bufname], writes=[sq_tag])
                    def mmsq(e):
                        ins = None
                        for dc in range(8):
                            ins = e.matmul(PSD1[0:1, :], ONE[:, 0:1],
                                           sq_fn(dc),
                                           start=(dc == 0), stop=(dc == 7))
                        return ins
                    ctx.op("pe", mmsq, reads=[sq_tag, "ONE", "p4mu"], writes=["PSD1"])
                    ctx.op("dve", lambda e: e.tensor_scalar_mul(out=XA[0:1, 0:512],
                                                                in0=PSD0[0:1, :],
                                                                scalar1=1.0 / 1024.0),
                           reads=["PSD0"], writes=["p4mu"])
                    ctx.op("dve", lambda e: e.tensor_scalar_mul(out=XA[0:1, 512:1024],
                                                                in0=PSD1[0:1, :],
                                                                scalar1=1.0 / 1024.0),
                           reads=["PSD1"], writes=["p4mu"])
                    ctx.op("dve", lambda e: e.tensor_tensor(out=XA[0:1, 1024:1536],
                                                            in0=XA[0:1, 0:512],
                                                            in1=XA[0:1, 0:512], op=ALU.mult),
                           reads=["p4mu"], writes=["p4mu"])
                    ctx.op("dve", lambda e: e.tensor_tensor(out=XA[0:1, 512:1024],
                                                            in0=XA[0:1, 512:1024],
                                                            in1=XA[0:1, 1024:1536],
                                                            op=ALU.subtract),
                           reads=["p4mu"], writes=["p4mu"])
                    ctx.op("dve", lambda e: e.tensor_scalar(out=XA[0:1, 512:1024],
                                                            in0=XA[0:1, 512:1024],
                                                            scalar1=1e-5, scalar2=None,
                                                            op0=ALU.add),
                           reads=["p4mu"], writes=["p4mu"])
                    rsqrt(XA[0:1, 1024:1536], XA[0:1, 512:1024], XA[0:1, 1536:2048],
                          XA[0:1, 2048:2560], ["p4mu"], ["p4mu"], nrows=1)
                    ctx.dma("act", aps[mu_off][0:1, n0:n0 + 512], XA[0:1, 0:512],
                            reads=["p4mu"], writes=[mu_off + "_d"])
                    ctx.dma("act", aps[mu_off][1:2, n0:n0 + 512], XA[0:1, 1024:1536],
                            reads=["p4mu"], writes=[mu_off + "_d"])
                    mb = bass.AP(aps[mu_off].tensor, n0, [[0, 128], [1, 512]])
                    rb = bass.AP(aps[mu_off].tensor, N + n0, [[0, 128], [1, 512]])
                    ctx.dma("sync", XD[:, 0:512], mb, reads=[mu_off + "_d"], writes=["p4mb"])
                    ctx.dma("sync", XD[:, 512:1024], rb, reads=[mu_off + "_d"], writes=["p4rb"])
                stats("p3x1", lambda dc: XC[:, dc * 512:(dc + 1) * 512], "mu_row", "p4sq")
                for dc in range(8):
                    ctx.op("dve", lambda e, dc=dc: e.tensor_tensor(
                        out=XB[:, dc * 512:(dc + 1) * 512], in0=XC[:, dc * 512:(dc + 1) * 512],
                        in1=XD[:, 0:512], op=ALU.subtract),
                           reads=["p3x1", "p4mb", "p4sq"], writes=["p4xn2"])
                    ctx.op("dve", lambda e, dc=dc: e.tensor_tensor(
                        out=XB[:, dc * 512:(dc + 1) * 512], in0=XB[:, dc * 512:(dc + 1) * 512],
                        in1=XD[:, 512:1024], op=ALU.mult),
                           reads=["p4xn2", "p4rb"], writes=["p4xn2"])
                    ctx.dma("act", aps["xn2_f"][dc * 128:(dc + 1) * 128, n0:n0 + 512],
                            XB[:, dc * 512:(dc + 1) * 512], reads=["p4xn2"], writes=["xn2f_d"])
                    ctx.op("dve", lambda e, dc=dc: e.tensor_copy(
                        out=B16B[:, dc * 512:(dc + 1) * 512], in_=XB[:, dc * 512:(dc + 1) * 512]),
                           reads=["p4xn2"], writes=["p4xn2b", "p3hqT", "p3xqT"])
                    ctx.dma("act", aps["xn2_bf"][dc * 128:(dc + 1) * 128, n0:n0 + 512],
                            B16B[:, dc * 512:(dc + 1) * 512], reads=["p4xn2b"], writes=["xn2b_d"])
                stats("p4xn2", lambda dc: XB[:, dc * 512:(dc + 1) * 512], "mu2_row", "p4sq2",
                      sq_fn=lambda dc: XB[:, 4096 + dc * 512:4096 + (dc + 1) * 512])
                for dc in range(8):
                    ctx.op("dve", lambda e, dc=dc: e.tensor_tensor(
                        out=XC[:, dc * 512:(dc + 1) * 512], in0=XB[:, dc * 512:(dc + 1) * 512],
                        in1=XD[:, 0:512], op=ALU.subtract),
                           reads=["p4xn2", "p4mb", "p3x1"], writes=["p4xni"])
                    ctx.op("dve", lambda e, dc=dc: e.tensor_tensor(
                        out=XC[:, dc * 512:(dc + 1) * 512], in0=XC[:, dc * 512:(dc + 1) * 512],
                        in1=XD[:, 512:1024], op=ALU.mult),
                           reads=["p4xni", "p4rb"], writes=["p4xni"])
                    ctx.op("dve", lambda e, dc=dc: e.tensor_scalar(
                        out=B16B[:, 4096 + dc * 512:4096 + (dc + 1) * 512],
                        in0=XC[:, dc * 512:(dc + 1) * 512],
                        scalar1=CONST[:, 128 + dc:129 + dc], scalar2=CONST[:, 136 + dc:137 + dc],
                        op0=ALU.mult, op1=ALU.add),
                           reads=["p4xni", "CONST", "p4xn2b"], writes=["p4xnib", "p3hqT"])
                    ctx.dma("act", aps["xni_bf"][dc * 128:(dc + 1) * 128, n0:n0 + 512],
                            B16B[:, 4096 + dc * 512:4096 + (dc + 1) * 512],
                            reads=["p4xnib"], writes=["xni_d"])
        ctx.barrier()

    def _ph_P4ba():
        with (nc.sbuf_tensor(_u("W1"), [128, 32768], BF16) as W1,
              nc.sbuf_tensor(_u("WS"), [128, 2048], BF16) as WS,
              nc.sbuf_tensor(_u("WS2"), [64, 2048], BF16) as WS2,
              nc.sbuf_tensor(_u("XA"), [128, 8192], F32) as XA,
              nc.sbuf_tensor(_u("XE"), [128, 3584], F32) as XE,
              nc.sbuf_tensor(_u("HALO"), [128, 48], F32) as HALO,
              nc.sbuf_tensor(_u("B16A"), [128, 8192], BF16) as B16A,
              nc.sbuf_tensor(_u("B16B"), [128, 8192], BF16) as B16B,
              nc.sbuf_tensor(_u("DTB"), [64, 512], BF16) as DTB,
              nc.sbuf_tensor(_u("DT16"), [128, 512], BF16) as DT16,
              nc.psum_tensor(_u("PSI0"), [128, 512], F32) as PSI0,
              nc.psum_tensor(_u("PSI1"), [128, 512], F32) as PSI1,
              nc.psum_tensor(_u("PSDT"), [64, 512], F32) as PSDT,
              nc.psum_tensor(_u("PSB"), [16, 512], F32) as PSB,
              nc.psum_tensor(_u("PSC"), [16, 512], F32) as PSC,
              nc.psum_tensor(_u("PSD2"), [128, 512], F32) as PSD2):
            for kc in range(8):
                ctx.dma("pool", W1[:, kc * 4096:(kc + 1) * 4096],
                        aps["in_w"][kc * 128:(kc + 1) * 128, :],
                        reads=["p4w1"], writes=["p4w1"])
            for kc in range(16):
                ctx.dma("pool", WS[:, kc * 96:kc * 96 + 64],
                        aps["xw_dt"][kc * 128:(kc + 1) * 128, :],
                        reads=["p4ws"], writes=["p4ws"])
                ctx.dma("pool", WS[:, kc * 96 + 64:kc * 96 + 80],
                        aps["xw_B"][kc * 128:(kc + 1) * 128, :],
                        reads=["p4ws"], writes=["p4ws"])
                ctx.dma("pool", WS[:, kc * 96 + 80:kc * 96 + 96],
                        aps["xw_C"][kc * 128:(kc + 1) * 128, :],
                        reads=["p4ws"], writes=["p4ws"])
            ctx.dma("pool", WS2[:], aps["dt_w"][:], reads=["p4ws2"], writes=["p4ws2"])
            ctx.op("dve", lambda e: e.memset(HALO[:], 0.0), writes=["p4halo"])
            for st in range(N // 512):
                n0 = st * 512
                for c in range(8):
                    ctx.dma("sync", B16A[:, c * 512:(c + 1) * 512],
                            aps["xni_bf"][c * 128:(c + 1) * 128, n0:n0 + 512],
                            reads=["xni_d"], writes=["p4xni"])
                for mc in range(32):
                    ps = PSI0 if mc % 2 == 0 else PSI1
                    pb = "PSI%d" % (mc % 2)
                    def mmi(e, mc=mc, ps=ps):
                        ins = None
                        for k in range(8):
                            ins = e.matmul(ps[:], W1[:, k * 4096 + mc * 128: k * 4096 + (mc + 1) * 128],
                                           B16A[:, k * 512:(k + 1) * 512],
                                           start=(k == 0), stop=(k == 7))
                        return ins
                    ctx.op("pe", mmi, reads=["p4w1", "p4xni"], writes=[pb])
                    if mc < 16:
                        ctx.op("dve", lambda e, mc=mc, ps=ps: e.tensor_copy(
                            out=XA[:, mc * 512:(mc + 1) * 512], in_=ps[:]),
                               reads=[pb], writes=["p4xp", "p4st"])
                    else:
                        ctx.op("act", lambda e, mc=mc, ps=ps: e.activation(
                            out=B16B[:, (mc - 16) * 512:(mc - 15) * 512], in_=ps[:], func=AF.Silu),
                               reads=[pb], writes=["p4sz"])
                for c in range(16):
                    ctx.dma("act", aps["siluz"][c * 128:(c + 1) * 128, n0:n0 + 512],
                            B16B[:, c * 512:(c + 1) * 512], reads=["p4sz"], writes=["siluz_d"])
                for c in range(16):
                    ext = XE[:, 0:515]
                    ctx.op("dve", lambda e, c=c: e.tensor_copy(out=XE[:, 0:3],
                                                               in_=HALO[:, c * 3:(c + 1) * 3]),
                           reads=["p4halo"], writes=["p4ext", "p4B", "p4C"])
                    ctx.op("dve", lambda e, c=c: e.tensor_copy(out=XE[:, 3:515],
                                                               in_=XA[:, c * 512:(c + 1) * 512]),
                           reads=["p4xp"], writes=["p4ext"])
                    ctx.op("dve", lambda e, c=c: e.tensor_copy(out=HALO[:, c * 3:(c + 1) * 3],
                                                               in_=XA[:, c * 512 + 509:c * 512 + 512]),
                           reads=["p4xp", "p4ext"], writes=["p4halo"])
                    ctx.op("dve", lambda e, c=c: e.tensor_scalar(out=XE[:, 515:1027],
                                                                 in0=XE[:, 0:512],
                                                                 scalar1=CONST[:, c * 4:c * 4 + 1],
                                                                 scalar2=None, op0=ALU.mult),
                           reads=["p4ext", "CONST"], writes=["p4cv"])
                    for kk in range(1, 4):
                        ctx.op("dve", lambda e, c=c, kk=kk: e.scalar_tensor_tensor(
                            out=XE[:, 515:1027], in0=XE[:, kk:kk + 512],
                            scalar=CONST[:, c * 4 + kk:c * 4 + kk + 1],
                            in1=XE[:, 515:1027], op0=ALU.mult, op1=ALU.add),
                               reads=["p4ext", "CONST", "p4cv"], writes=["p4cv"])
                    ctx.op("act", lambda e, c=c: e.activation(out=XE[:, 1027:1539],
                                                              in_=XE[:, 515:1027], func=AF.Silu,
                                                              bias=CONST[:, 64 + c:65 + c]),
                           reads=["p4cv", "CONST"], writes=["p4xc"])
                    ctx.dma("act", aps["xc_f"][c * 128:(c + 1) * 128, n0:n0 + 512],
                            XE[:, 1027:1539], reads=["p4xc"], writes=["xc_d"])
                    ctx.op("dve", lambda e, c=c: e.tensor_copy(
                        out=B16A[:, c * 512:(c + 1) * 512], in_=XE[:, 1027:1539]),
                           reads=["p4xc"], writes=["p4xcb", "p4xni"])
                def mmx(e):
                    ins = None
                    for k in range(16):
                        ins = e.matmul(PSDT[:], WS[:, k * 96:k * 96 + 64],
                                       B16A[:, k * 512:(k + 1) * 512],
                                       start=(k == 0), stop=(k == 15))
                    for k in range(16):
                        ins = e.matmul(PSB[:], WS[:, k * 96 + 64:k * 96 + 80],
                                       B16A[:, k * 512:(k + 1) * 512],
                                       start=(k == 0), stop=(k == 15))
                    for k in range(16):
                        ins = e.matmul(PSC[:], WS[:, k * 96 + 80:k * 96 + 96],
                                       B16A[:, k * 512:(k + 1) * 512],
                                       start=(k == 0), stop=(k == 15))
                    return ins
                ctx.op("pe", mmx, reads=["p4ws", "p4xcb", "p4xni"], writes=["PSX"])
                ctx.op("dve", lambda e: e.tensor_copy(out=XE[0:16, 0:512], in_=PSB[:]),
                       reads=["PSX"], writes=["p4B"])
                ctx.dma("act", aps["B_d"][:, n0:n0 + 512], XE[0:16, 0:512],
                        reads=["p4B"], writes=["B_dd"])
                ctx.op("dve", lambda e: e.tensor_copy(out=XE[0:16, 512:1024], in_=PSC[:]),
                       reads=["PSX", "p4B"], writes=["p4C"])
                ctx.dma("act", aps["C_d"][:, n0:n0 + 512], XE[0:16, 512:1024],
                        reads=["p4C"], writes=["C_dd"])
                ctx.op("dve", lambda e: e.tensor_copy(out=DTB[:], in_=PSDT[:]),
                       reads=["PSX"], writes=["p4dtb"])
                for c in range(16):
                    def mmdt(e, c=c):
                        return e.matmul(PSD2[:], WS2[:, c * 128:(c + 1) * 128], DTB[:],
                                        start=True, stop=True)
                    ctx.op("pe", mmdt, reads=["p4ws2", "p4dtb"], writes=["PSDc"])
                    ctx.op("dve", lambda e, c=c: e.tensor_scalar(
                        out=XE[:, 1664:2176], in0=PSD2[:], scalar1=CONST[:, 80 + c:81 + c],
                        scalar2=85.0, op0=ALU.add, op1=ALU.min),
                           reads=["PSDc", "CONST"], writes=["p4dta", "p4dt"])
                    ctx.op("act", lambda e: e.activation(out=XE[:, 2176:2688], in_=XE[:, 1664:2176],
                                                         func=AF.Exp),
                           reads=["p4dta"], writes=["p4dte"])
                    ctx.op("act", lambda e: e.activation(out=XE[:, 1664:2176], in_=XE[:, 2176:2688],
                                                         func=AF.Ln, bias=1.0),
                           reads=["p4dte", "p4dta"], writes=["p4dt"])
                    ctx.dma("act", aps["dt_f"][c * 128:(c + 1) * 128, n0:n0 + 512],
                            XE[:, 1664:2176], reads=["p4dt"], writes=["dt_dd"])
                    ctx.op("dve", lambda e, c=c: e.tensor_tensor(
                        out=DT16[:, 0:512], in0=XE[:, 1664:2176],
                        in1=B16A[:, c * 512:(c + 1) * 512], op=ALU.mult),
                           reads=["p4dt", "p4xcb", "p4dtu", "p4xni"], writes=["p4dtu"])
                    ctx.dma("act", aps["dtu_f"][c * 128:(c + 1) * 128, n0:n0 + 512],
                            DT16[:, 0:512], reads=["p4dtu"], writes=["dtu_dd"])
        ctx.barrier()

    def _ph_P4ca():
        with (nc.sbuf_tensor(_u("WSR"), [128, 64], BF16) as WSR,
              nc.sbuf_tensor(_u("XA"), [128, 1024], F32) as XA,
              nc.sbuf_tensor(_u("B16A"), [128, 4096], BF16) as B16A,
              nc.psum_tensor(_u("PSR"), [8, 512], F32) as PSR,
              nc.psum_tensor(_u("PSZ2"), [128, 8], F32) as PSZ2,
              nc.psum_tensor(_u("PSZ3"), [8, 128], F32) as PSZ3):
            for kc in range(8):
                ctx.dma("pool", WSR[:, kc * 8:(kc + 1) * 8],
                        aps["Wr"][kc * 128:(kc + 1) * 128, :],
                        reads=["p4wr"], writes=["p4wr"])
            for st in range(N // 512):
                n0 = st * 512
                for c in range(8):
                    ctx.dma("sync", B16A[:, c * 512:(c + 1) * 512],
                            aps["xn2_bf"][c * 128:(c + 1) * 128, n0:n0 + 512],
                            reads=["xn2b_d"], writes=["p4cx"])
                def mmr(e):
                    ins = None
                    for k in range(8):
                        ins = e.matmul(PSR[:], WSR[:, k * 8:(k + 1) * 8],
                                       B16A[:, k * 512:(k + 1) * 512],
                                       start=(k == 0), stop=(k == 7))
                    return ins
                ctx.op("pe", mmr, reads=["p4wr", "p4cx"], writes=["PSR"])
                ctx.op("dve", lambda e: e.tensor_copy(out=PP[0:8, 0:512], in_=PSR[:]),
                       reads=["PSR"], writes=["p4lg"])
                for s in range(4):
                    ctx.op("pe", lambda e, s=s: e.transpose(PSZ2[:], PP[0:8, s * 128:(s + 1) * 128],
                                                            IDT[0:8, 0:8]),
                           reads=["p4lg", "IDT", "p4sm"], writes=["PSZ2b"])
                    p = XA[:, 0:8]
                    ctx.op("dve", lambda e: e.tensor_copy(out=p, in_=PSZ2[:]),
                           reads=["PSZ2b"], writes=["p4sm"])
                    ctx.op("dve", lambda e: e.reduce_max(out=SMALL[:, 0:1], in_=p, axis=AX.X),
                           reads=["p4sm"], writes=["SMALL"])
                    ctx.op("dve", lambda e: e.tensor_scalar(out=p, in0=p, scalar1=SMALL[:, 0:1],
                                                            scalar2=None, op0=ALU.subtract),
                           reads=["p4sm", "SMALL"], writes=["p4sm"])
                    ctx.op("act", lambda e: e.activation(out=p, in_=p, func=AF.Exp),
                           reads=["p4sm"], writes=["p4sm"])
                    ctx.op("dve", lambda e: e.reduce_sum(out=SMALL[:, 0:1], in_=p, axis=AX.X),
                           reads=["p4sm"], writes=["SMALL"])
                    ctx.op("dve", lambda e: e.reciprocal(out=SMALL[:, 1:2], in_=SMALL[:, 0:1]),
                           reads=["SMALL"], writes=["SMALL"])
                    ctx.op("dve", lambda e: e.tensor_scalar(out=p, in0=p, scalar1=SMALL[:, 1:2],
                                                            scalar2=None, op0=ALU.mult),
                           reads=["p4sm", "SMALL"], writes=["p4sm"])
                    ctx.op("dve", lambda e: e.reduce_max(out=SMALL[:, 2:3], in_=p, axis=AX.X),
                           reads=["p4sm"], writes=["SMALL"])
                    eq1 = XA[:, 8:16]
                    ctx.op("dve", lambda e, eq1=eq1: e.tensor_scalar(out=eq1, in0=p,
                                                                     scalar1=SMALL[:, 2:3],
                                                                     scalar2=None, op0=ALU.is_ge),
                           reads=["p4sm", "SMALL"], writes=["p4eq1"])
                    pm = XA[:, 16:24]
                    ctx.op("dve", lambda e, pm=pm, eq1=eq1: e.tensor_tensor(out=pm, in0=p, in1=eq1,
                                                                            op=ALU.mult),
                           reads=["p4sm", "p4eq1"], writes=["p4pm"])
                    ctx.op("dve", lambda e, pm=pm: e.tensor_tensor(out=pm, in0=p, in1=pm,
                                                                   op=ALU.subtract),
                           reads=["p4sm", "p4pm"], writes=["p4pm"])
                    ctx.op("dve", lambda e, pm=pm: e.reduce_max(out=SMALL[:, 3:4], in_=pm,
                                                                axis=AX.X),
                           reads=["p4pm"], writes=["SMALL"])
                    ctx.op("dve", lambda e: e.tensor_tensor(out=SMALL[:, 4:5], in0=SMALL[:, 2:3],
                                                            in1=SMALL[:, 3:4], op=ALU.add),
                           reads=["SMALL"], writes=["SMALL"])
                    ctx.op("dve", lambda e: e.tensor_scalar(out=SMALL[:, 4:5], in0=SMALL[:, 4:5],
                                                            scalar1=1e-8, scalar2=None,
                                                            op0=ALU.add),
                           reads=["SMALL"], writes=["SMALL"])
                    ctx.op("dve", lambda e: e.reciprocal(out=SMALL[:, 5:6], in_=SMALL[:, 4:5]),
                           reads=["SMALL"], writes=["SMALL"])
                    ctx.op("dve", lambda e: e.tensor_tensor(out=SMALL[:, 6:7], in0=SMALL[:, 2:3],
                                                            in1=SMALL[:, 5:6], op=ALU.mult),
                           reads=["SMALL"], writes=["SMALL"])
                    ctx.op("dve", lambda e: e.tensor_tensor(out=SMALL[:, 7:8], in0=SMALL[:, 3:4],
                                                            in1=SMALL[:, 5:6], op=ALU.mult),
                           reads=["SMALL"], writes=["SMALL"])
                    eq2 = XA[:, 24:32]
                    ctx.op("dve", lambda e, eq2=eq2, pm=pm: e.tensor_scalar(
                        out=eq2, in0=pm, scalar1=SMALL[:, 3:4], scalar2=None, op0=ALU.is_ge),
                           reads=["p4pm", "SMALL"], writes=["p4eq2"])
                    ew = XA[:, 32:40]
                    ctx.op("dve", lambda e, ew=ew, eq1=eq1: e.tensor_scalar(
                        out=ew, in0=eq1, scalar1=SMALL[:, 6:7], scalar2=None, op0=ALU.mult),
                           reads=["p4eq1", "SMALL"], writes=["p4ew"])
                    ctx.op("dve", lambda e, ew=ew, eq2=eq2: e.scalar_tensor_tensor(
                        out=ew, in0=eq2, scalar=SMALL[:, 7:8], in1=ew, op0=ALU.mult, op1=ALU.add),
                           reads=["p4eq2", "SMALL", "p4ew"], writes=["p4ew"])
                    ctx.op("pe", lambda e, ew=ew: e.transpose(PSZ3[:], ew, IDT[:]),
                           reads=["p4ew", "IDT", "p4ewr"], writes=["PSZ3b"])
                    ctx.op("dve", lambda e, s=s: e.tensor_copy(out=PP[64:72, s * 128:(s + 1) * 128],
                                                               in_=PSZ3[:]),
                           reads=["PSZ3b"], writes=["p4ewr"])
                ctx.dma("act", aps["ew_rows"][:, n0:n0 + 512], PP[64:72, 0:512],
                        reads=["p4ewr"], writes=["ew_d"])
        ctx.barrier()

    def _ph_P67():
        GU = NE * D * ME
        NTC = N // TC
        with contextlib.ExitStack() as _st:
            _a = lambda nm, shape, dt: _st.enter_context(nc.sbuf_tensor(_u(nm), shape, dt))
            _p = lambda nm, shape: _st.enter_context(nc.psum_tensor(_u(nm), shape, F32))
            XA = _a("XA", [128, 4096], F32)
            XB = _a("XB", [128, 4096], BF16)
            XCB = _a("XCB", [128, 12288], BF16)
            XD = _a("XD", [128, 1536], F32)
            DTB = _a("DTB", [128, 512], BF16)
            SCB = _a("SCB", [128, 1024], BF16)
            ASB = _a("ASB", [128, 256], F32)
            HC = _a("HC", [128, 256], F32)
            W1 = _a("W1", [128, 16384], BF16)
            W2 = _a("W2", [128, 16384], BF16)
            H16 = _a("H16", [128, 8192], BF16)
            X16 = _a("X16", [128, 8192], BF16)
            XF = _a("XF", [128, 4608], F32)
            EWBP = [_a("EWB%d" % i, [128, 512], F32) for i in range(2)]
            EWB16 = _a("EWB16", [8, 512], BF16)
            BOE = _a("BOE", [8, 1024], BF16)
            PSGL = [_p("PSG%d" % i, [128, 512]) for i in range(3)]
            PSUL = [_p("PSU%d" % i, [128, 512]) for i in range(3)]
            PSOL = [_p("PSO%d" % i, [128, 512]) for i in range(2)]
            WSLOT = [(W1, 0), (W1, 8192), (W2, 0), (W2, 8192)]
            for dc in range(16):
                ctx.dma("sync", SMALL[:, 0:16], aps["A_log"][dc * 128:(dc + 1) * 128, :],
                        reads=["p6al"], writes=["p6al"])
                ctx.op("act", lambda e, dc=dc: e.activation(out=ASB[:, dc * 16:(dc + 1) * 16],
                                                            in_=SMALL[:, 0:16], func=AF.Exp),
                       reads=["p6al"], writes=["ASB"])
            ctx.op("dve", lambda e: e.tensor_scalar_mul(out=ASB[:], in0=ASB[:], scalar1=-1.0),
                   reads=["ASB"], writes=["ASB"])
            ctx.op("dve", lambda e: e.memset(HC[:], 0.0), writes=["HC"])
            ctx.dma("pool", BOE[:], aps["bo_e"][:], writes=["p7boe"])

            def su_prolog(tc):
                t0 = tc * TC
                bb = bass.AP(aps["B_d"].tensor, t0, [[0, 128], [N, DS], [1, TC]])
                cb = bass.AP(aps["C_d"].tensor, t0, [[0, 128], [N, DS], [1, TC]])
                ctx.dma("pool", XCB[:, 0:DS * TC].rearrange("p (n t) -> p n t", t=TC), bb,
                        reads=["B_dd"], writes=["p6Bb"])
                ctx.dma("pool", XCB[:, DS * TC:2 * DS * TC].rearrange("p (n t) -> p n t", t=TC),
                        cb, reads=["C_dd"], writes=["p6Cb"])

            def su_dc(tc, dc):
                t0 = tc * TC
                r0 = dc * 128
                par = dc % 2
                xo = par * 3 * TC
                so = par * 2 * TC
                dtt = "p6dt%d" % par
                dut = "p6dtu%d" % par
                ut = "p6u%d" % par
                szt = "p6sz%d" % par
                units = []

                def u_load():
                    ctx.dma("sync", XD[:, xo:xo + TC], aps["dt_f"][r0:r0 + 128, t0:t0 + TC],
                            reads=["dt_dd"], writes=[dtt])
                    ctx.dma("sync", DTB[:, par * TC:(par + 1) * TC],
                            aps["dtu_f"][r0:r0 + 128, t0:t0 + TC],
                            reads=["dtu_dd"], writes=[dut])
                    ctx.dma("sync", XD[:, xo + TC:xo + 2 * TC],
                            aps["xc_f"][r0:r0 + 128, t0:t0 + TC],
                            reads=["xc_d"], writes=[ut])
                    ctx.dma("sync", SCB[:, so:so + TC], aps["siluz"][r0:r0 + 128, t0:t0 + TC],
                            reads=["siluz_d"], writes=[szt])
                units.append(u_load)

                def mk_exp(g):
                    extra = {0: ["p6t0"], 1: ["p6t0"], 2: ["p6t1"], 3: ["p6t2", "p6t3"]}[g]
                    def u_exp():
                        def fn(e, g=g, dc=dc):
                            ins = None
                            for n_ in range(4 * g, 4 * g + 4):
                                ins = e.activation(
                                    out=XA[:, n_ * TC:(n_ + 1) * TC], in_=XD[:, xo:xo + TC],
                                    func=AF.Exp,
                                    scale=ASB[:, dc * 16 + n_:dc * 16 + n_ + 1])
                            return ins
                        ctx.op("act", fn, reads=[dtt, "ASB"],
                               writes=["p6dA%d" % g] + extra)
                    return u_exp
                units.append(mk_exp(0))
                units.append(mk_exp(1))

                def u_dBu():
                    dtu_b = DTB[:, par * TC:(par + 1) * TC].unsqueeze(1).broadcast_to(
                        (128, DS, TC))
                    ctx.op("pool", lambda e, dtu_b=dtu_b: e.tensor_tensor(
                        out=XCB[:, 2 * DS * TC:3 * DS * TC].rearrange("p (n t) -> p n t", t=TC),
                        in0=dtu_b, in1=XCB[:, 0:DS * TC].rearrange("p (n t) -> p n t", t=TC),
                        op=ALU.mult),
                           reads=[dut, "p6Bb"], writes=["p6dBu"])
                units.append(u_dBu)
                units.append(mk_exp(2))
                units.append(mk_exp(3))

                def mk_scan(g, hh):
                    def u_scan():
                        def fn(e, g=g, hh=hh, dc=dc):
                            ins = None
                            for n_ in range(4 * g + 2 * hh, 4 * g + 2 * hh + 2):
                                ins = e.tensor_tensor_scan(
                                    out=XB[:, n_ * TC:(n_ + 1) * TC],
                                    data0=XA[:, n_ * TC:(n_ + 1) * TC],
                                    data1=XCB[:, 2 * DS * TC + n_ * TC:
                                              2 * DS * TC + (n_ + 1) * TC],
                                    initial=HC[:, dc * 16 + n_:dc * 16 + n_ + 1],
                                    op0=ALU.mult, op1=ALU.add)
                            return ins
                        ctx.op("dve", fn, reads=["p6dA%d" % g, "p6dBu", "HC"],
                               writes=["p6hs%d" % g])
                    return u_scan
                for g in range(4):
                    units.append(mk_scan(g, 0))
                    units.append(mk_scan(g, 1))

                def u_hc():
                    hs3 = XB[:, 0:DS * TC].rearrange("p (n t) -> p n t", t=TC)
                    ctx.op("act", lambda e, hs3=hs3, dc=dc: e.copy(
                        out=HC[:, dc * 16:(dc + 1) * 16], in_=hs3[:, :, TC - 1]),
                           reads=["p6hs0", "p6hs1", "p6hs2", "p6hs3"], writes=["HC"])
                    ctx.op("pool", lambda e: e.tensor_tensor(
                        out=XCB[:, 2 * DS * TC:3 * DS * TC],
                        in0=XB[:, 0:DS * TC],
                        in1=XCB[:, DS * TC:2 * DS * TC], op=ALU.mult),
                           reads=["p6hs0", "p6hs1", "p6hs2", "p6hs3", "p6Cb"],
                           writes=["p6hc", "p6dBu"])
                units.append(u_hc)

                def u_l01():
                    ctx.op("pool", lambda e: e.tensor_tensor(
                        out=XA[:, 0:8 * TC],
                        in0=XCB[:, 2 * DS * TC:2 * DS * TC + 8 * TC],
                        in1=XCB[:, 2 * DS * TC + 8 * TC:2 * DS * TC + 16 * TC],
                        op=ALU.add),
                           reads=["p6hc"], writes=["p6t0", "p6dA0", "p6dA1"])
                    ctx.op("pool", lambda e: e.tensor_tensor(
                        out=XA[:, 8 * TC:12 * TC],
                        in0=XA[:, 0:4 * TC], in1=XA[:, 4 * TC:8 * TC], op=ALU.add),
                           reads=["p6t0"], writes=["p6t1", "p6dA2"])
                units.append(u_l01)

                def u_l23y():
                    ctx.op("dve", lambda e: e.tensor_tensor(
                        out=XA[:, 12 * TC:14 * TC],
                        in0=XA[:, 8 * TC:10 * TC], in1=XA[:, 10 * TC:12 * TC], op=ALU.add),
                           reads=["p6t1"], writes=["p6t2", "p6dA3"])
                    ctx.op("dve", lambda e: e.tensor_tensor(
                        out=XA[:, 14 * TC:15 * TC],
                        in0=XA[:, 12 * TC:13 * TC], in1=XA[:, 13 * TC:14 * TC], op=ALU.add),
                           reads=["p6t2"], writes=["p6t3"])
                    ctx.op("dve", lambda e, dc=dc: e.scalar_tensor_tensor(
                        out=XD[:, xo + 2 * TC:xo + 3 * TC], in0=XD[:, xo + TC:xo + 2 * TC],
                        scalar=CONST[:, 96 + dc:97 + dc], in1=XA[:, 14 * TC:15 * TC],
                        op0=ALU.mult, op1=ALU.add),
                           reads=[ut, "CONST", "p6t3"], writes=["p6y%d" % par])
                units.append(u_l23y)

                def u_y3():
                    ctx.op("pool", lambda e: e.tensor_tensor(
                        out=SCB[:, so + TC:so + 2 * TC],
                        in0=XD[:, xo + 2 * TC:xo + 3 * TC],
                        in1=SCB[:, so:so + TC], op=ALU.mult),
                           reads=["p6y%d" % par, szt], writes=["p6y3%d" % par])
                    ctx.dma("pool", aps["y3"][r0:r0 + 128, t0:t0 + TC],
                            SCB[:, so + TC:so + 2 * TC],
                            reads=["p6y3%d" % par], writes=["y3_d"])
                units.append(u_y3)
                return units

            def mu_quantum(nt, e_):
                n0 = nt * 512
                qp = (nt * NE + e_) % 2
                xp = nt % 2
                xoff = xp * 4096
                xt = "p7x%d" % xp
                par = e_ % 2
                ewo = par * 512
                boff = 32 + par * 32
                ewt = "p7ewb%d" % par
                bgt = "p7bg%d" % par
                s_a = 2 * qp
                s_b = 2 * qp + 1
                s_c = 2 - 2 * qp
                s_d = 3 - 2 * qp
                stag = lambda s: "WS%d" % s

                def wview(s):
                    t, off = WSLOT[s]
                    return t[:, off:off + 8192].rearrange("p (k m) -> p k m", m=1024)

                def wsl(s, k, m0, m1):
                    t, off = WSLOT[s]
                    return t[:, off + k * 1024 + m0:off + k * 1024 + m1]
                units = []

                def u_a():
                    if e_ == 0:
                        for c in range(8):
                            ctx.dma("sync", X16[:, xoff + c * 512:xoff + (c + 1) * 512],
                                    aps["xn2_bf"][c * 128:(c + 1) * 128, n0:n0 + 512],
                                    reads=["xn2b_d"], writes=[xt])
                        ctx.dma("pool", EWB16[:], aps["ew_rows"][:, n0:n0 + 512],
                                reads=["p7ew16"], writes=["p7ew16"])
                    ewb_src = bass.AP(aps["ew_rows"].tensor, e_ * N + n0, [[0, 128], [1, 512]])
                    ctx.dma("sync", EWBP[par][:], ewb_src,
                            reads=["ew_d"], writes=[ewt])
                    ctx.dma("sync", SMALL[:, boff:boff + 16], aps["bg_e"][e_],
                            reads=[bgt], writes=[bgt])
                    ctx.dma("sync", SMALL[:, boff + 16:boff + 32], aps["bu_e"][e_],
                            reads=[bgt], writes=[bgt])
                units.append(u_a)

                def mk_w(half):
                    sg = s_a if half == 0 else s_c
                    su_ = s_b if half == 0 else s_d

                    def u_w():
                        gsrc = bass.AP(aps["wbf"].tensor, e_ * D * ME + half * 1024,
                                       [[ME, 128], [128 * ME, 8], [1, 1024]])
                        ctx.dma("sync", wview(sg), gsrc, reads=["wbf_d"], writes=[stag(sg)])
                        usrc = bass.AP(aps["wbf"].tensor, GU + e_ * D * ME + half * 1024,
                                       [[ME, 128], [128 * ME, 8], [1, 1024]])
                        ctx.dma("act", wview(su_), usrc, reads=["wbf_d"], writes=[stag(su_)])
                    return u_w
                units.append(mk_w(0))
                units.append(mk_w(1))

                def mk_gu(half, mc8):
                    sg = s_a if half == 0 else s_c
                    su_ = s_b if half == 0 else s_d
                    mcg = half * 8 + mc8
                    psg = PSGL[mcg % 3]
                    psu = PSUL[mcg % 3]
                    gb = "PSG%d" % (mcg % 3)
                    ub = "PSU%d" % (mcg % 3)
                    ht = "p7h%d" % half

                    def u_gu():
                        def mmg(e, mc8=mc8, psg=psg, sg=sg):
                            ins = None
                            for k in range(8):
                                ins = e.matmul(psg[:], wsl(sg, k, mc8 * 128, (mc8 + 1) * 128),
                                               X16[:, xoff + k * 512:xoff + (k + 1) * 512],
                                               start=(k == 0), stop=(k == 7))
                            return ins
                        ctx.op("pe", mmg, reads=[stag(sg), xt], writes=[gb])
                        ctx.op("act", lambda e, mcg=mcg, psg=psg: e.activation(
                            out=H16[:, mcg * 512:mcg * 512 + 512], in_=psg[:], func=AF.Silu,
                            bias=SMALL[:, boff + mcg:boff + mcg + 1]),
                               reads=[gb, bgt], writes=[ht])
                        def mmu(e, mc8=mc8, psu=psu, su_=su_):
                            ins = None
                            for k in range(8):
                                ins = e.matmul(psu[:], wsl(su_, k, mc8 * 128, (mc8 + 1) * 128),
                                               X16[:, xoff + k * 512:xoff + (k + 1) * 512],
                                               start=(k == 0), stop=(k == 7))
                            return ins
                        ctx.op("pe", mmu, reads=[stag(su_), xt], writes=[ub])
                        ctx.op("dve", lambda e, mcg=mcg, psu=psu: e.scalar_tensor_tensor(
                            out=XF[:, 4096:4608], in0=psu[:],
                            scalar=SMALL[:, boff + 16 + mcg:boff + 17 + mcg],
                            in1=EWBP[par][:],
                            op0=ALU.add, op1=ALU.mult),
                               reads=[ub, bgt, ewt], writes=["p7tmp"])
                        ctx.op("dve", lambda e, mcg=mcg: e.tensor_tensor(
                            out=H16[:, mcg * 512:mcg * 512 + 512], in0=XF[:, 4096:4608],
                            in1=H16[:, mcg * 512:mcg * 512 + 512], op=ALU.mult),
                               reads=["p7tmp", ht], writes=[ht])
                    return u_gu
                for mc8 in range(8):
                    units.append(mk_gu(0, mc8))

                def u_wo():
                    osrc0 = bass.AP(aps["wbf"].tensor, 2 * GU + e_ * ME * D,
                                    [[D, 128], [128 * D, 8], [1, 1024]])
                    ctx.dma("sync", wview(s_a), osrc0, reads=["wbf_d"], writes=[stag(s_a)])
                    osrc1 = bass.AP(aps["wbf"].tensor, 2 * GU + e_ * ME * D + 8 * 128 * D,
                                    [[D, 128], [128 * D, 8], [1, 1024]])
                    ctx.dma("act", wview(s_b), osrc1, reads=["wbf_d"], writes=[stag(s_b)])
                units.append(u_wo)
                for mc8 in range(8):
                    units.append(mk_gu(1, mc8))

                def mk_out(pp, pas):
                    dcs = [2 * pp, 2 * pp + 1]
                    so_ = s_a if pas == 0 else s_b
                    obanks = ([(PSOL[0], "PSO0"), (PSOL[1], "PSO1")] if pp % 2 == 0
                              else [(PSGL[0], "PSG0"), (PSGL[1], "PSG1")])

                    def u_out():
                        def mmo(e, pas=pas, dcs=dcs, so_=so_, e_=e_, obanks=obanks):
                            ins = None
                            for j, dc in enumerate(dcs):
                                pso = obanks[j][0]
                                for mc8 in range(8):
                                    mc = pas * 8 + mc8
                                    ins = e.matmul(
                                        pso[:], wsl(so_, mc8, dc * 128, (dc + 1) * 128),
                                        H16[:, mc * 512:(mc + 1) * 512],
                                        start=(pas == 0 and mc8 == 0),
                                        stop=(pas == 1 and mc8 == 7 and e_ != 0))
                                if pas == 1 and e_ == 0:
                                    ins = e.matmul(pso[:], BOE[0:8, dc * 128:(dc + 1) * 128],
                                                   EWB16[0:8, :], start=False, stop=True)
                            return ins
                        ctx.op("pe", mmo, reads=[stag(so_), "p7h%d" % pas, "p7boe", "p7ew16"],
                               writes=[b[1] for b in obanks])
                        if pas == 1:
                            for j, dc in enumerate(dcs):
                                pso, pob = obanks[j]
                                if e_ == 0:
                                    ctx.op("dve", lambda e, dc=dc, pso=pso: e.tensor_copy(
                                        out=XF[:, dc * 512:(dc + 1) * 512], in_=pso[:]),
                                           reads=[pob], writes=["p7acc"])
                                else:
                                    ctx.op("dve", lambda e, dc=dc, pso=pso: e.tensor_tensor(
                                        out=XF[:, dc * 512:(dc + 1) * 512],
                                        in0=XF[:, dc * 512:(dc + 1) * 512],
                                        in1=pso[:], op=ALU.add),
                                           reads=[pob, "p7acc"], writes=["p7acc"])
                    return u_out
                for pp in range(4):
                    units.append(mk_out(pp, 0))
                    units.append(mk_out(pp, 1))

                if e_ == NE - 1:
                    def u_store():
                        for dc in range(8):
                            ctx.dma("pool", aps["moe_f"][dc * 128:(dc + 1) * 128, n0:n0 + 512],
                                    XF[:, dc * 512:(dc + 1) * 512],
                                    reads=["p7acc"], writes=["moe_d"])
                    units.append(u_store)
                return units
            su = []
            for tc in range(NTC):
                su.append(lambda tc=tc: su_prolog(tc))
                for dc in range(DI // 128):
                    su.extend(su_dc(tc, dc))
            mu = []
            for nt in range(N // 512):
                for e_ in range(NE):
                    mu.extend(mu_quantum(nt, e_))
            ratio = float(len(mu)) / float(len(su))
            credit = 0.0
            mi = 0
            for s in su:
                s()
                credit += ratio
                while credit >= 1.0 and mi < len(mu):
                    mu[mi]()
                    mi += 1
                    credit -= 1.0
            while mi < len(mu):
                mu[mi]()
                mi += 1
        ctx.barrier()
    def _ph_P65a():
        with (nc.sbuf_tensor(_u("W1"), [128, 16384], BF16) as W1,
              nc.sbuf_tensor(_u("W2"), [128, 16384], BF16) as W2,
              nc.sbuf_tensor(_u("B16A"), [128, 8192], BF16) as B16A,
              nc.sbuf_tensor(_u("B16M"), [128, 4096], BF16) as B16M,
              nc.sbuf_tensor(_u("B16S"), [128, 4096], BF16) as B16S,
              nc.sbuf_tensor(_u("XA"), [128, 3072], F32) as XA,
              nc.psum_tensor(_u("PSA0"), [128, 512], F32) as PSA0,
              nc.psum_tensor(_u("PSA1"), [128, 512], F32) as PSA1):
            for kc in range(16):
                cast_load(W1[:, kc * 1024:(kc + 1) * 1024],
                          aps["out_w"][kc * 128:(kc + 1) * 128, :], PP, "p65w1", 1024)
            for kc in range(8):
                cast_load(W2[:, kc * 2048:(kc + 1) * 2048],
                          aps["glu_w"][kc * 128:(kc + 1) * 128, :], PP, "p65w2", 2048)
            for nt in range(N // 512):
                n0 = nt * 512
                for c in range(16):
                    ctx.dma("sync", B16A[:, c * 512:(c + 1) * 512],
                            aps["y3"][c * 128:(c + 1) * 128, n0:n0 + 512],
                            reads=["y3_d"], writes=["p65y"])
                for mc in range(8):
                    ps = PSA0 if mc % 2 == 0 else PSA1
                    pb = "PSA%d" % (mc % 2)
                    def mmop(e, mc=mc, ps=ps):
                        ins = None
                        for k in range(16):
                            ins = e.matmul(ps[:], W1[:, k * 1024 + mc * 128: k * 1024 + (mc + 1) * 128],
                                           B16A[:, k * 512:(k + 1) * 512],
                                           start=(k == 0), stop=(k == 15))
                        return ins
                    ctx.op("pe", mmop, reads=["p65w1", "p65y"], writes=[pb])
                    ctx.op("act", lambda e, mc=mc, ps=ps: e.activation(
                        out=B16M[:, mc * 512:(mc + 1) * 512], in_=ps[:], func=AF.Gelu),
                           reads=[pb], writes=["p65m"])
                for gc in range(8, 16):
                    ps = PSA0 if gc % 2 == 0 else PSA1
                    pb = "PSA%d" % (gc % 2)
                    def mmgl(e, gc=gc, ps=ps):
                        ins = None
                        for k in range(8):
                            ins = e.matmul(ps[:], W2[:, k * 2048 + gc * 128: k * 2048 + (gc + 1) * 128],
                                           B16M[:, k * 512:(k + 1) * 512],
                                           start=(k == 0), stop=(k == 7))
                        return ins
                    ctx.op("pe", mmgl, reads=["p65w2", "p65m"], writes=[pb])
                    ctx.op("act", lambda e, gc=gc, ps=ps: e.activation(
                        out=B16S[:, (gc - 8) * 512:(gc - 7) * 512], in_=ps[:], func=AF.Sigmoid,
                        bias=CONST[:, 112 + gc:113 + gc]),
                           reads=[pb, "CONST"], writes=["p65sig"])
                for gc in range(8):
                    ps = PSA0 if gc % 2 == 0 else PSA1
                    pb = "PSA%d" % (gc % 2)
                    def mmgl2(e, gc=gc, ps=ps):
                        ins = None
                        for k in range(8):
                            ins = e.matmul(ps[:], W2[:, k * 2048 + gc * 128: k * 2048 + (gc + 1) * 128],
                                           B16M[:, k * 512:(k + 1) * 512],
                                           start=(k == 0), stop=(k == 7))
                        return ins
                    ctx.op("pe", mmgl2, reads=["p65w2", "p65m"], writes=[pb])
                    ctx.op("dve", lambda e, gc=gc, ps=ps: e.scalar_tensor_tensor(
                        out=XA[:, 0:512], in0=ps[:], scalar=CONST[:, 112 + gc:113 + gc],
                        in1=B16S[:, gc * 512:(gc + 1) * 512], op0=ALU.add, op1=ALU.mult),
                           reads=[pb, "CONST", "p65sig"], writes=["p65m2"])
                    ctx.dma("sync", XA[:, 512:1024],
                            aps["xn2_f"][gc * 128:(gc + 1) * 128, n0:n0 + 512],
                            reads=["xn2f_d", "p65sk"], writes=["p65sk"])
                    ctx.op("dve", lambda e: e.tensor_tensor(out=XA[:, 1024:1536],
                                                            in0=XA[:, 0:512], in1=XA[:, 512:1024],
                                                            op=ALU.add),
                           reads=["p65m2", "p65sk"], writes=["p65s1"])
                    ctx.dma("sync", XA[:, 1536:2048],
                            aps["moe_f"][gc * 128:(gc + 1) * 128, n0:n0 + 512],
                            reads=["moe_d", "p65mo"], writes=["p65mo", "p65o"])
                    ctx.op("act", lambda e: e.activation(out=XA[:, 2048:2560],
                                                         in_=XA[:, 1536:2048], func=AF.Silu),
                           reads=["p65mo"], writes=["p65sg"])
                    ctx.op("dve", lambda e: e.tensor_tensor(out=XA[:, 0:512],
                                                            in0=XA[:, 2048:2560],
                                                            in1=XA[:, 1024:1536], op=ALU.mult),
                           reads=["p65sg", "p65s1", "p65m2"], writes=["p65sg", "p65m2"])
                    ctx.dma("sync", XA[:, 2560:3072],
                            aps["x1_f"][gc * 128:(gc + 1) * 128, n0:n0 + 512],
                            reads=["x1_d", "p65x1"], writes=["p65x1"])
                    ctx.op("dve", lambda e: e.tensor_tensor(out=XA[:, 1536:2048],
                                                            in0=XA[:, 0:512],
                                                            in1=XA[:, 2560:3072], op=ALU.add),
                           reads=["p65sg", "p65x1", "p65m2"], writes=["p65o", "p65mo"])
                    ctx.dma("pool", aps["out"][gc * 128:(gc + 1) * 128, n0:n0 + 512],
                            XA[:, 1536:2048], reads=["p65o"], writes=["out_d"])
        ctx.barrier()


    if "P1" in phases:
        _ph_prologue()
    if "P2" in phases:
        _ph_P2a()
    if "P3a" in phases:
        _ph_P3aa()
    if "PW" in phases:
        _ph_casts()
    if "P3b" in phases:
        _ph_P3bma()
    if "P4b" in phases:
        _ph_P4ba()
    if "P4c" in phases:
        _ph_P4ca()
    if "P6" in phases and "P7" in phases:
        _ph_P67()
    if "P65" in phases:
        _ph_P65a()

    ctx.finalize(sems)
    stack.close()
    return nc, sorted(need_in | always_const)


GROUPS = [ALL_PHASES]


def kernel(**inputs):
    n_tok = inputs["L0_attn_x"].shape[1]
    w = prep_weights(inputs)
    x = np.ascontiguousarray(np.asarray(inputs["L0_attn_x"], dtype=np.float32))
    ncore = x.shape[0]
    host = {}
    for c in range(ncore):
        host[("x_tm", c)] = np.ascontiguousarray(x[c])
        host[("x_fm", c)] = np.ascontiguousarray(x[c].T)
    state = [dict() for _ in range(ncore)]
    out = None
    for gi, group in enumerate(GROUPS):
        nc, in_names = build_program(n_tok, group)
        in_maps = []
        for c in range(ncore):
            m = {}
            for name in in_names:
                if name in w:
                    m[name] = w[name]
                elif (name, c) in host:
                    m[name] = host[(name, c)]
                else:
                    m[name] = state[c][name]
            in_maps.append(m)
        res = run_bass_kernel_spmd(nc, in_maps, core_ids=list(range(ncore)))
        for c in range(ncore):
            for k, arr in res.results[c].items():
                state[c][k] = arr
    outs = [np.ascontiguousarray(state[c]["out"].T) for c in range(ncore)]
    return np.stack(outs, axis=0).astype(np.float32)


# version 15
# speedup vs baseline: 1.0311x; 1.0197x over previous
import contextlib
import math
import os
import numpy as np
import concourse.bass as bass
import concourse.mybir as mybir
from concourse.bass_utils import run_bass_kernel_spmd

F32 = mybir.dt.float32
BF16 = mybir.dt.bfloat16
I32 = mybir.dt.int32
AF = mybir.ActivationFunctionType
ALU = mybir.AluOpType
AX = mybir.AxisListType

NCORE = 8
D = 1024
DI = 2048
MF = 2816
DS = 16
DR = 64
NE = 8
ME = 2048
NH = 8
TWO_PI = float(2.0 * np.pi)
MAGIC = 12582912.0
TC = 256

ENG = ("pe", "act", "dve", "pool", "sync")


class Buf:
    __slots__ = ("name", "writer", "readers")

    def __init__(self, name):
        self.name = name
        self.writer = None
        self.readers = []


NDMASEM = 8


class Ctx:
    def __init__(self, nc):
        self.nc = nc
        self.prog = {e: [] for e in ENG}
        self.seq = {"pe": 0, "act": 0, "dve": 0, "pool": 0, "sync": 0}
        for q in ("sync", "pool", "act"):
            for i in range(NDMASEM):
                self.seq["dq_%s_%d" % (q, i)] = 0
        self.dmacnt = {"sync": 0, "pool": 0, "act": 0}
        self.hw = {e: {} for e in ENG}
        self.bufs = {}

    def buf(self, name):
        if name not in self.bufs:
            self.bufs[name] = Buf(name)
        return self.bufs[name]

    PARANOID = False

    def op(self, eng, fn, reads=(), writes=(), dma=False):
        reads = [self.buf(r) if isinstance(r, str) else r for r in reads]
        writes = [self.buf(w) if isinstance(w, str) else w for w in writes]
        need = {}
        if Ctx.PARANOID:
            for k in self.seq:
                if Ctx.PARANOID is not True and k not in Ctx.PARANOID:
                    continue
                if self.seq[k] > 0 and self.hw[eng].get(k, 0) < self.seq[k] and not (k == eng == "pe"):
                    need[k] = self.seq[k]

        def add(src):
            if src is None:
                return
            key, s = src
            if key == eng and eng == "pe":
                return
            if self.hw[eng].get(key, 0) >= s:
                return
            if need.get(key, 0) < s:
                need[key] = s
        for b in reads:
            add(b.writer)
        for b in writes:
            add(b.writer)
            for r in b.readers:
                add(r)
        if dma:
            idx = self.dmacnt[eng] % NDMASEM
            self.dmacnt[eng] += 1
            key = "dq_%s_%d" % (eng, idx)
            if self.seq[key] > 0 and self.hw[eng].get(key, 0) < self.seq[key]:
                if need.get(key, 0) < self.seq[key]:
                    need[key] = self.seq[key]
        else:
            key = eng
        for k, val in need.items():
            self.hw[eng][k] = val
        inc = 16 if dma else 1
        self.seq[key] += inc
        myseq = self.seq[key]
        self.prog[eng].append((list(need.items()), fn, key, inc))
        for b in writes:
            b.writer = (key, myseq)
            b.readers = []
        for b in reads:
            b.readers.append((key, myseq))
        return myseq

    def dma(self, q, out_ap, in_ap, reads=(), writes=(), **kw):
        def fn(e, out_ap=out_ap, in_ap=in_ap, kw=kw):
            return e.dma_start(out=out_ap, in_=in_ap, **kw)
        return self.op(q, fn, reads=reads, writes=writes, dma=True)

    def dmat(self, q, out_ap, in_ap, reads=(), writes=()):
        def fn(e, out_ap=out_ap, in_ap=in_ap):
            return e.dma_start_transpose(out=out_ap, in_=in_ap)
        return self.op(q, fn, reads=reads, writes=writes, dma=True)

    def barrier(self):
        snap = dict(self.seq)
        for e in ENG:
            waits = []
            for k, val in snap.items():
                if val > 0 and self.hw[e].get(k, 0) < val:
                    waits.append((k, val))
                    self.hw[e][k] = val
            if waits:
                self.prog[e].append((waits, None, None, 0))

    def finalize(self, sems):
        self.barrier()
        final = dict(self.seq)
        nc = self.nc
        with nc.Block() as block:
            def runner(eng):
                def run(obj):
                    for waits, fn, key, inc in self.prog[eng]:
                        for k, val in waits:
                            obj.wait_ge(sems[k], val)
                        if fn is not None:
                            ins = fn(obj)
                            ins.then_inc(sems[key], inc)
                return run
            block.tensor(runner("pe"))
            block.scalar(runner("act"))
            block.vector(runner("dve"))
            block.gpsimd(runner("pool"))
            block.sync(runner("sync"))


def prep_weights(inp):
    w = {}
    f32 = lambda a: np.ascontiguousarray(np.asarray(a), dtype=np.float32)
    qk_w = f32(inp["L0_attn_qk_w"])
    qk_b = f32(inp["L0_attn_qk_b"])
    perm = np.r_[np.arange(0, D, 2), np.arange(1, D, 2)]
    w["qk_w"] = np.ascontiguousarray(qk_w[:, np.r_[perm, D + perm]])
    w["qk_b"] = np.ascontiguousarray(qk_b[np.r_[perm, D + perm]][None, :])
    w["lepe_w"] = f32(inp["L0_attn_lepe_w"])
    w["lepe_b"] = np.ascontiguousarray(f32(inp["L0_attn_lepe_b"]).reshape(8, 128).T)
    w["wg"] = f32(inp["L0_ffn_w_gate"])
    w["wu"] = f32(inp["L0_ffn_w_up"])
    w["wd"] = f32(inp["L0_ffn_w_down"])
    w["ln_g"] = np.ascontiguousarray(f32(inp["L1_attn_ln_g"]).reshape(8, 128).T)
    w["ln_b"] = np.ascontiguousarray(f32(inp["L1_attn_ln_b"]).reshape(8, 128).T)
    w["in_w"] = np.ascontiguousarray(f32(inp["L1_attn_in_proj_w"]).T)
    w["conv_w"] = np.ascontiguousarray(
        f32(inp["L1_attn_conv_w"])[:, 0, :].reshape(16, 128, 4).transpose(1, 0, 2).reshape(128, 64))
    w["conv_b"] = np.ascontiguousarray(f32(inp["L1_attn_conv_b"]).reshape(16, 128).T)
    xw = f32(inp["L1_attn_x_proj_w"])
    w["xw_dt"] = np.ascontiguousarray(xw[0:DR].T)
    w["xw_B"] = np.ascontiguousarray(xw[DR:DR + DS].T)
    w["xw_C"] = np.ascontiguousarray(xw[DR + DS:].T)
    w["dt_w"] = np.ascontiguousarray(f32(inp["L1_attn_dt_proj_w"]).T)
    w["dt_b"] = np.ascontiguousarray(f32(inp["L1_attn_dt_proj_b"]).reshape(16, 128).T)
    w["A_log"] = f32(inp["L1_attn_A_log"])
    w["D_"] = np.ascontiguousarray(f32(inp["L1_attn_D"]).reshape(16, 128).T)
    w["out_w"] = np.ascontiguousarray(f32(inp["L1_attn_out_proj_w"]).T)
    w["glu_w"] = np.ascontiguousarray(f32(inp["L1_attn_glu_w"]).T)
    w["glu_b"] = np.ascontiguousarray(f32(inp["L1_attn_glu_b"]).reshape(16, 128).T)
    w["Wr"] = f32(inp["L1_ffn_W_router"])
    w["Wg_e"] = f32(inp["L1_ffn_W_gate"])
    w["Wu_e"] = f32(inp["L1_ffn_W_up"])
    w["Wo_e"] = f32(inp["L1_ffn_W_out"])
    w["bg_e"] = np.ascontiguousarray(f32(inp["L1_ffn_b_gate"]).reshape(8, 16, 128).transpose(0, 2, 1))
    w["bu_e"] = np.ascontiguousarray(f32(inp["L1_ffn_b_up"]).reshape(8, 16, 128).transpose(0, 2, 1))
    w["bo_e"] = f32(inp["L1_ffn_b_out"])
    w["ident"] = np.eye(128, dtype=np.float32)
    return w


WEIGHT_SHAPES = {
    "qk_w": [D, 2 * D], "qk_b": [1, 2 * D], "lepe_w": [3, D, D], "lepe_b": [128, 8],
    "wg": [MF, D], "wu": [MF, D], "wd": [D, MF],
    "ln_g": [128, 8], "ln_b": [128, 8], "in_w": [D, 2 * DI],
    "conv_w": [128, 64], "conv_b": [128, 16],
    "xw_dt": [DI, DR], "xw_B": [DI, DS], "xw_C": [DI, DS],
    "dt_w": [DR, DI], "dt_b": [128, 16], "A_log": [DI, DS], "D_": [128, 16],
    "out_w": [DI, D], "glu_w": [D, DI], "glu_b": [128, 16],
    "Wr": [D, NE], "Wg_e": [NE, D, ME], "Wu_e": [NE, D, ME], "Wo_e": [NE, ME, D],
    "bg_e": [NE, 128, 16], "bu_e": [NE, 128, 16], "bo_e": [NE, D],
    "ident": [128, 128],
}

SCRATCH = {
    "xn_tm": (lambda N: [N, D], BF16),
    "xq_tm": (lambda N: [N, D], BF16),
    "cos_sp": (lambda N: [N, 512], F32),
    "sin_sp": (lambda N: [N, 512], F32),
    "theta_row": (lambda N: [1, 512], F32),
    "q_tm": (lambda N: [N, D], BF16),
    "qr_tm": (lambda N: [N, D], BF16),
    "kv_d": (lambda N: [128, 1024], BF16),
    "ks_d": (lambda N: [128, 8], F32),
    "kmean_row": (lambda N: [1, D], F32),
    "z_rows": (lambda N: [NH, N], F32),
    "s0_f": (lambda N: [D, N], F32),
    "wgqT_d": (lambda N: [D, MF], BF16),
    "wuqT_d": (lambda N: [D, MF], BF16),
    "wdqT_d": (lambda N: [8, MF, 128], BF16),
    "hq_st": (lambda N: [512, MF], BF16),
    "x1_f": (lambda N: [D, N], F32),
    "xn2_bf": (lambda N: [D, N], BF16),
    "xn2_f": (lambda N: [D, N], F32),
    "xni_bf": (lambda N: [D, N], BF16),
    "mu_row": (lambda N: [2, N], F32),
    "mu2_row": (lambda N: [2, N], F32),
    "siluz": (lambda N: [DI, N], BF16),
    "xc_f": (lambda N: [DI, N], F32),
    "dt_f": (lambda N: [DI, N], F32),
    "dtu_f": (lambda N: [DI, N], BF16),
    "B_d": (lambda N: [DS, N], F32),
    "C_d": (lambda N: [DS, N], F32),
    "ew_rows": (lambda N: [NE, N], F32),
    "y3": (lambda N: [DI, N], BF16),
    "s1_f": (lambda N: [D, N], F32),
    "moe_f": (lambda N: [D, N], F32),
    "wbf": (lambda N: [3 * NE * D * ME], BF16),
    "dbg1": (lambda N: [D, N], F32),
    "dbg2": (lambda N: [D, N], F32),
    "dbgb": (lambda N: [128, 8192], BF16),
    "dbgf": (lambda N: [128, 8192], F32),
}

PHASE_IO = {
    "P1": (["x_tm"], ["xn_tm", "xq_tm", "cos_sp", "sin_sp"]),
    "PW": (["Wg_e", "Wu_e", "Wo_e"], ["wbf"]),
    "P2": (["xn_tm", "cos_sp", "sin_sp", "qk_w", "qk_b"], ["q_tm", "qr_tm", "kv_d", "ks_d"]),
    "P3a": (["q_tm", "qr_tm", "xn_tm", "kv_d", "ks_d", "lepe_w", "lepe_b"], ["s0_f"]),
    "P3b": (["xq_tm", "wg", "wu", "wd", "s0_f", "x_fm", "ln_g", "ln_b"],
            ["x1_f", "xn2_bf", "xn2_f", "xni_bf"]),
    "P4b": (["xni_bf", "in_w", "conv_w", "conv_b", "xw_dt", "xw_B", "xw_C", "dt_w", "dt_b"],
            ["siluz", "xc_f", "dt_f", "dtu_f", "B_d", "C_d"]),
    "P4c": (["xn2_bf", "Wr"], ["ew_rows"]),
    "P6": (["dt_f", "dtu_f", "xc_f", "siluz", "B_d", "C_d", "A_log", "D_"], ["y3"]),
    "P7": (["wbf", "xn2_bf", "ew_rows", "bg_e", "bu_e", "bo_e"], ["moe_f"]),
    "P65": (["y3", "out_w", "glu_w", "glu_b", "xn2_f", "moe_f", "x1_f"], ["out"]),
}
ALL_PHASES = ["P1", "PW", "P2", "P3a", "P3b", "P4b", "P4c", "P6", "P7", "P65"]


def build_program(n_tok, phases, taps=()):
    nc = bass.Bass("TRN2", target_bir_lowering=False)
    N = n_tok
    NT1 = N // 128
    NT5 = N // 512
    NTC = N // TC
    phases = list(phases)
    need_in = set()
    need_out = set()
    produced = set()
    for ph in phases:
        ins, outs = PHASE_IO[ph]
        for x in ins:
            if x not in produced:
                need_in.add(x)
        for x in outs:
            produced.add(x)
    for ph in phases:
        for x in PHASE_IO[ph][1]:
            need_out.add(x)
    aps = {}
    always_const = {"ident"}
    for name in sorted(need_in | always_const):
        if name in WEIGHT_SHAPES:
            aps[name] = nc.dram_tensor(name, WEIGHT_SHAPES[name], F32, kind="ExternalInput").ap()
        elif name in ("x_tm",):
            aps[name] = nc.dram_tensor(name, [N, D], F32, kind="ExternalInput").ap()
        elif name == "x_fm":
            aps[name] = nc.dram_tensor(name, [D, N], F32, kind="ExternalInput").ap()
        else:
            s, dt = SCRATCH[name][0](N), SCRATCH[name][1]
            aps[name] = nc.dram_tensor(name, s, dt, kind="ExternalInput").ap()
    for name in sorted(need_out):
        if name == "out":
            aps[name] = nc.dram_tensor(name, [D, N], F32, kind="ExternalOutput").ap()
        else:
            s, dt = SCRATCH[name][0](N), SCRATCH[name][1]
            aps[name] = nc.dram_tensor(name, s, dt, kind="ExternalOutput").ap()
    for name, (sfn, dt) in SCRATCH.items():
        if name not in aps:
            kind = "ExternalOutput" if name in taps else "Internal"
            aps[name] = nc.dram_tensor(name, sfn(N), dt, kind=kind).ap()
    if "out" not in aps:
        aps["out"] = nc.dram_tensor("out", [D, N], F32,
                                    kind="ExternalOutput" if "P8" in phases else "Internal").ap()

    ctx = Ctx(nc)
    v = ctx.buf
    _uc = [0]
    def _u(name):
        _uc[0] += 1
        return "%s_%d" % (name, _uc[0])
    stack = contextlib.ExitStack()
    sems = {}
    semnames = ["pe", "act", "dve", "pool", "sync"]
    for q in ("sync", "pool", "act"):
        for i in range(NDMASEM):
            semnames.append("dq_%s_%d" % (q, i))
    for k in semnames:
        sems[k] = stack.enter_context(nc.semaphore(k))

    CONST = stack.enter_context(nc.sbuf_tensor(_u("CONST"), [128, 512], F32))
    SMALL = stack.enter_context(nc.sbuf_tensor(_u("SMALL"), [128, 512], F32))
    PP = stack.enter_context(nc.sbuf_tensor(_u("PP"), [128, 2048], F32))
    BC1 = stack.enter_context(nc.sbuf_tensor(_u("BC1"), [128, 2048], F32))
    IDT = stack.enter_context(nc.sbuf_tensor(_u("IDT"), [128, 128], F32))
    IDB = stack.enter_context(nc.sbuf_tensor(_u("IDB"), [128, 128], BF16))
    ONE = stack.enter_context(nc.sbuf_tensor(_u("ONE"), [128, 8], F32))
    ONEB = stack.enter_context(nc.sbuf_tensor(_u("ONEB"), [128, 8], BF16))

    ctx.dma("sync", IDT[:], aps["ident"][:], writes=["IDT"])
    ctx.op("dve", lambda e: e.tensor_copy(out=IDB[:], in_=IDT[:]), reads=["IDT"], writes=["IDB"])
    ctx.op("dve", lambda e: e.memset(ONE[:], 1.0), writes=["ONE"])
    ctx.op("dve", lambda e: e.memset(ONEB[:], 1.0), writes=["ONEB"])
    cmap = {"conv_w": (0, 64), "conv_b": (64, 16), "dt_b": (80, 16), "D_": (96, 16),
            "glu_b": (112, 16), "ln_g": (128, 8), "ln_b": (136, 8), "lepe_b": (144, 8)}
    for name, (off, w_) in cmap.items():
        if name in aps:
            ctx.dma("sync", CONST[:, off:off + w_], aps[name][:], writes=["CONST"])

    def rsqrt(out_ap, in_ap, t1, t2, rd, wr, nrows=128):
        ctx.op("dve", lambda e: e.tensor_scalar(out=t1.bitcast(I32), in0=in_ap.bitcast(I32),
                                                scalar1=1, scalar2=None,
                                                op0=ALU.logical_shift_right),
               reads=rd, writes=["rsq1"])
        ctx.op("dve", lambda e: e.tensor_scalar(out=t1.bitcast(I32), in0=t1.bitcast(I32),
                                                scalar1=-1, scalar2=0x5f3759df,
                                                op0=ALU.mult, op1=ALU.add),
               reads=["rsq1"], writes=["rsq1"])
        for _ in range(3):
            ctx.op("dve", lambda e: e.tensor_tensor(out=t2, in0=t1, in1=t1, op=ALU.mult),
                   reads=["rsq1"], writes=["rsq2"])
            ctx.op("dve", lambda e: e.tensor_tensor(out=t2, in0=t2, in1=in_ap, op=ALU.mult),
                   reads=["rsq2"] + rd, writes=["rsq2"])
            ctx.op("dve", lambda e: e.tensor_scalar(out=t2, in0=t2, scalar1=-0.5, scalar2=1.5,
                                                    op0=ALU.mult, op1=ALU.add),
                   reads=["rsq2"], writes=["rsq2"])
            ctx.op("dve", lambda e: e.tensor_tensor(out=t1, in0=t1, in1=t2, op=ALU.mult),
                   reads=["rsq1", "rsq2"], writes=["rsq1"])
        ctx.op("dve", lambda e: e.tensor_copy(out=out_ap, in_=t1),
               reads=["rsq1"], writes=wr)

    def quant(eng, x_ap, out_ap, F, bm, sg, vv, acc, tag, rd, wr):
        nb = F // 16
        x3 = x_ap.rearrange("p (a b) -> p a b", b=16)
        ctx.op(eng, lambda e: e.tensor_reduce(out=bm[:, 0:nb], in_=x3, axis=AX.X, op=ALU.max,
                                              apply_absolute_value=True),
               reads=rd, writes=[tag + "bm"])
        ctx.op(eng, lambda e: e.tensor_scalar(out=bm[:, 0:nb], in0=bm[:, 0:nb],
                                              scalar1=1.0 / 6.0, scalar2=1e-12,
                                              op0=ALU.mult, op1=ALU.max),
               reads=[tag + "bm"], writes=[tag + "bm"])
        ctx.op("dve", lambda e: e.reciprocal(out=bm[:, nb:2 * nb], in_=bm[:, 0:nb]),
               reads=[tag + "bm"], writes=[tag + "bm"])
        ctx.op("act", lambda e: e.sign(out=sg[:, 0:F], in_=x_ap), reads=rd, writes=[tag + "sg"])
        ctx.op("act", lambda e: e.activation(out=vv[:, 0:F], in_=x_ap, func=AF.Abs),
               reads=rd, writes=[tag + "v"])
        rscb = bm[:, nb:2 * nb].unsqueeze(2).broadcast_to((128, nb, 16))
        v3 = vv[:, 0:F].rearrange("p (a b) -> p a b", b=16)
        ctx.op(eng, lambda e: e.tensor_tensor(out=v3, in0=v3, in1=rscb, op=ALU.mult),
               reads=[tag + "v", tag + "bm"], writes=[tag + "v"])
        ctx.op(eng, lambda e: e.tensor_scalar(out=bm[:, 0:nb], in0=bm[:, 0:nb], scalar1=0.5,
                                              scalar2=None, op0=ALU.mult),
               reads=[tag + "bm"], writes=[tag + "bm"])
        schb = bm[:, 0:nb].unsqueeze(2).broadcast_to((128, nb, 16))
        s3 = sg[:, 0:F].rearrange("p (a b) -> p a b", b=16)
        ctx.op(eng, lambda e: e.tensor_tensor(out=s3, in0=s3, in1=schb, op=ALU.mult),
               reads=[tag + "sg", tag + "bm"], writes=[tag + "sg"])
        ctx.op(eng, lambda e: e.tensor_scalar(out=acc[:, 0:F], in0=vv[:, 0:F], scalar1=5.0,
                                              scalar2=2.0, op0=ALU.is_gt, op1=ALU.mult),
               reads=[tag + "v"], writes=[tag + "acc"])
        for m in (3.5, 2.5):
            ctx.op(eng, lambda e, m=m: e.scalar_tensor_tensor(out=acc[:, 0:F], in0=vv[:, 0:F],
                                                              scalar=m, in1=acc[:, 0:F],
                                                              op0=ALU.is_gt, op1=ALU.add),
                   reads=[tag + "v", tag + "acc"], writes=[tag + "acc"])
        ctx.op(eng, lambda e: e.tensor_scalar(out=acc[:, 0:F], in0=acc[:, 0:F], scalar1=2.0,
                                              scalar2=None, op0=ALU.mult),
               reads=[tag + "acc"], writes=[tag + "acc"])
        for m in (1.75, 1.25, 0.75, 0.25):
            ctx.op(eng, lambda e, m=m: e.scalar_tensor_tensor(out=acc[:, 0:F], in0=vv[:, 0:F],
                                                              scalar=m, in1=acc[:, 0:F],
                                                              op0=ALU.is_gt, op1=ALU.add),
                   reads=[tag + "v", tag + "acc"], writes=[tag + "acc"])
        ctx.op(eng, lambda e: e.tensor_tensor(out=out_ap, in0=acc[:, 0:F], in1=sg[:, 0:F],
                                              op=ALU.mult),
               reads=[tag + "acc", tag + "sg"], writes=wr)

    def rrange(ap_, tmp_, rdwr):
        ctx.op("dve", lambda e: e.tensor_scalar(out=tmp_, in0=ap_, scalar1=1.0 / TWO_PI,
                                                scalar2=MAGIC, op0=ALU.mult, op1=ALU.add),
               reads=[rdwr], writes=["rrtmp"])
        ctx.op("dve", lambda e: e.tensor_scalar(out=tmp_, in0=tmp_, scalar1=MAGIC, scalar2=None,
                                                op0=ALU.subtract),
               reads=["rrtmp"], writes=["rrtmp"])
        ctx.op("dve", lambda e: e.scalar_tensor_tensor(out=ap_, in0=tmp_, scalar=-TWO_PI,
                                                       in1=ap_, op0=ALU.mult, op1=ALU.add),
               reads=["rrtmp", rdwr], writes=[rdwr])

    def cast_load(dst_ap, src_ap, stage_ap, tag, cols):
        ctx.dma("pool", dst_ap, src_ap, reads=[tag], writes=[tag])

    def _ph_prologue():
        with (nc.sbuf_tensor(_u("XA"), [128, 6656], F32) as XA,
              nc.sbuf_tensor(_u("XB"), [128, 4096], F32) as XB,
              nc.sbuf_tensor(_u("B16"), [128, 4096], BF16) as B16,
              nc.sbuf_tensor(_u("QXA"), [128, 7680], F32) as QXA,
              nc.sbuf_tensor(_u("QXB"), [128, 8448], F32) as QXB,
              nc.sbuf_tensor(_u("QB16A"), [128, 11264], BF16) as QB16A,
              nc.sbuf_tensor(_u("QB16B"), [128, 512], BF16) as QB16B,
              nc.sbuf_tensor(_u("SM2"), [128, 512], F32) as SM2,
              nc.psum_tensor(_u("PST0"), [128, 128], BF16) as PST0,
              nc.psum_tensor(_u("PST1"), [128, 128], BF16) as PST1):
            ctx.op("pool", lambda e: e.iota(PP[0:1, 0:512], [[1, 512]], base=0,
                                            channel_multiplier=0,
                                            allow_small_or_imprecise_dtypes=True),
                   writes=["PP"])
            ctx.op("act", lambda e: e.activation(out=PP[0:1, 512:1024], in_=PP[0:1, 0:512],
                                                 func=AF.Exp, scale=-math.log(10000.0) / 512.0),
                   reads=["PP"], writes=["PP"])
            ctx.dma("pool", aps["theta_row"][:], PP[0:1, 512:1024], reads=["PP"],
                    writes=["theta_d"])
            th_b = bass.AP(aps["theta_row"].tensor, 0, [[0, 128], [1, 512]])
            ctx.dma("sync", BC1[:, 0:512], th_b, reads=["theta_d"], writes=["BC1"])

            def p1_tile(i):
                par = i % 2
                x_t = XA[:, par * 1024:par * 1024 + 1024]
                xn_t = XA[:, 2048:3072]
                xtag = "p1x%d" % par
                ctx.dma("sync", x_t, aps["x_tm"][i * 128:(i + 1) * 128, :], writes=[xtag])
                ctx.op("dve", lambda e, x_t=x_t: e.bn_stats(out=SMALL[:, 0:6],
                                                            in_=x_t[:, 0:512]),
                       reads=[xtag], writes=["SMALL"])
                ctx.op("dve", lambda e, x_t=x_t: e.bn_stats(out=SMALL[:, 6:12],
                                                            in_=x_t[:, 512:1024]),
                       reads=[xtag], writes=["SMALL"])
                ctx.op("dve", lambda e: e.bn_aggr(out=SMALL[:, 12:14], in_=SMALL[:, 0:12]),
                       reads=["SMALL"], writes=["SMALL"])
                ctx.op("dve", lambda e: e.tensor_scalar(out=SMALL[:, 14:15], in0=SMALL[:, 13:14],
                                                        scalar1=1e-5, scalar2=None, op0=ALU.add),
                       reads=["SMALL"], writes=["SMALL"])
                rsqrt(SMALL[:, 15:16], SMALL[:, 14:15], SMALL[:, 16:17], SMALL[:, 17:18],
                      ["SMALL"], ["SMALL"])
                ctx.op("dve", lambda e, x_t=x_t, xn_t=xn_t: e.tensor_scalar(
                    out=xn_t, in0=x_t, scalar1=SMALL[:, 12:13], scalar2=SMALL[:, 15:16],
                    op0=ALU.subtract, op1=ALU.mult),
                       reads=[xtag, "SMALL"], writes=["p1xn"])
                nbt = "p1xnb%d" % par
                ctx.op("dve", lambda e, xn_t=xn_t, par=par: e.tensor_copy(
                    out=B16[:, par * 1024:par * 1024 + 1024], in_=xn_t),
                       reads=["p1xn"], writes=[nbt])
                ctx.dma("act", aps["xn_tm"][i * 128:(i + 1) * 128, :],
                        B16[:, par * 1024:par * 1024 + 1024],
                        reads=[nbt], writes=["xn_tm_d"])
                qbt = "p1xqb%d" % par
                quant("dve", xn_t, B16[:, 2048 + par * 1024:2048 + par * 1024 + 1024], 1024,
                      SMALL[:, 64:192], XB[:, 0:1024],
                      XB[:, 1024:2048], XB[:, 2048:3072], "p1q", ["p1xn"], [qbt])
                ctx.dma("act", aps["xq_tm"][i * 128:(i + 1) * 128, :],
                        B16[:, 2048 + par * 1024:2048 + par * 1024 + 1024],
                        reads=[qbt], writes=["xq_tm_d"])
                ctx.op("pool", lambda e, i=i: e.iota(SMALL[:, 20:21], [[0, 1]], base=i * 128,
                                                     channel_multiplier=1,
                                                     allow_small_or_imprecise_dtypes=True),
                       writes=["p1tok"])
                a_s = XA[:, 3072:3584]
                a_c = XA[:, 3584:4096]
                ctx.op("dve", lambda e, a_s=a_s: e.tensor_scalar(out=a_s, in0=BC1[:, 0:512],
                                                                 scalar1=SMALL[:, 20:21],
                                                                 scalar2=None, op0=ALU.mult),
                       reads=["BC1", "p1tok"], writes=["p1as"])
                ctx.op("dve", lambda e, a_s=a_s, a_c=a_c: e.tensor_scalar(
                    out=a_c, in0=a_s, scalar1=math.pi / 2, scalar2=None, op0=ALU.add),
                       reads=["p1as"], writes=["p1ac"])
                rrange(a_s, XA[:, 4096:4608], "p1as")
                rrange(a_c, XA[:, 4096:4608], "p1ac")
                ctx.op("act", lambda e, a_s=a_s: e.activation(out=XA[:, 4608:5120], in_=a_s,
                                                              func=AF.Sin),
                       reads=["p1as"], writes=["p1sin"])
                ctx.dma("act", aps["sin_sp"][i * 128:(i + 1) * 128, :], XA[:, 4608:5120],
                        reads=["p1sin"], writes=["sin_d"])
                ctx.op("act", lambda e, a_c=a_c: e.activation(out=XA[:, 5120:5632], in_=a_c,
                                                              func=AF.Sin),
                       reads=["p1ac"], writes=["p1cos"])
                ctx.dma("act", aps["cos_sp"][i * 128:(i + 1) * 128, :], XA[:, 5120:5632],
                        reads=["p1cos"], writes=["cos_d"])

            _punit = [0]

            def p30_gu(wname, dname, mc):
                par = _punit[0] % 2
                _punit[0] += 1
                xtag = "p30x%d" % par
                qtag = "p30qb%d" % par
                qa = QXA[:, par * 1024:par * 1024 + 1024]
                qb = QB16A[:, par * 5632:par * 5632 + 1024]
                ctx.dma("sync", qa, aps[wname][mc * 128:(mc + 1) * 128, :],
                        reads=[xtag], writes=[xtag])
                quant("dve", qa, qb, 1024, SM2[:, 0:256],
                      QXB[:, 0:1024], QXB[:, 1024:2048], QXB[:, 2048:3072], "p30q",
                      [xtag], [qtag])
                for kc in range(8):
                    pst = PST0 if kc % 2 == 0 else PST1
                    ptag = "PST%d" % (kc % 2)
                    btag = "p30tb%d" % (kc % 4)
                    bsl = QB16B[:, (kc % 4) * 128:(kc % 4) * 128 + 128]
                    ctx.op("pe", lambda e, kc=kc, pst=pst, qb=qb: e.transpose(
                        pst[:], qb[:, kc * 128:(kc + 1) * 128], IDB[:]),
                           reads=[qtag, "IDB", btag], writes=[ptag])
                    ctx.op("dve", lambda e, bsl=bsl, pst=pst: e.tensor_copy(out=bsl, in_=pst[:]),
                           reads=[ptag], writes=[btag])
                    ctx.dma("pool", aps[dname][kc * 128:(kc + 1) * 128, mc * 128:(mc + 1) * 128],
                            bsl, reads=[btag], writes=[dname + "_d"])

            def p30_wd(dc):
                par = _punit[0] % 2
                _punit[0] += 1
                xtag = "p30x%d" % par
                qtag = "p30qb%d" % par
                qa = QXA[:, 2048 + par * 2816:2048 + par * 2816 + 2816]
                qb = QB16A[:, par * 5632:par * 5632 + 2816]
                ctx.dma("sync", qa, aps["wd"][dc * 128:(dc + 1) * 128, :],
                        reads=[xtag], writes=[xtag])
                quant("dve", qa, qb, 2816, SM2[:, 0:384],
                      QXB[:, 0:2816], QXB[:, 2816:5632], QXB[:, 5632:8448],
                      "p30q", [xtag], [qtag])
                for mc in range(22):
                    pst = PST0 if mc % 2 == 0 else PST1
                    ptag = "PST%d" % (mc % 2)
                    btag = "p30tb%d" % (mc % 4)
                    bsl = QB16B[:, (mc % 4) * 128:(mc % 4) * 128 + 128]
                    ctx.op("pe", lambda e, mc=mc, pst=pst, qb=qb: e.transpose(
                        pst[:], qb[:, mc * 128:(mc + 1) * 128], IDB[:]),
                           reads=[qtag, "IDB", btag], writes=[ptag])
                    ctx.op("dve", lambda e, bsl=bsl, pst=pst: e.tensor_copy(out=bsl, in_=pst[:]),
                           reads=[ptag], writes=[btag])
                    ctx.dma("pool", aps["wdqT_d"][dc, mc * 128:(mc + 1) * 128, :],
                            bsl, reads=[btag], writes=["wdqT_d_d"])

            lp1 = [lambda i=i: p1_tile(i) for i in range(NT1)]
            l30 = ([lambda w=w, d=d, mc=mc: p30_gu(w, d, mc)
                    for w, d in (("wg", "wgqT_d"), ("wu", "wuqT_d")) for mc in range(22)]
                   + [lambda dc=dc: p30_wd(dc) for dc in range(8)])
            r30 = float(len(l30)) / float(len(lp1))
            credit = 0.0
            j = 0
            for t in lp1:
                t()
                credit += r30
                while credit >= 1.0 and j < len(l30):
                    l30[j]()
                    j += 1
                    credit -= 1.0
            while j < len(l30):
                l30[j]()
                j += 1
        ctx.barrier()

    def _ph_casts():
        GU = NE * D * ME
        for e_ in range(NE):
            for base, srcname, nr, ncol in ((0, "Wg_e", D, ME), (GU, "Wu_e", D, ME),
                                            (2 * GU, "Wo_e", ME, D)):
                dst = bass.AP(aps["wbf"].tensor, base + e_ * D * ME, [[ncol, nr], [1, ncol]])
                ctx.dma("pool", dst, aps[srcname][e_], writes=["wbf_d"])

    def _ph_P2a():
        with (nc.sbuf_tensor(_u("W1"), [128, 16384], BF16) as W1,
              nc.sbuf_tensor(_u("XA"), [128, 6144], F32) as XA,
              nc.sbuf_tensor(_u("XB"), [128, 4096], F32) as XB,
              nc.sbuf_tensor(_u("B16S"), [128, 4096], BF16) as B16S,
              nc.sbuf_tensor(_u("B16T"), [128, 4096], BF16) as B16T,
              nc.psum_tensor(_u("PSQ0"), [128, 512], F32) as PSQ0,
              nc.psum_tensor(_u("PSQ1"), [128, 512], F32) as PSQ1,
              nc.psum_tensor(_u("PSKV0"), [128, 512], F32) as PSKV0,
              nc.psum_tensor(_u("PSKV1"), [128, 512], F32) as PSKV1,
              nc.psum_tensor(_u("PSKS"), [128, 8], F32) as PSKS):
            for c in range(8):
                cast_load(W1[:, c * 2048:(c + 1) * 2048],
                          aps["qk_w"][c * 128:(c + 1) * 128, :], PP, "w1", 2048)
            qkb_b = bass.AP(aps["qk_b"].tensor, 0, [[0, 128], [1, 2048]])
            ctx.dma("sync", BC1[:, 0:2048], qkb_b, reads=["BC1"], writes=["BC1"])
            NT5_ = N // 512
            for st in range(NT5_):
                n0 = st * 512
                for c in range(8):
                    ctx.dmat("sync", B16S[:, c * 512:(c + 1) * 512],
                             aps["xn_tm"][n0:n0 + 512, c * 128:(c + 1) * 128],
                             reads=["xn_tm_d"], writes=["p2xnT"])
                for sub in range(4):
                    i = st * 4 + sub
                    first = (i == 0)
                    last = (i == NT1 - 1)
                    ctx.dma("sync", B16T[:, 0:1024],
                            aps["xn_tm"][i * 128:(i + 1) * 128, :],
                            reads=["xn_tm_d"], writes=["p2vh"])
                    for fg in range(4):
                        ps = PSQ0 if fg % 2 == 0 else PSQ1
                        pbuf = "PSQ%d" % (fg % 2)
                        def mmq2(e, fg=fg, sub=sub, ps=ps):
                            ins = None
                            for k in range(8):
                                ins = e.matmul(ps[:],
                                               B16S[:, k * 512 + sub * 128: k * 512 + sub * 128 + 128],
                                               W1[:, k * 2048 + fg * 512: k * 2048 + fg * 512 + 512],
                                               start=(k == 0), stop=(k == 7))
                            return ins
                        ctx.op("pe", mmq2, reads=["p2xnT", "w1"], writes=[pbuf])
                        ctx.op("dve", lambda e, fg=fg, ps=ps: e.tensor_tensor(
                            out=XA[:, fg * 512:(fg + 1) * 512], in0=ps[:],
                            in1=BC1[:, fg * 512:(fg + 1) * 512], op=ALU.add),
                               reads=[pbuf, "BC1"], writes=["p2qk"])
                    for half, dst0 in ((0, 4096), (1, 5120)):
                        src = XA[:, half * 1024:(half + 1) * 1024]
                        ctx.op("dve", lambda e, src=src: e.tensor_scalar_min(
                            out=XB[:, 0:1024], in0=src, scalar1=0.0),
                               reads=["p2qk"], writes=["p2t"])
                        ctx.op("act", lambda e: e.activation(out=XB[:, 0:1024], in_=XB[:, 0:1024],
                                                             func=AF.Exp),
                               reads=["p2t"], writes=["p2t"])
                        ctx.op("dve", lambda e, src=src, dst0=dst0: e.scalar_tensor_tensor(
                            out=XA[:, dst0:dst0 + 1024], in0=src, scalar=0.0,
                            in1=XB[:, 0:1024], op0=ALU.max, op1=ALU.add),
                               reads=["p2qk", "p2t"], writes=["p2q" if half == 0 else "p2k"])
                    ctx.op("dve", lambda e: e.tensor_copy(out=B16T[:, 1024:2048],
                                                          in_=XA[:, 4096:5120]),
                           reads=["p2q"], writes=["p2qb"])
                    ctx.dma("pool", aps["q_tm"][i * 128:(i + 1) * 128, :], B16T[:, 1024:2048],
                            reads=["p2qb"], writes=["q_tm_d"])
                    ctx.op("dve", lambda e: e.tensor_copy(out=B16T[:, 2048:3072],
                                                          in_=XA[:, 5120:6144]),
                           reads=["p2k"], writes=["p2kb"])
                    def mmks(e, first=first, last=last):
                        ins = None
                        for c in range(8):
                            ins = e.matmul(PSKS[:, c:c + 1], B16T[:, 2048 + c * 128:2048 + (c + 1) * 128],
                                           ONEB[:, 0:1], start=(first and c == 0), stop=last)
                        return ins
                    ctx.op("pe", mmks, reads=["p2kb", "ONEB"], writes=["PSKS"])
                    ctx.dma("sync", XB[:, 1024:1536], aps["cos_sp"][i * 128:(i + 1) * 128, :],
                            reads=["cos_d"], writes=["p2cos"])
                    ctx.dma("sync", XB[:, 1536:2048], aps["sin_sp"][i * 128:(i + 1) * 128, :],
                            reads=["sin_d"], writes=["p2sin"])
                    for half, qoff, rbuf in ((0, 4096, "p2qr"), (1, 5120, "p2kr")):
                        qe = XA[:, qoff:qoff + 512]
                        qo = XA[:, qoff + 512:qoff + 1024]
                        ctx.op("dve", lambda e, qe=qe: e.tensor_tensor(out=XB[:, 2048:2560],
                                                                       in0=XB[:, 1024:1536], in1=qe,
                                                                       op=ALU.mult),
                               reads=["p2cos", "p2q", "p2k"], writes=["p2r1"])
                        ctx.op("dve", lambda e, qo=qo: e.tensor_tensor(out=XB[:, 2560:3072],
                                                                       in0=XB[:, 1536:2048], in1=qo,
                                                                       op=ALU.mult),
                               reads=["p2sin", "p2q", "p2k"], writes=["p2r2"])
                        ctx.op("dve", lambda e: e.tensor_tensor(out=XB[:, 3072:3584],
                                                                in0=XB[:, 2048:2560],
                                                                in1=XB[:, 2560:3072],
                                                                op=ALU.subtract),
                               reads=["p2r1", "p2r2"], writes=["p2pr"])
                        ctx.op("dve", lambda e, qo=qo: e.tensor_tensor(out=XB[:, 2048:2560],
                                                                       in0=XB[:, 1024:1536], in1=qo,
                                                                       op=ALU.mult),
                               reads=["p2cos", "p2q", "p2k", "p2r1"], writes=["p2r1"])
                        ctx.op("dve", lambda e, qe=qe: e.tensor_tensor(out=XB[:, 2560:3072],
                                                                       in0=XB[:, 1536:2048], in1=qe,
                                                                       op=ALU.mult),
                               reads=["p2sin", "p2q", "p2k", "p2r2"], writes=["p2r2"])
                        ctx.op("dve", lambda e: e.tensor_tensor(out=XB[:, 3584:4096],
                                                                in0=XB[:, 2048:2560],
                                                                in1=XB[:, 2560:3072],
                                                                op=ALU.add),
                               reads=["p2r1", "p2r2"], writes=["p2pi"])
                        ctx.op("dve", lambda e: e.tensor_copy(out=B16T[:, 3072:3584],
                                                              in_=XB[:, 3072:3584]),
                               reads=["p2pr"], writes=[rbuf])
                        ctx.op("dve", lambda e: e.tensor_copy(out=B16T[:, 3584:4096],
                                                              in_=XB[:, 3584:4096]),
                               reads=["p2pi"], writes=[rbuf])
                        if half == 0:
                            ctx.dma("pool", aps["qr_tm"][i * 128:(i + 1) * 128, :],
                                    B16T[:, 3072:4096], reads=[rbuf], writes=["qr_tm_d"])
                        else:
                            ctx.op("dve", lambda e: e.tensor_copy(out=B16T[:, 2048:3072],
                                                                  in_=B16T[:, 3072:4096]),
                                   reads=[rbuf], writes=["p2kb"])
                    def mmkv(e, first=first, last=last):
                        ins = None
                        for h in range(8):
                            ps = PSKV0 if h < 4 else PSKV1
                            ins = e.matmul(ps[:, (h % 4) * 128:(h % 4 + 1) * 128],
                                           B16T[:, 2048 + h * 128:2048 + (h + 1) * 128],
                                           B16T[:, h * 128:(h + 1) * 128],
                                           start=(first and h % 4 == 0), stop=last)
                        return ins
                    ctx.op("pe", mmkv, reads=["p2kb", "p2vh"], writes=["PSKV0", "PSKV1"])
            sc = 1.0 / math.sqrt(float(N))
            ctx.op("act", lambda e: e.activation(out=B16T[:, 0:512], in_=PSKV0[:],
                                                 func=AF.Identity, scale=sc),
                   reads=["PSKV0"], writes=["p2kvb"])
            ctx.op("act", lambda e: e.activation(out=B16T[:, 512:1024], in_=PSKV1[:],
                                                 func=AF.Identity, scale=sc),
                   reads=["PSKV1"], writes=["p2kvb"])
            ctx.dma("pool", aps["kv_d"][:], B16T[:, 0:1024], reads=["p2kvb"], writes=["kv_dd"])
            ctx.op("dve", lambda e: e.tensor_copy(out=SMALL[:, 0:8], in_=PSKS[:]),
                   reads=["PSKS"], writes=["p2ks"])
            ctx.dma("pool", aps["ks_d"][:], SMALL[:, 0:8], reads=["p2ks"], writes=["ks_dd"])
        ctx.barrier()

    def _ph_P3aa():
        with (nc.sbuf_tensor(_u("W2"), [128, 24576], BF16) as W2,
              nc.sbuf_tensor(_u("XA"), [128, 4096], F32) as XA,
              nc.sbuf_tensor(_u("XD"), [128, 2048], F32) as XD,
              nc.sbuf_tensor(_u("B16S"), [128, 8192], BF16) as B16S,
              nc.sbuf_tensor(_u("B16T"), [128, 2048], BF16) as B16T,
              nc.sbuf_tensor(_u("KVZ"), [128, 1024], BF16) as KVZ,
              nc.psum_tensor(_u("PSL0"), [128, 512], F32) as PSL0,
              nc.psum_tensor(_u("PSL1"), [128, 512], F32) as PSL1,
              nc.psum_tensor(_u("PSA"), [128, 512], F32) as PSA,
              nc.psum_tensor(_u("PSZ"), [128, 128], F32) as PSZ):
            for w_ in range(3):
                for kc in range(8):
                    cast_load(W2[:, (w_ * 8 + kc) * 1024:(w_ * 8 + kc + 1) * 1024],
                              aps["lepe_w"][w_, kc * 128:(kc + 1) * 128, :], PP, "w2", 1024)
            ctx.dma("sync", KVZ[:], aps["kv_d"][:], reads=["kv_dd"], writes=["KVZ"])
            ctx.dma("sync", SMALL[:, 0:8], aps["ks_d"][:], reads=["ks_dd"], writes=["p3ks"])
            ctx.op("dve", lambda e: e.tensor_scalar_mul(out=SMALL[:, 8:16], in0=SMALL[:, 0:8],
                                                        scalar1=1.0 / float(N)),
                   reads=["p3ks"], writes=["p3km"])
            ctx.op("pe", lambda e: e.transpose(PSZ[0:8, :], SMALL[:, 8:16], IDT[:]),
                   reads=["p3km", "IDT"], writes=["PSZ"])
            ctx.op("dve", lambda e: e.tensor_copy(out=PP[0:8, 0:128], in_=PSZ[0:8, :]),
                   reads=["PSZ"], writes=["p3kmr"])
            km_dst = bass.AP(aps["kmean_row"].tensor, 0, [[128, 8], [1, 128]])
            ctx.dma("pool", km_dst, PP[0:8, 0:128], reads=["p3kmr"], writes=["kmean_d"])
            km_b = bass.AP(aps["kmean_row"].tensor, 0, [[0, 128], [1, 1024]])
            ctx.dma("sync", BC1[:, 0:1024], km_b, reads=["kmean_d", "BC1"], writes=["BC1"])
            for i in range(NT1):
                ctx.dma("sync", B16T[:, 0:1024], aps["q_tm"][i * 128:(i + 1) * 128, :],
                        reads=["q_tm_d"], writes=["p3q"])
                ctx.op("dve", lambda e: e.tensor_tensor(out=XA[:, 0:1024], in0=B16T[:, 0:1024],
                                                        in1=BC1[:, 0:1024], op=ALU.mult),
                       reads=["p3q", "BC1"], writes=["p3t"])
                den3 = XA[:, 0:1024].rearrange("p (g d) -> p g d", d=64)
                ctx.op("dve", lambda e, den3=den3: e.tensor_reduce(out=SMALL[:, 24:40], in_=den3,
                                                                   axis=AX.X, op=ALU.add),
                       reads=["p3t"], writes=["p3den"])
                ctx.op("dve", lambda e: e.tensor_tensor(out=SMALL[:, 40:48], in0=SMALL[:, 24:32],
                                                        in1=SMALL[:, 32:40], op=ALU.add),
                       reads=["p3den"], writes=["p3den"])
                ctx.op("dve", lambda e: e.tensor_scalar(out=SMALL[:, 40:48], in0=SMALL[:, 40:48],
                                                        scalar1=1e-6, scalar2=None, op0=ALU.add),
                       reads=["p3den"], writes=["p3den"])
                ctx.op("dve", lambda e: e.reciprocal(out=SMALL[:, 48:56], in_=SMALL[:, 40:48]),
                       reads=["p3den"], writes=["p3z"])
                ctx.op("pe", lambda e: e.transpose(PSZ[0:8, :], SMALL[:, 48:56], IDT[:]),
                       reads=["p3z", "IDT", "p3zr"], writes=["PSZ"])
                ctx.op("dve", lambda e, i=i: e.tensor_copy(out=PP[64:72, (i % 4) * 128:(i % 4 + 1) * 128],
                                                           in_=PSZ[0:8, :]),
                       reads=["PSZ"], writes=["p3zr"])
                if i % 4 == 3:
                    st = i // 4
                    ctx.dma("pool", aps["z_rows"][:, st * 512:(st + 1) * 512], PP[64:72, 0:512],
                            reads=["p3zr"], writes=["z_rows_d"])
            for st in range(N // 512):
                n0 = st * 512
                for c in range(8):
                    dst = B16S[:, c * 576:(c + 1) * 576]
                    ctx.dmat("sync", dst[:, 32:544],
                             aps["xn_tm"][n0:n0 + 512, c * 128:(c + 1) * 128],
                             reads=["xn_tm_d"], writes=["p3xh"])
                    if st > 0:
                        ctx.dmat("sync", B16T[:, 1536:1664],
                                 aps["xn_tm"][n0 - 128:n0, c * 128:(c + 1) * 128],
                                 reads=["xn_tm_d"], writes=["p3edge"])
                        ctx.op("dve", lambda e, dst=dst: e.tensor_copy(out=dst[:, 31:32],
                                                                      in_=B16T[:, 1663:1664]),
                               reads=["p3edge"], writes=["p3xh"])
                    else:
                        ctx.op("dve", lambda e, dst=dst: e.memset(dst[:, 31:32], 0.0),
                               writes=["p3xh"])
                    if st < N // 512 - 1:
                        ctx.dmat("sync", B16T[:, 1536:1664],
                                 aps["xn_tm"][n0 + 512:n0 + 640, c * 128:(c + 1) * 128],
                                 reads=["xn_tm_d"], writes=["p3edge"])
                        ctx.op("dve", lambda e, dst=dst: e.tensor_copy(out=dst[:, 544:545],
                                                                      in_=B16T[:, 1536:1537]),
                               reads=["p3edge"], writes=["p3xh"])
                    else:
                        ctx.op("dve", lambda e, dst=dst: e.memset(dst[:, 544:545], 0.0),
                               writes=["p3xh"])
                for h in range(8):
                    psl = PSL0 if h % 2 == 0 else PSL1
                    plbuf = "PSL%d" % (h % 2)
                    def mml(e, h=h, psl=psl):
                        ins = None
                        for w_ in range(3):
                            for kc in range(8):
                                ins = e.matmul(psl[:],
                                               W2[:, (w_ * 8 + kc) * 1024 + h * 128:(w_ * 8 + kc) * 1024 + (h + 1) * 128],
                                               B16S[:, kc * 576 + 31 + w_: kc * 576 + 31 + w_ + 512],
                                               start=(w_ == 0 and kc == 0), stop=(w_ == 2 and kc == 7))
                        return ins
                    ctx.op("pe", mml, reads=["w2", "p3xh"], writes=[plbuf])
                    ctx.dmat("sync", B16T[:, 1024:1536],
                             aps["qr_tm"][n0:n0 + 512, h * 128:(h + 1) * 128],
                             reads=["qr_tm_d"], writes=["p3qrT"])
                    def mma(e, h=h):
                        return e.matmul(PSA[:], KVZ[:, h * 128:(h + 1) * 128], B16T[:, 1024:1536],
                                        start=True, stop=True)
                    ctx.op("pe", mma, reads=["KVZ", "p3qrT"], writes=["PSA"])
                    zb = bass.AP(aps["z_rows"].tensor, h * N + n0, [[0, 128], [1, 512]])
                    ctx.dma("sync", XD[:, 0:512], zb, reads=["z_rows_d"], writes=["p3zb"])
                    ctx.op("dve", lambda e: e.tensor_tensor(out=XD[:, 512:1024], in0=PSA[:],
                                                            in1=XD[:, 0:512], op=ALU.mult),
                           reads=["PSA", "p3zb"], writes=["p3oz"])
                    ctx.op("dve", lambda e, h=h, psl=psl: e.scalar_tensor_tensor(
                        out=XD[:, 1024:1536], in0=psl[:], scalar=CONST[:, 144 + h:145 + h],
                        in1=XD[:, 512:1024], op0=ALU.add, op1=ALU.add),
                           reads=[plbuf, "p3oz", "CONST"], writes=["p3s0"])
                    ctx.dma("pool", aps["s0_f"][h * 128:(h + 1) * 128, n0:n0 + 512],
                            XD[:, 1024:1536], reads=["p3s0"], writes=["s0_d"])
        ctx.barrier()

    def _ph_P3bma():
        with (nc.sbuf_tensor(_u("W1"), [128, 22528], BF16) as W1,
              nc.sbuf_tensor(_u("W2"), [128, 22528], BF16) as W2,
              nc.sbuf_tensor(_u("W3"), [128, 2816], BF16) as W3,
              nc.sbuf_tensor(_u("XA"), [128, 2816], F32) as XA,
              nc.sbuf_tensor(_u("XC"), [128, 4096], F32) as XC,
              nc.sbuf_tensor(_u("XD"), [128, 1536], F32) as XD,
              nc.sbuf_tensor(_u("XB"), [128, 8448], F32) as XB,
              nc.sbuf_tensor(_u("B16A"), [128, 2816], BF16) as B16A,
              nc.sbuf_tensor(_u("B16B"), [128, 11264], BF16) as B16B,
              nc.psum_tensor(_u("PSG0"), [128, 512], F32) as PSG0,
              nc.psum_tensor(_u("PSG1"), [128, 512], F32) as PSG1,
              nc.psum_tensor(_u("PSG2"), [128, 512], F32) as PSG2,
              nc.psum_tensor(_u("PSG3"), [128, 512], F32) as PSG3,
              nc.psum_tensor(_u("PSG4"), [128, 512], F32) as PSG4,
              nc.psum_tensor(_u("PSG5"), [128, 512], F32) as PSG5,
              nc.psum_tensor(_u("PSD0"), [128, 512], F32) as PSD0,
              nc.psum_tensor(_u("PSD1"), [128, 512], F32) as PSD1):
            PSG = [PSG0, PSG1, PSG2, PSG3, PSG4, PSG5]
            for kc in (range(8) if int(os.environ.get("P3B_LVL", "9")) >= 1 else []):
                ctx.dma("sync", W1[:, kc * 2816:(kc + 1) * 2816],
                        aps["wgqT_d"][kc * 128:(kc + 1) * 128, :],
                        reads=["wgqT_d_d"], writes=["p3w1"])
                ctx.dma("sync", W2[:, kc * 2816:(kc + 1) * 2816],
                        aps["wuqT_d"][kc * 128:(kc + 1) * 128, :],
                        reads=["wuqT_d_d"], writes=["p3w2"])
            for st in range(N // 512):
                n0 = st * 512
                for c in range(8):
                    ctx.dmat("sync", B16B[:, c * 512:(c + 1) * 512],
                             aps["xq_tm"][n0:n0 + 512, c * 128:(c + 1) * 128],
                             reads=["xq_tm_d"], writes=["p3xqT"])
                for ns in range(4):
                    mgs = [(0, 512), (512, 1024), (1024, 1536), (1536, 2048), (2048, 2560), (2560, 2816)]
                    for W_, tag in ((W1, "p3w1"), (W2, "p3w2")):
                        def mmg(e, W_=W_, ns=ns):
                            ins = None
                            for gi, (a, b) in enumerate(mgs):
                                for k in range(8):
                                    ins = e.matmul(PSG[gi][:, 0:b - a],
                                                   B16B[:, k * 512 + ns * 128: k * 512 + ns * 128 + 128],
                                                   W_[:, k * 2816 + a: k * 2816 + b],
                                                   start=(k == 0), stop=(k == 7))
                            return ins
                        if W_ is W1:
                            ctx.op("pe", mmg, reads=["p3xqT", tag], writes=["PSGx"])
                            for gi, (a, b) in enumerate(mgs):
                                ctx.op("act", lambda e, gi=gi, a=a, b=b: e.activation(
                                    out=XA[:, a:b], in_=PSG[gi][:, 0:b - a], func=AF.Silu),
                                       reads=["PSGx"], writes=["p3sg"])
                        else:
                            ctx.op("pe", mmg, reads=["p3xqT", tag, "p3sg"], writes=["PSGx"])
                            for gi, (a, b) in enumerate(mgs):
                                ctx.op("dve", lambda e, gi=gi, a=a, b=b: e.tensor_tensor(
                                    out=B16A[:, a:b], in0=XA[:, a:b], in1=PSG[gi][:, 0:b - a],
                                    op=ALU.mult),
                                       reads=["PSGx", "p3sg"], writes=["p3h"])
                    quant("dve", B16A[:, 0:2816], B16A[:, 0:2816], 2816, SMALL[:, 64:416],
                          XB[:, 0:2816], XB[:, 2816:5632], XB[:, 5632:8448], "p3hq",
                          ["p3h"], ["p3hqb", "p3h"])
                    ctx.dma("act", aps["hq_st"][ns * 128:(ns + 1) * 128, :], B16A[:, 0:2816],
                            reads=["p3hqb", "p3h"], writes=["hq_st_d"])
                for mc in range(22):
                    ctx.dmat("sync", B16B[:, mc * 512:(mc + 1) * 512],
                             aps["hq_st"][0:512, mc * 128:(mc + 1) * 128],
                             reads=["hq_st_d"], writes=["p3hqT", "p3xqT"])
                for dc in range(8):
                    ctx.dma("sync", W3[:, 0:2816],
                            aps["wdqT_d"][dc].rearrange("(mc p) d -> p mc d", p=128),
                            reads=["wdqT_d_d", "p3dw"], writes=["p3w3"])
                    psd = PSD0 if dc % 2 == 0 else PSD1
                    pdb = "PSD%d" % (dc % 2)
                    def mmd(e, psd=psd):
                        ins = None
                        for mc in range(22):
                            ins = e.matmul(psd[:], W3[:, mc * 128:(mc + 1) * 128],
                                           B16B[:, mc * 512:(mc + 1) * 512],
                                           start=(mc == 0), stop=(mc == 21))
                        return ins
                    ctx.op("pe", mmd, reads=["p3w3", "p3hqT"], writes=[pdb])
                    ctx.op("act", lambda e, psd=psd: e.activation(out=XD[:, 0:512], in_=psd[:],
                                                                  func=AF.Silu),
                           reads=[pdb], writes=["p3g0"])
                    ctx.dma("sync", XD[:, 512:1024], aps["s0_f"][dc * 128:(dc + 1) * 128, n0:n0 + 512],
                            reads=["s0_d"], writes=["p3s0l"])
                    ctx.op("dve", lambda e: e.tensor_tensor(out=XD[:, 1024:1536], in0=XD[:, 0:512],
                                                            in1=XD[:, 512:1024], op=ALU.mult),
                           reads=["p3g0", "p3s0l"], writes=["p3sg0"])
                    ctx.dma("sync", XD[:, 0:512], aps["x_fm"][dc * 128:(dc + 1) * 128, n0:n0 + 512],
                            writes=["p3xl", "p3g0"])
                    ctx.op("dve", lambda e, dc=dc: e.tensor_tensor(
                        out=XC[:, dc * 512:(dc + 1) * 512], in0=XD[:, 1024:1536],
                        in1=XD[:, 0:512], op=ALU.add),
                           reads=["p3sg0", "p3xl"], writes=["p3x1"])
                    ctx.dma("act", aps["x1_f"][dc * 128:(dc + 1) * 128, n0:n0 + 512],
                            XC[:, dc * 512:(dc + 1) * 512], reads=["p3x1"], writes=["x1_d"])
                def stats(src_bufname, srcap_fn, mu_off, sq_tag, sq_fn=None):
                    if sq_fn is None:
                        sq_fn = lambda dc: XB[:, dc * 512:(dc + 1) * 512]
                    def mmu(e):
                        ins = None
                        for dc in range(8):
                            ins = e.matmul(PSD0[0:1, :], ONE[:, 0:1], srcap_fn(dc),
                                           start=(dc == 0), stop=(dc == 7))
                        return ins
                    ctx.op("pe", mmu, reads=[src_bufname, "ONE", "p4mu"], writes=["PSD0"])
                    for dc in range(8):
                        ctx.op("act", lambda e, dc=dc: e.activation(
                            out=sq_fn(dc), in_=srcap_fn(dc), func=AF.Square),
                               reads=[src_bufname], writes=[sq_tag])
                    def mmsq(e):
                        ins = None
                        for dc in range(8):
                            ins = e.matmul(PSD1[0:1, :], ONE[:, 0:1],
                                           sq_fn(dc),
                                           start=(dc == 0), stop=(dc == 7))
                        return ins
                    ctx.op("pe", mmsq, reads=[sq_tag, "ONE", "p4mu"], writes=["PSD1"])
                    ctx.op("dve", lambda e: e.tensor_scalar_mul(out=XA[0:1, 0:512],
                                                                in0=PSD0[0:1, :],
                                                                scalar1=1.0 / 1024.0),
                           reads=["PSD0"], writes=["p4mu"])
                    ctx.op("dve", lambda e: e.tensor_scalar_mul(out=XA[0:1, 512:1024],
                                                                in0=PSD1[0:1, :],
                                                                scalar1=1.0 / 1024.0),
                           reads=["PSD1"], writes=["p4mu"])
                    ctx.op("dve", lambda e: e.tensor_tensor(out=XA[0:1, 1024:1536],
                                                            in0=XA[0:1, 0:512],
                                                            in1=XA[0:1, 0:512], op=ALU.mult),
                           reads=["p4mu"], writes=["p4mu"])
                    ctx.op("dve", lambda e: e.tensor_tensor(out=XA[0:1, 512:1024],
                                                            in0=XA[0:1, 512:1024],
                                                            in1=XA[0:1, 1024:1536],
                                                            op=ALU.subtract),
                           reads=["p4mu"], writes=["p4mu"])
                    ctx.op("dve", lambda e: e.tensor_scalar(out=XA[0:1, 512:1024],
                                                            in0=XA[0:1, 512:1024],
                                                            scalar1=1e-5, scalar2=None,
                                                            op0=ALU.add),
                           reads=["p4mu"], writes=["p4mu"])
                    rsqrt(XA[0:1, 1024:1536], XA[0:1, 512:1024], XA[0:1, 1536:2048],
                          XA[0:1, 2048:2560], ["p4mu"], ["p4mu"], nrows=1)
                    ctx.dma("act", aps[mu_off][0:1, n0:n0 + 512], XA[0:1, 0:512],
                            reads=["p4mu"], writes=[mu_off + "_d"])
                    ctx.dma("act", aps[mu_off][1:2, n0:n0 + 512], XA[0:1, 1024:1536],
                            reads=["p4mu"], writes=[mu_off + "_d"])
                    mb = bass.AP(aps[mu_off].tensor, n0, [[0, 128], [1, 512]])
                    rb = bass.AP(aps[mu_off].tensor, N + n0, [[0, 128], [1, 512]])
                    ctx.dma("sync", XD[:, 0:512], mb, reads=[mu_off + "_d"], writes=["p4mb"])
                    ctx.dma("sync", XD[:, 512:1024], rb, reads=[mu_off + "_d"], writes=["p4rb"])
                stats("p3x1", lambda dc: XC[:, dc * 512:(dc + 1) * 512], "mu_row", "p4sq")
                for dc in range(8):
                    ctx.op("dve", lambda e, dc=dc: e.tensor_tensor(
                        out=XB[:, dc * 512:(dc + 1) * 512], in0=XC[:, dc * 512:(dc + 1) * 512],
                        in1=XD[:, 0:512], op=ALU.subtract),
                           reads=["p3x1", "p4mb", "p4sq"], writes=["p4xn2"])
                    ctx.op("dve", lambda e, dc=dc: e.tensor_tensor(
                        out=XB[:, dc * 512:(dc + 1) * 512], in0=XB[:, dc * 512:(dc + 1) * 512],
                        in1=XD[:, 512:1024], op=ALU.mult),
                           reads=["p4xn2", "p4rb"], writes=["p4xn2"])
                    ctx.dma("act", aps["xn2_f"][dc * 128:(dc + 1) * 128, n0:n0 + 512],
                            XB[:, dc * 512:(dc + 1) * 512], reads=["p4xn2"], writes=["xn2f_d"])
                    ctx.op("dve", lambda e, dc=dc: e.tensor_copy(
                        out=B16B[:, dc * 512:(dc + 1) * 512], in_=XB[:, dc * 512:(dc + 1) * 512]),
                           reads=["p4xn2"], writes=["p4xn2b", "p3hqT", "p3xqT"])
                    ctx.dma("act", aps["xn2_bf"][dc * 128:(dc + 1) * 128, n0:n0 + 512],
                            B16B[:, dc * 512:(dc + 1) * 512], reads=["p4xn2b"], writes=["xn2b_d"])
                stats("p4xn2", lambda dc: XB[:, dc * 512:(dc + 1) * 512], "mu2_row", "p4sq2",
                      sq_fn=lambda dc: XB[:, 4096 + dc * 512:4096 + (dc + 1) * 512])
                for dc in range(8):
                    ctx.op("dve", lambda e, dc=dc: e.tensor_tensor(
                        out=XC[:, dc * 512:(dc + 1) * 512], in0=XB[:, dc * 512:(dc + 1) * 512],
                        in1=XD[:, 0:512], op=ALU.subtract),
                           reads=["p4xn2", "p4mb", "p3x1"], writes=["p4xni"])
                    ctx.op("dve", lambda e, dc=dc: e.tensor_tensor(
                        out=XC[:, dc * 512:(dc + 1) * 512], in0=XC[:, dc * 512:(dc + 1) * 512],
                        in1=XD[:, 512:1024], op=ALU.mult),
                           reads=["p4xni", "p4rb"], writes=["p4xni"])
                    ctx.op("dve", lambda e, dc=dc: e.tensor_scalar(
                        out=B16B[:, 4096 + dc * 512:4096 + (dc + 1) * 512],
                        in0=XC[:, dc * 512:(dc + 1) * 512],
                        scalar1=CONST[:, 128 + dc:129 + dc], scalar2=CONST[:, 136 + dc:137 + dc],
                        op0=ALU.mult, op1=ALU.add),
                           reads=["p4xni", "CONST", "p4xn2b"], writes=["p4xnib", "p3hqT"])
                    ctx.dma("act", aps["xni_bf"][dc * 128:(dc + 1) * 128, n0:n0 + 512],
                            B16B[:, 4096 + dc * 512:4096 + (dc + 1) * 512],
                            reads=["p4xnib"], writes=["xni_d"])
        ctx.barrier()

    def _ph_P4ba():
        with (nc.sbuf_tensor(_u("W1"), [128, 32768], BF16) as W1,
              nc.sbuf_tensor(_u("WS"), [128, 2048], BF16) as WS,
              nc.sbuf_tensor(_u("WS2"), [64, 2048], BF16) as WS2,
              nc.sbuf_tensor(_u("XA"), [128, 8192], F32) as XA,
              nc.sbuf_tensor(_u("XE"), [128, 3584], F32) as XE,
              nc.sbuf_tensor(_u("HALO"), [128, 48], F32) as HALO,
              nc.sbuf_tensor(_u("B16A"), [128, 8192], BF16) as B16A,
              nc.sbuf_tensor(_u("B16B"), [128, 8192], BF16) as B16B,
              nc.sbuf_tensor(_u("DTB"), [64, 512], BF16) as DTB,
              nc.sbuf_tensor(_u("DT16"), [128, 512], BF16) as DT16,
              nc.psum_tensor(_u("PSI0"), [128, 512], F32) as PSI0,
              nc.psum_tensor(_u("PSI1"), [128, 512], F32) as PSI1,
              nc.psum_tensor(_u("PSDT"), [64, 512], F32) as PSDT,
              nc.psum_tensor(_u("PSB"), [16, 512], F32) as PSB,
              nc.psum_tensor(_u("PSC"), [16, 512], F32) as PSC,
              nc.psum_tensor(_u("PSD2"), [128, 512], F32) as PSD2):
            for kc in range(8):
                ctx.dma("pool", W1[:, kc * 4096:(kc + 1) * 4096],
                        aps["in_w"][kc * 128:(kc + 1) * 128, :],
                        reads=["p4w1"], writes=["p4w1"])
            for kc in range(16):
                ctx.dma("pool", WS[:, kc * 96:kc * 96 + 64],
                        aps["xw_dt"][kc * 128:(kc + 1) * 128, :],
                        reads=["p4ws"], writes=["p4ws"])
                ctx.dma("pool", WS[:, kc * 96 + 64:kc * 96 + 80],
                        aps["xw_B"][kc * 128:(kc + 1) * 128, :],
                        reads=["p4ws"], writes=["p4ws"])
                ctx.dma("pool", WS[:, kc * 96 + 80:kc * 96 + 96],
                        aps["xw_C"][kc * 128:(kc + 1) * 128, :],
                        reads=["p4ws"], writes=["p4ws"])
            ctx.dma("pool", WS2[:], aps["dt_w"][:], reads=["p4ws2"], writes=["p4ws2"])
            ctx.op("dve", lambda e: e.memset(HALO[:], 0.0), writes=["p4halo"])
            for st in range(N // 512):
                n0 = st * 512
                for c in range(8):
                    ctx.dma("sync", B16A[:, c * 512:(c + 1) * 512],
                            aps["xni_bf"][c * 128:(c + 1) * 128, n0:n0 + 512],
                            reads=["xni_d"], writes=["p4xni"])
                for mc in range(32):
                    ps = PSI0 if mc % 2 == 0 else PSI1
                    pb = "PSI%d" % (mc % 2)
                    def mmi(e, mc=mc, ps=ps):
                        ins = None
                        for k in range(8):
                            ins = e.matmul(ps[:], W1[:, k * 4096 + mc * 128: k * 4096 + (mc + 1) * 128],
                                           B16A[:, k * 512:(k + 1) * 512],
                                           start=(k == 0), stop=(k == 7))
                        return ins
                    ctx.op("pe", mmi, reads=["p4w1", "p4xni"], writes=[pb])
                    if mc < 16:
                        ctx.op("dve", lambda e, mc=mc, ps=ps: e.tensor_copy(
                            out=XA[:, mc * 512:(mc + 1) * 512], in_=ps[:]),
                               reads=[pb], writes=["p4xp", "p4st"])
                    else:
                        ctx.op("act", lambda e, mc=mc, ps=ps: e.activation(
                            out=B16B[:, (mc - 16) * 512:(mc - 15) * 512], in_=ps[:], func=AF.Silu),
                               reads=[pb], writes=["p4sz"])
                for c in range(16):
                    ctx.dma("act", aps["siluz"][c * 128:(c + 1) * 128, n0:n0 + 512],
                            B16B[:, c * 512:(c + 1) * 512], reads=["p4sz"], writes=["siluz_d"])
                for c in range(16):
                    ext = XE[:, 0:515]
                    ctx.op("dve", lambda e, c=c: e.tensor_copy(out=XE[:, 0:3],
                                                               in_=HALO[:, c * 3:(c + 1) * 3]),
                           reads=["p4halo"], writes=["p4ext", "p4B", "p4C"])
                    ctx.op("dve", lambda e, c=c: e.tensor_copy(out=XE[:, 3:515],
                                                               in_=XA[:, c * 512:(c + 1) * 512]),
                           reads=["p4xp"], writes=["p4ext"])
                    ctx.op("dve", lambda e, c=c: e.tensor_copy(out=HALO[:, c * 3:(c + 1) * 3],
                                                               in_=XA[:, c * 512 + 509:c * 512 + 512]),
                           reads=["p4xp", "p4ext"], writes=["p4halo"])
                    ctx.op("dve", lambda e, c=c: e.tensor_scalar(out=XE[:, 515:1027],
                                                                 in0=XE[:, 0:512],
                                                                 scalar1=CONST[:, c * 4:c * 4 + 1],
                                                                 scalar2=None, op0=ALU.mult),
                           reads=["p4ext", "CONST"], writes=["p4cv"])
                    for kk in range(1, 4):
                        ctx.op("dve", lambda e, c=c, kk=kk: e.scalar_tensor_tensor(
                            out=XE[:, 515:1027], in0=XE[:, kk:kk + 512],
                            scalar=CONST[:, c * 4 + kk:c * 4 + kk + 1],
                            in1=XE[:, 515:1027], op0=ALU.mult, op1=ALU.add),
                               reads=["p4ext", "CONST", "p4cv"], writes=["p4cv"])
                    ctx.op("act", lambda e, c=c: e.activation(out=XE[:, 1027:1539],
                                                              in_=XE[:, 515:1027], func=AF.Silu,
                                                              bias=CONST[:, 64 + c:65 + c]),
                           reads=["p4cv", "CONST"], writes=["p4xc"])
                    ctx.dma("act", aps["xc_f"][c * 128:(c + 1) * 128, n0:n0 + 512],
                            XE[:, 1027:1539], reads=["p4xc"], writes=["xc_d"])
                    ctx.op("dve", lambda e, c=c: e.tensor_copy(
                        out=B16A[:, c * 512:(c + 1) * 512], in_=XE[:, 1027:1539]),
                           reads=["p4xc"], writes=["p4xcb", "p4xni"])
                def mmx(e):
                    ins = None
                    for k in range(16):
                        ins = e.matmul(PSDT[:], WS[:, k * 96:k * 96 + 64],
                                       B16A[:, k * 512:(k + 1) * 512],
                                       start=(k == 0), stop=(k == 15))
                    for k in range(16):
                        ins = e.matmul(PSB[:], WS[:, k * 96 + 64:k * 96 + 80],
                                       B16A[:, k * 512:(k + 1) * 512],
                                       start=(k == 0), stop=(k == 15))
                    for k in range(16):
                        ins = e.matmul(PSC[:], WS[:, k * 96 + 80:k * 96 + 96],
                                       B16A[:, k * 512:(k + 1) * 512],
                                       start=(k == 0), stop=(k == 15))
                    return ins
                ctx.op("pe", mmx, reads=["p4ws", "p4xcb", "p4xni"], writes=["PSX"])
                ctx.op("dve", lambda e: e.tensor_copy(out=XE[0:16, 0:512], in_=PSB[:]),
                       reads=["PSX"], writes=["p4B"])
                ctx.dma("act", aps["B_d"][:, n0:n0 + 512], XE[0:16, 0:512],
                        reads=["p4B"], writes=["B_dd"])
                ctx.op("dve", lambda e: e.tensor_copy(out=XE[0:16, 512:1024], in_=PSC[:]),
                       reads=["PSX", "p4B"], writes=["p4C"])
                ctx.dma("act", aps["C_d"][:, n0:n0 + 512], XE[0:16, 512:1024],
                        reads=["p4C"], writes=["C_dd"])
                ctx.op("dve", lambda e: e.tensor_copy(out=DTB[:], in_=PSDT[:]),
                       reads=["PSX"], writes=["p4dtb"])
                for c in range(16):
                    def mmdt(e, c=c):
                        return e.matmul(PSD2[:], WS2[:, c * 128:(c + 1) * 128], DTB[:],
                                        start=True, stop=True)
                    ctx.op("pe", mmdt, reads=["p4ws2", "p4dtb"], writes=["PSDc"])
                    ctx.op("dve", lambda e, c=c: e.tensor_scalar(
                        out=XE[:, 1664:2176], in0=PSD2[:], scalar1=CONST[:, 80 + c:81 + c],
                        scalar2=85.0, op0=ALU.add, op1=ALU.min),
                           reads=["PSDc", "CONST"], writes=["p4dta", "p4dt"])
                    ctx.op("act", lambda e: e.activation(out=XE[:, 2176:2688], in_=XE[:, 1664:2176],
                                                         func=AF.Exp),
                           reads=["p4dta"], writes=["p4dte"])
                    ctx.op("act", lambda e: e.activation(out=XE[:, 1664:2176], in_=XE[:, 2176:2688],
                                                         func=AF.Ln, bias=1.0),
                           reads=["p4dte", "p4dta"], writes=["p4dt"])
                    ctx.dma("act", aps["dt_f"][c * 128:(c + 1) * 128, n0:n0 + 512],
                            XE[:, 1664:2176], reads=["p4dt"], writes=["dt_dd"])
                    ctx.op("dve", lambda e, c=c: e.tensor_tensor(
                        out=DT16[:, 0:512], in0=XE[:, 1664:2176],
                        in1=B16A[:, c * 512:(c + 1) * 512], op=ALU.mult),
                           reads=["p4dt", "p4xcb", "p4dtu", "p4xni"], writes=["p4dtu"])
                    ctx.dma("act", aps["dtu_f"][c * 128:(c + 1) * 128, n0:n0 + 512],
                            DT16[:, 0:512], reads=["p4dtu"], writes=["dtu_dd"])
        ctx.barrier()

    def _ph_P4ca():
        with (nc.sbuf_tensor(_u("WSR"), [128, 64], BF16) as WSR,
              nc.sbuf_tensor(_u("XA"), [128, 1024], F32) as XA,
              nc.sbuf_tensor(_u("B16A"), [128, 4096], BF16) as B16A,
              nc.psum_tensor(_u("PSR"), [8, 512], F32) as PSR,
              nc.psum_tensor(_u("PSZ2"), [128, 8], F32) as PSZ2,
              nc.psum_tensor(_u("PSZ3"), [8, 128], F32) as PSZ3):
            for kc in range(8):
                ctx.dma("pool", WSR[:, kc * 8:(kc + 1) * 8],
                        aps["Wr"][kc * 128:(kc + 1) * 128, :],
                        reads=["p4wr"], writes=["p4wr"])
            for st in range(N // 512):
                n0 = st * 512
                for c in range(8):
                    ctx.dma("sync", B16A[:, c * 512:(c + 1) * 512],
                            aps["xn2_bf"][c * 128:(c + 1) * 128, n0:n0 + 512],
                            reads=["xn2b_d"], writes=["p4cx"])
                def mmr(e):
                    ins = None
                    for k in range(8):
                        ins = e.matmul(PSR[:], WSR[:, k * 8:(k + 1) * 8],
                                       B16A[:, k * 512:(k + 1) * 512],
                                       start=(k == 0), stop=(k == 7))
                    return ins
                ctx.op("pe", mmr, reads=["p4wr", "p4cx"], writes=["PSR"])
                ctx.op("dve", lambda e: e.tensor_copy(out=PP[0:8, 0:512], in_=PSR[:]),
                       reads=["PSR"], writes=["p4lg"])
                for s in range(4):
                    ctx.op("pe", lambda e, s=s: e.transpose(PSZ2[:], PP[0:8, s * 128:(s + 1) * 128],
                                                            IDT[0:8, 0:8]),
                           reads=["p4lg", "IDT", "p4sm"], writes=["PSZ2b"])
                    p = XA[:, 0:8]
                    ctx.op("dve", lambda e: e.tensor_copy(out=p, in_=PSZ2[:]),
                           reads=["PSZ2b"], writes=["p4sm"])
                    ctx.op("dve", lambda e: e.reduce_max(out=SMALL[:, 0:1], in_=p, axis=AX.X),
                           reads=["p4sm"], writes=["SMALL"])
                    ctx.op("dve", lambda e: e.tensor_scalar(out=p, in0=p, scalar1=SMALL[:, 0:1],
                                                            scalar2=None, op0=ALU.subtract),
                           reads=["p4sm", "SMALL"], writes=["p4sm"])
                    ctx.op("act", lambda e: e.activation(out=p, in_=p, func=AF.Exp),
                           reads=["p4sm"], writes=["p4sm"])
                    ctx.op("dve", lambda e: e.reduce_sum(out=SMALL[:, 0:1], in_=p, axis=AX.X),
                           reads=["p4sm"], writes=["SMALL"])
                    ctx.op("dve", lambda e: e.reciprocal(out=SMALL[:, 1:2], in_=SMALL[:, 0:1]),
                           reads=["SMALL"], writes=["SMALL"])
                    ctx.op("dve", lambda e: e.tensor_scalar(out=p, in0=p, scalar1=SMALL[:, 1:2],
                                                            scalar2=None, op0=ALU.mult),
                           reads=["p4sm", "SMALL"], writes=["p4sm"])
                    ctx.op("dve", lambda e: e.reduce_max(out=SMALL[:, 2:3], in_=p, axis=AX.X),
                           reads=["p4sm"], writes=["SMALL"])
                    eq1 = XA[:, 8:16]
                    ctx.op("dve", lambda e, eq1=eq1: e.tensor_scalar(out=eq1, in0=p,
                                                                     scalar1=SMALL[:, 2:3],
                                                                     scalar2=None, op0=ALU.is_ge),
                           reads=["p4sm", "SMALL"], writes=["p4eq1"])
                    pm = XA[:, 16:24]
                    ctx.op("dve", lambda e, pm=pm, eq1=eq1: e.tensor_tensor(out=pm, in0=p, in1=eq1,
                                                                            op=ALU.mult),
                           reads=["p4sm", "p4eq1"], writes=["p4pm"])
                    ctx.op("dve", lambda e, pm=pm: e.tensor_tensor(out=pm, in0=p, in1=pm,
                                                                   op=ALU.subtract),
                           reads=["p4sm", "p4pm"], writes=["p4pm"])
                    ctx.op("dve", lambda e, pm=pm: e.reduce_max(out=SMALL[:, 3:4], in_=pm,
                                                                axis=AX.X),
                           reads=["p4pm"], writes=["SMALL"])
                    ctx.op("dve", lambda e: e.tensor_tensor(out=SMALL[:, 4:5], in0=SMALL[:, 2:3],
                                                            in1=SMALL[:, 3:4], op=ALU.add),
                           reads=["SMALL"], writes=["SMALL"])
                    ctx.op("dve", lambda e: e.tensor_scalar(out=SMALL[:, 4:5], in0=SMALL[:, 4:5],
                                                            scalar1=1e-8, scalar2=None,
                                                            op0=ALU.add),
                           reads=["SMALL"], writes=["SMALL"])
                    ctx.op("dve", lambda e: e.reciprocal(out=SMALL[:, 5:6], in_=SMALL[:, 4:5]),
                           reads=["SMALL"], writes=["SMALL"])
                    ctx.op("dve", lambda e: e.tensor_tensor(out=SMALL[:, 6:7], in0=SMALL[:, 2:3],
                                                            in1=SMALL[:, 5:6], op=ALU.mult),
                           reads=["SMALL"], writes=["SMALL"])
                    ctx.op("dve", lambda e: e.tensor_tensor(out=SMALL[:, 7:8], in0=SMALL[:, 3:4],
                                                            in1=SMALL[:, 5:6], op=ALU.mult),
                           reads=["SMALL"], writes=["SMALL"])
                    eq2 = XA[:, 24:32]
                    ctx.op("dve", lambda e, eq2=eq2, pm=pm: e.tensor_scalar(
                        out=eq2, in0=pm, scalar1=SMALL[:, 3:4], scalar2=None, op0=ALU.is_ge),
                           reads=["p4pm", "SMALL"], writes=["p4eq2"])
                    ew = XA[:, 32:40]
                    ctx.op("dve", lambda e, ew=ew, eq1=eq1: e.tensor_scalar(
                        out=ew, in0=eq1, scalar1=SMALL[:, 6:7], scalar2=None, op0=ALU.mult),
                           reads=["p4eq1", "SMALL"], writes=["p4ew"])
                    ctx.op("dve", lambda e, ew=ew, eq2=eq2: e.scalar_tensor_tensor(
                        out=ew, in0=eq2, scalar=SMALL[:, 7:8], in1=ew, op0=ALU.mult, op1=ALU.add),
                           reads=["p4eq2", "SMALL", "p4ew"], writes=["p4ew"])
                    ctx.op("pe", lambda e, ew=ew: e.transpose(PSZ3[:], ew, IDT[:]),
                           reads=["p4ew", "IDT", "p4ewr"], writes=["PSZ3b"])
                    ctx.op("dve", lambda e, s=s: e.tensor_copy(out=PP[64:72, s * 128:(s + 1) * 128],
                                                               in_=PSZ3[:]),
                           reads=["PSZ3b"], writes=["p4ewr"])
                ctx.dma("act", aps["ew_rows"][:, n0:n0 + 512], PP[64:72, 0:512],
                        reads=["p4ewr"], writes=["ew_d"])
        ctx.barrier()

    def _ph_P67():
        GU = NE * D * ME
        NTC = N // TC
        with contextlib.ExitStack() as _st:
            _a = lambda nm, shape, dt: _st.enter_context(nc.sbuf_tensor(_u(nm), shape, dt))
            _p = lambda nm, shape: _st.enter_context(nc.psum_tensor(_u(nm), shape, F32))
            XA = _a("XA", [128, 4096], F32)
            XB = _a("XB", [128, 8192], BF16)
            XCB = _a("XCB", [128, 12288], BF16)
            TREE = _a("TREE", [128, 4096], BF16)
            XD = _a("XD", [128, 1024], F32)
            DTB = _a("DTB", [128, 512], BF16)
            SCB = _a("SCB", [128, 1024], BF16)
            ASB = _a("ASB", [128, 256], F32)
            HC = _a("HC", [128, 256], F32)
            W1 = _a("W1", [128, 16384], BF16)
            W2 = _a("W2", [128, 16384], BF16)
            H16 = _a("H16", [128, 8192], BF16)
            X16 = _a("X16", [128, 4096], BF16)
            XF = _a("XF", [128, 4608], F32)
            EWBP = [_a("EWB%d" % i, [128, 512], F32) for i in range(2)]
            EWB16 = _a("EWB16", [8, 512], BF16)
            BOE = _a("BOE", [8, 1024], BF16)
            PSGL = [_p("PSG%d" % i, [128, 512]) for i in range(3)]
            PSUL = [_p("PSU%d" % i, [128, 512]) for i in range(3)]
            PSOL = [_p("PSO%d" % i, [128, 512]) for i in range(2)]
            WSLOT = [(W1, 0), (W1, 8192), (W2, 0), (W2, 8192)]
            for dc in range(16):
                ctx.dma("sync", SMALL[:, 0:16], aps["A_log"][dc * 128:(dc + 1) * 128, :],
                        reads=["p6al"], writes=["p6al"])
                ctx.op("act", lambda e, dc=dc: e.activation(out=ASB[:, dc * 16:(dc + 1) * 16],
                                                            in_=SMALL[:, 0:16], func=AF.Exp),
                       reads=["p6al"], writes=["ASB"])
            ctx.op("dve", lambda e: e.tensor_scalar_mul(out=ASB[:], in0=ASB[:], scalar1=-1.0),
                   reads=["ASB"], writes=["ASB"])
            ctx.op("dve", lambda e: e.memset(HC[:], 0.0), writes=["HC"])
            ctx.dma("pool", BOE[:], aps["bo_e"][:], writes=["p7boe"])

            def su_prolog(tc):
                t0 = tc * TC
                bb = bass.AP(aps["B_d"].tensor, t0, [[0, 128], [N, DS], [1, TC]])
                cb = bass.AP(aps["C_d"].tensor, t0, [[0, 128], [N, DS], [1, TC]])
                ctx.dma("pool", XCB[:, 0:DS * TC].rearrange("p (n t) -> p n t", t=TC), bb,
                        reads=["B_dd"], writes=["p6Bb"])
                ctx.dma("pool", XCB[:, DS * TC:2 * DS * TC].rearrange("p (n t) -> p n t", t=TC),
                        cb, reads=["C_dd"], writes=["p6Cb"])

            def su_dc(tc, dc):
                t0 = tc * TC
                r0 = dc * 128
                par = dc % 2
                xo = par * 2 * TC
                ho = par * DS * TC
                so = par * 2 * TC
                dtt = "p6dt%d" % par
                dut = "p6dtu%d" % par
                ut = "p6u%d" % par
                szt = "p6sz%d" % par
                hst = lambda g: "p6hs%d_%d" % (g, par)
                units = []

                def u_load():
                    ctx.dma("sync", XD[:, xo:xo + TC], aps["dt_f"][r0:r0 + 128, t0:t0 + TC],
                            reads=["dt_dd"], writes=[dtt])
                    ctx.dma("sync", DTB[:, par * TC:(par + 1) * TC],
                            aps["dtu_f"][r0:r0 + 128, t0:t0 + TC],
                            reads=["dtu_dd"], writes=[dut])
                    ctx.dma("sync", XD[:, xo + TC:xo + 2 * TC],
                            aps["xc_f"][r0:r0 + 128, t0:t0 + TC],
                            reads=["xc_d"], writes=[ut])
                    ctx.dma("sync", SCB[:, so:so + TC], aps["siluz"][r0:r0 + 128, t0:t0 + TC],
                            reads=["siluz_d"], writes=[szt])
                units.append(u_load)

                def mk_exp(g):
                    def u_exp():
                        def fn(e, g=g, dc=dc):
                            ins = None
                            for n_ in range(4 * g, 4 * g + 4):
                                ins = e.activation(
                                    out=XA[:, n_ * TC:(n_ + 1) * TC], in_=XD[:, xo:xo + TC],
                                    func=AF.Exp,
                                    scale=ASB[:, dc * 16 + n_:dc * 16 + n_ + 1])
                            return ins
                        ctx.op("act", fn, reads=[dtt, "ASB"], writes=["p6dA%d" % g])
                    return u_exp
                units.append(mk_exp(0))
                units.append(mk_exp(1))

                def u_dBu():
                    dtu_b = DTB[:, par * TC:(par + 1) * TC].unsqueeze(1).broadcast_to(
                        (128, DS, TC))
                    ctx.op("pool", lambda e, dtu_b=dtu_b: e.tensor_tensor(
                        out=XCB[:, 2 * DS * TC:3 * DS * TC].rearrange("p (n t) -> p n t", t=TC),
                        in0=dtu_b, in1=XCB[:, 0:DS * TC].rearrange("p (n t) -> p n t", t=TC),
                        op=ALU.mult),
                           reads=[dut, "p6Bb"], writes=["p6dBu"])
                units.append(u_dBu)
                units.append(mk_exp(2))
                units.append(mk_exp(3))

                def mk_scan(g, hh):
                    def u_scan():
                        def fn(e, g=g, hh=hh, dc=dc):
                            ins = None
                            for n_ in range(4 * g + 2 * hh, 4 * g + 2 * hh + 2):
                                ins = e.tensor_tensor_scan(
                                    out=XB[:, ho + n_ * TC:ho + (n_ + 1) * TC],
                                    data0=XA[:, n_ * TC:(n_ + 1) * TC],
                                    data1=XCB[:, 2 * DS * TC + n_ * TC:
                                              2 * DS * TC + (n_ + 1) * TC],
                                    initial=HC[:, dc * 16 + n_:dc * 16 + n_ + 1],
                                    op0=ALU.mult, op1=ALU.add)
                            return ins
                        ctx.op("dve", fn, reads=["p6dA%d" % g, "p6dBu", "HC"],
                               writes=[hst(g)])
                    return u_scan
                for g in range(4):
                    units.append(mk_scan(g, 0))
                    units.append(mk_scan(g, 1))

                def u_hc():
                    hs3 = XB[:, ho:ho + DS * TC].rearrange("p (n t) -> p n t", t=TC)
                    ctx.op("act", lambda e, hs3=hs3, dc=dc: e.copy(
                        out=HC[:, dc * 16:(dc + 1) * 16], in_=hs3[:, :, TC - 1]),
                           reads=[hst(0), hst(1), hst(2), hst(3)], writes=["HC"])
                    ctx.op("pool", lambda e: e.tensor_tensor(
                        out=XCB[:, 2 * DS * TC:3 * DS * TC],
                        in0=XB[:, ho:ho + DS * TC],
                        in1=XCB[:, DS * TC:2 * DS * TC], op=ALU.mult),
                           reads=[hst(0), hst(1), hst(2), hst(3), "p6Cb"],
                           writes=["p6hc", "p6dBu"])
                units.append(u_hc)

                def u_l01():
                    ctx.op("pool", lambda e: e.tensor_tensor(
                        out=TREE[:, 0:8 * TC],
                        in0=XCB[:, 2 * DS * TC:2 * DS * TC + 8 * TC],
                        in1=XCB[:, 2 * DS * TC + 8 * TC:2 * DS * TC + 16 * TC],
                        op=ALU.add),
                           reads=["p6hc"], writes=["p6t0"])
                    ctx.op("pool", lambda e: e.tensor_tensor(
                        out=TREE[:, 8 * TC:12 * TC],
                        in0=TREE[:, 0:4 * TC], in1=TREE[:, 4 * TC:8 * TC], op=ALU.add),
                           reads=["p6t0"], writes=["p6t1"])
                units.append(u_l01)

                def u_l23y():
                    ctx.op("dve", lambda e: e.tensor_tensor(
                        out=TREE[:, 12 * TC:14 * TC],
                        in0=TREE[:, 8 * TC:10 * TC], in1=TREE[:, 10 * TC:12 * TC], op=ALU.add),
                           reads=["p6t1"], writes=["p6t2"])
                    ctx.op("dve", lambda e: e.tensor_tensor(
                        out=TREE[:, 14 * TC:15 * TC],
                        in0=TREE[:, 12 * TC:13 * TC], in1=TREE[:, 13 * TC:14 * TC], op=ALU.add),
                           reads=["p6t2"], writes=["p6t3"])
                    ctx.op("dve", lambda e, dc=dc: e.scalar_tensor_tensor(
                        out=TREE[:, 15 * TC:16 * TC], in0=XD[:, xo + TC:xo + 2 * TC],
                        scalar=CONST[:, 96 + dc:97 + dc], in1=TREE[:, 14 * TC:15 * TC],
                        op0=ALU.mult, op1=ALU.add),
                           reads=[ut, "CONST", "p6t3"], writes=["p6y"])
                units.append(u_l23y)

                def u_y3():
                    ctx.op("pool", lambda e: e.tensor_tensor(
                        out=SCB[:, so + TC:so + 2 * TC],
                        in0=TREE[:, 15 * TC:16 * TC],
                        in1=SCB[:, so:so + TC], op=ALU.mult),
                           reads=["p6y", szt], writes=["p6y3%d" % par])
                    ctx.dma("pool", aps["y3"][r0:r0 + 128, t0:t0 + TC],
                            SCB[:, so + TC:so + 2 * TC],
                            reads=["p6y3%d" % par], writes=["y3_d"])
                units.append(u_y3)
                return units

            def mu_quantum(nt, e_):
                n0 = nt * 512
                qp = (nt * NE + e_) % 2
                xp = nt % 2
                xoff = 0
                xt = "p7x"
                par = e_ % 2
                ewo = par * 512
                boff = 32 + par * 32
                ewt = "p7ewb%d" % par
                bgt = "p7bg%d" % par
                s_a = 2 * qp
                s_b = 2 * qp + 1
                s_c = 2 - 2 * qp
                s_d = 3 - 2 * qp
                stag = lambda s: "WS%d" % s

                def wview(s):
                    t, off = WSLOT[s]
                    return t[:, off:off + 8192].rearrange("p (k m) -> p k m", m=1024)

                def wsl(s, k, m0, m1):
                    t, off = WSLOT[s]
                    return t[:, off + k * 1024 + m0:off + k * 1024 + m1]
                units = []

                def u_a():
                    if e_ == 0:
                        for c in range(8):
                            ctx.dma("sync", X16[:, xoff + c * 512:xoff + (c + 1) * 512],
                                    aps["xn2_bf"][c * 128:(c + 1) * 128, n0:n0 + 512],
                                    reads=["xn2b_d"], writes=[xt])
                        ctx.dma("pool", EWB16[:], aps["ew_rows"][:, n0:n0 + 512],
                                reads=["p7ew16"], writes=["p7ew16"])
                    ewb_src = bass.AP(aps["ew_rows"].tensor, e_ * N + n0, [[0, 128], [1, 512]])
                    ctx.dma("sync", EWBP[par][:], ewb_src,
                            reads=["ew_d"], writes=[ewt])
                    ctx.dma("sync", SMALL[:, boff:boff + 16], aps["bg_e"][e_],
                            reads=[bgt], writes=[bgt])
                    ctx.dma("sync", SMALL[:, boff + 16:boff + 32], aps["bu_e"][e_],
                            reads=[bgt], writes=[bgt])
                units.append(u_a)

                def mk_w(half):
                    sg = s_a if half == 0 else s_c
                    su_ = s_b if half == 0 else s_d

                    def u_w():
                        gsrc = bass.AP(aps["wbf"].tensor, e_ * D * ME + half * 1024,
                                       [[ME, 128], [128 * ME, 8], [1, 1024]])
                        ctx.dma("sync", wview(sg), gsrc, reads=["wbf_d"], writes=[stag(sg)])
                        usrc = bass.AP(aps["wbf"].tensor, GU + e_ * D * ME + half * 1024,
                                       [[ME, 128], [128 * ME, 8], [1, 1024]])
                        ctx.dma("act", wview(su_), usrc, reads=["wbf_d"], writes=[stag(su_)])
                    return u_w
                units.append(mk_w(0))
                units.append(mk_w(1))

                def mk_gu(half, mc8):
                    sg = s_a if half == 0 else s_c
                    su_ = s_b if half == 0 else s_d
                    mcg = half * 8 + mc8
                    psg = PSGL[mcg % 3]
                    psu = PSUL[mcg % 3]
                    gb = "PSG%d" % (mcg % 3)
                    ub = "PSU%d" % (mcg % 3)
                    ht = "p7h%d" % half

                    def u_gu():
                        def mmg(e, mc8=mc8, psg=psg, sg=sg):
                            ins = None
                            for k in range(8):
                                ins = e.matmul(psg[:], wsl(sg, k, mc8 * 128, (mc8 + 1) * 128),
                                               X16[:, xoff + k * 512:xoff + (k + 1) * 512],
                                               start=(k == 0), stop=(k == 7))
                            return ins
                        ctx.op("pe", mmg, reads=[stag(sg), xt], writes=[gb])
                        ctx.op("act", lambda e, mcg=mcg, psg=psg: e.activation(
                            out=H16[:, mcg * 512:mcg * 512 + 512], in_=psg[:], func=AF.Silu,
                            bias=SMALL[:, boff + mcg:boff + mcg + 1]),
                               reads=[gb, bgt], writes=[ht])
                        def mmu(e, mc8=mc8, psu=psu, su_=su_):
                            ins = None
                            for k in range(8):
                                ins = e.matmul(psu[:], wsl(su_, k, mc8 * 128, (mc8 + 1) * 128),
                                               X16[:, xoff + k * 512:xoff + (k + 1) * 512],
                                               start=(k == 0), stop=(k == 7))
                            return ins
                        ctx.op("pe", mmu, reads=[stag(su_), xt], writes=[ub])
                        ctx.op("dve", lambda e, mcg=mcg, psu=psu: e.scalar_tensor_tensor(
                            out=XF[:, 4096:4608], in0=psu[:],
                            scalar=SMALL[:, boff + 16 + mcg:boff + 17 + mcg],
                            in1=EWBP[par][:],
                            op0=ALU.add, op1=ALU.mult),
                               reads=[ub, bgt, ewt], writes=["p7tmp"])
                        ctx.op("dve", lambda e, mcg=mcg: e.tensor_tensor(
                            out=H16[:, mcg * 512:mcg * 512 + 512], in0=XF[:, 4096:4608],
                            in1=H16[:, mcg * 512:mcg * 512 + 512], op=ALU.mult),
                               reads=["p7tmp", ht], writes=[ht])
                    return u_gu
                for mc8 in range(8):
                    units.append(mk_gu(0, mc8))

                def u_wo():
                    osrc0 = bass.AP(aps["wbf"].tensor, 2 * GU + e_ * ME * D,
                                    [[D, 128], [128 * D, 8], [1, 1024]])
                    ctx.dma("sync", wview(s_a), osrc0, reads=["wbf_d"], writes=[stag(s_a)])
                    osrc1 = bass.AP(aps["wbf"].tensor, 2 * GU + e_ * ME * D + 8 * 128 * D,
                                    [[D, 128], [128 * D, 8], [1, 1024]])
                    ctx.dma("act", wview(s_b), osrc1, reads=["wbf_d"], writes=[stag(s_b)])
                units.append(u_wo)
                for mc8 in range(8):
                    units.append(mk_gu(1, mc8))

                def mk_out(pp, pas):
                    dcs = [2 * pp, 2 * pp + 1]
                    so_ = s_a if pas == 0 else s_b
                    obanks = ([(PSOL[0], "PSO0"), (PSOL[1], "PSO1")] if pp % 2 == 0
                              else [(PSGL[0], "PSG0"), (PSGL[1], "PSG1")])

                    def u_out():
                        def mmo(e, pas=pas, dcs=dcs, so_=so_, e_=e_, obanks=obanks):
                            ins = None
                            for j, dc in enumerate(dcs):
                                pso = obanks[j][0]
                                for mc8 in range(8):
                                    mc = pas * 8 + mc8
                                    ins = e.matmul(
                                        pso[:], wsl(so_, mc8, dc * 128, (dc + 1) * 128),
                                        H16[:, mc * 512:(mc + 1) * 512],
                                        start=(pas == 0 and mc8 == 0),
                                        stop=(pas == 1 and mc8 == 7 and e_ != 0))
                                if pas == 1 and e_ == 0:
                                    ins = e.matmul(pso[:], BOE[0:8, dc * 128:(dc + 1) * 128],
                                                   EWB16[0:8, :], start=False, stop=True)
                            return ins
                        ctx.op("pe", mmo, reads=[stag(so_), "p7h%d" % pas, "p7boe", "p7ew16"],
                               writes=[b[1] for b in obanks])
                        if pas == 1:
                            for j, dc in enumerate(dcs):
                                pso, pob = obanks[j]
                                if e_ == 0:
                                    ctx.op("dve", lambda e, dc=dc, pso=pso: e.tensor_copy(
                                        out=XF[:, dc * 512:(dc + 1) * 512], in_=pso[:]),
                                           reads=[pob], writes=["p7acc"])
                                else:
                                    ctx.op("dve", lambda e, dc=dc, pso=pso: e.tensor_tensor(
                                        out=XF[:, dc * 512:(dc + 1) * 512],
                                        in0=XF[:, dc * 512:(dc + 1) * 512],
                                        in1=pso[:], op=ALU.add),
                                           reads=[pob, "p7acc"], writes=["p7acc"])
                    return u_out
                for pp in range(4):
                    units.append(mk_out(pp, 0))
                    units.append(mk_out(pp, 1))

                if e_ == NE - 1:
                    def u_store():
                        for dc in range(8):
                            ctx.dma("pool", aps["moe_f"][dc * 128:(dc + 1) * 128, n0:n0 + 512],
                                    XF[:, dc * 512:(dc + 1) * 512],
                                    reads=["p7acc"], writes=["moe_d"])
                    units.append(u_store)
                return units
            su = []
            for tc in range(NTC):
                su.append(lambda tc=tc: su_prolog(tc))
                for dc in range(DI // 128):
                    su.extend(su_dc(tc, dc))
            mu = []
            for nt in range(N // 512):
                for e_ in range(NE):
                    mu.extend(mu_quantum(nt, e_))
            ratio = float(len(mu)) / float(len(su))
            credit = 0.0
            mi = 0
            for s in su:
                s()
                credit += ratio
                while credit >= 1.0 and mi < len(mu):
                    mu[mi]()
                    mi += 1
                    credit -= 1.0
            while mi < len(mu):
                mu[mi]()
                mi += 1
        ctx.barrier()
    def _ph_P65a():
        with (nc.sbuf_tensor(_u("W1"), [128, 16384], BF16) as W1,
              nc.sbuf_tensor(_u("W2"), [128, 16384], BF16) as W2,
              nc.sbuf_tensor(_u("B16A"), [128, 8192], BF16) as B16A,
              nc.sbuf_tensor(_u("B16M"), [128, 4096], BF16) as B16M,
              nc.sbuf_tensor(_u("B16S"), [128, 4096], BF16) as B16S,
              nc.sbuf_tensor(_u("XA"), [128, 3072], F32) as XA,
              nc.psum_tensor(_u("PSA0"), [128, 512], F32) as PSA0,
              nc.psum_tensor(_u("PSA1"), [128, 512], F32) as PSA1):
            for kc in range(16):
                cast_load(W1[:, kc * 1024:(kc + 1) * 1024],
                          aps["out_w"][kc * 128:(kc + 1) * 128, :], PP, "p65w1", 1024)
            for kc in range(8):
                cast_load(W2[:, kc * 2048:(kc + 1) * 2048],
                          aps["glu_w"][kc * 128:(kc + 1) * 128, :], PP, "p65w2", 2048)
            for nt in range(N // 512):
                n0 = nt * 512
                for c in range(16):
                    ctx.dma("sync", B16A[:, c * 512:(c + 1) * 512],
                            aps["y3"][c * 128:(c + 1) * 128, n0:n0 + 512],
                            reads=["y3_d"], writes=["p65y"])
                for mc in range(8):
                    ps = PSA0 if mc % 2 == 0 else PSA1
                    pb = "PSA%d" % (mc % 2)
                    def mmop(e, mc=mc, ps=ps):
                        ins = None
                        for k in range(16):
                            ins = e.matmul(ps[:], W1[:, k * 1024 + mc * 128: k * 1024 + (mc + 1) * 128],
                                           B16A[:, k * 512:(k + 1) * 512],
                                           start=(k == 0), stop=(k == 15))
                        return ins
                    ctx.op("pe", mmop, reads=["p65w1", "p65y"], writes=[pb])
                    ctx.op("act", lambda e, mc=mc, ps=ps: e.activation(
                        out=B16M[:, mc * 512:(mc + 1) * 512], in_=ps[:], func=AF.Gelu),
                           reads=[pb], writes=["p65m"])
                for gc in range(8, 16):
                    ps = PSA0 if gc % 2 == 0 else PSA1
                    pb = "PSA%d" % (gc % 2)
                    def mmgl(e, gc=gc, ps=ps):
                        ins = None
                        for k in range(8):
                            ins = e.matmul(ps[:], W2[:, k * 2048 + gc * 128: k * 2048 + (gc + 1) * 128],
                                           B16M[:, k * 512:(k + 1) * 512],
                                           start=(k == 0), stop=(k == 7))
                        return ins
                    ctx.op("pe", mmgl, reads=["p65w2", "p65m"], writes=[pb])
                    ctx.op("act", lambda e, gc=gc, ps=ps: e.activation(
                        out=B16S[:, (gc - 8) * 512:(gc - 7) * 512], in_=ps[:], func=AF.Sigmoid,
                        bias=CONST[:, 112 + gc:113 + gc]),
                           reads=[pb, "CONST"], writes=["p65sig"])
                for gc in range(8):
                    ps = PSA0 if gc % 2 == 0 else PSA1
                    pb = "PSA%d" % (gc % 2)
                    def mmgl2(e, gc=gc, ps=ps):
                        ins = None
                        for k in range(8):
                            ins = e.matmul(ps[:], W2[:, k * 2048 + gc * 128: k * 2048 + (gc + 1) * 128],
                                           B16M[:, k * 512:(k + 1) * 512],
                                           start=(k == 0), stop=(k == 7))
                        return ins
                    ctx.op("pe", mmgl2, reads=["p65w2", "p65m"], writes=[pb])
                    ctx.op("dve", lambda e, gc=gc, ps=ps: e.scalar_tensor_tensor(
                        out=XA[:, 0:512], in0=ps[:], scalar=CONST[:, 112 + gc:113 + gc],
                        in1=B16S[:, gc * 512:(gc + 1) * 512], op0=ALU.add, op1=ALU.mult),
                           reads=[pb, "CONST", "p65sig"], writes=["p65m2"])
                    ctx.dma("sync", XA[:, 512:1024],
                            aps["xn2_f"][gc * 128:(gc + 1) * 128, n0:n0 + 512],
                            reads=["xn2f_d", "p65sk"], writes=["p65sk"])
                    ctx.op("dve", lambda e: e.tensor_tensor(out=XA[:, 1024:1536],
                                                            in0=XA[:, 0:512], in1=XA[:, 512:1024],
                                                            op=ALU.add),
                           reads=["p65m2", "p65sk"], writes=["p65s1"])
                    ctx.dma("sync", XA[:, 1536:2048],
                            aps["moe_f"][gc * 128:(gc + 1) * 128, n0:n0 + 512],
                            reads=["moe_d", "p65mo"], writes=["p65mo", "p65o"])
                    ctx.op("act", lambda e: e.activation(out=XA[:, 2048:2560],
                                                         in_=XA[:, 1536:2048], func=AF.Silu),
                           reads=["p65mo"], writes=["p65sg"])
                    ctx.op("dve", lambda e: e.tensor_tensor(out=XA[:, 0:512],
                                                            in0=XA[:, 2048:2560],
                                                            in1=XA[:, 1024:1536], op=ALU.mult),
                           reads=["p65sg", "p65s1", "p65m2"], writes=["p65sg", "p65m2"])
                    ctx.dma("sync", XA[:, 2560:3072],
                            aps["x1_f"][gc * 128:(gc + 1) * 128, n0:n0 + 512],
                            reads=["x1_d", "p65x1"], writes=["p65x1"])
                    ctx.op("dve", lambda e: e.tensor_tensor(out=XA[:, 1536:2048],
                                                            in0=XA[:, 0:512],
                                                            in1=XA[:, 2560:3072], op=ALU.add),
                           reads=["p65sg", "p65x1", "p65m2"], writes=["p65o", "p65mo"])
                    ctx.dma("pool", aps["out"][gc * 128:(gc + 1) * 128, n0:n0 + 512],
                            XA[:, 1536:2048], reads=["p65o"], writes=["out_d"])
        ctx.barrier()


    if "P1" in phases:
        _ph_prologue()
    if "P2" in phases:
        _ph_P2a()
    if "P3a" in phases:
        _ph_P3aa()
    if "PW" in phases:
        _ph_casts()
    if "P3b" in phases:
        _ph_P3bma()
    if "P4b" in phases:
        _ph_P4ba()
    if "P4c" in phases:
        _ph_P4ca()
    if "P6" in phases and "P7" in phases:
        _ph_P67()
    if "P65" in phases:
        _ph_P65a()

    ctx.finalize(sems)
    stack.close()
    return nc, sorted(need_in | always_const)


GROUPS = [ALL_PHASES]


def kernel(**inputs):
    n_tok = inputs["L0_attn_x"].shape[1]
    w = prep_weights(inputs)
    x = np.ascontiguousarray(np.asarray(inputs["L0_attn_x"], dtype=np.float32))
    ncore = x.shape[0]
    host = {}
    for c in range(ncore):
        host[("x_tm", c)] = np.ascontiguousarray(x[c])
        host[("x_fm", c)] = np.ascontiguousarray(x[c].T)
    state = [dict() for _ in range(ncore)]
    out = None
    for gi, group in enumerate(GROUPS):
        nc, in_names = build_program(n_tok, group)
        in_maps = []
        for c in range(ncore):
            m = {}
            for name in in_names:
                if name in w:
                    m[name] = w[name]
                elif (name, c) in host:
                    m[name] = host[(name, c)]
                else:
                    m[name] = state[c][name]
            in_maps.append(m)
        res = run_bass_kernel_spmd(nc, in_maps, core_ids=list(range(ncore)))
        for c in range(ncore):
            for k, arr in res.results[c].items():
                state[c][k] = arr
    outs = [np.ascontiguousarray(state[c]["out"].T) for c in range(ncore)]
    return np.stack(outs, axis=0).astype(np.float32)
